# version 01
# baseline (speedup 1.0000x reference)
import contextlib
import numpy as np
import concourse.bass as bass
import concourse.mybir as mybir
from concourse.bass_utils import run_bass_kernel_spmd

F32 = mybir.dt.float32
BF16 = mybir.dt.bfloat16
I32 = mybir.dt.int32
AF = mybir.ActivationFunctionType
ALU = mybir.AluOpType
AX = mybir.AxisListType.X

C = 128
H = 16
DM = 1024
DFF = 3072
T = 2048
DPHI = 273
PACK = 195
EPS = 1e-9
LNEPS = 1e-5
KQSC = 1.0 / 32.0
KRSC = 0.25
FFSC = float(1.0 / (127.0 * 127.0))


def build_nc(nch=T // C, npre=T // C, dumps=()):
    nc = bass.Bass("TRN2", target_bir_lowering=False)
    x_h = nc.dram_tensor("x", [T, DM], F32, kind="ExternalInput").ap()
    xp_h = nc.dram_tensor("xp", [T, DM], F32, kind="ExternalInput").ap()
    cst_h = nc.dram_tensor("cst", [128, 8], F32, kind="ExternalInput").ap()
    wq_h = nc.dram_tensor("wq", [DM, 256], F32, kind="ExternalInput").ap()
    wk_h = nc.dram_tensor("wk", [DM, 256], F32, kind="ExternalInput").ap()
    wv_h = nc.dram_tensor("wv", [DM, DM], F32, kind="ExternalInput").ap()
    wo_h = nc.dram_tensor("wo", [DM, DM], F32, kind="ExternalInput").ap()
    wg_h = nc.dram_tensor("wg", [DM, DFF], I32, kind="ExternalInput").ap()
    wu_h = nc.dram_tensor("wu", [DM, DFF], I32, kind="ExternalInput").ap()
    wd_h = nc.dram_tensor("wd", [DFF, DM], I32, kind="ExternalInput").ap()
    id_h = nc.dram_tensor("ident", [128, 128], F32, kind="ExternalInput").ap()
    m32_h = nc.dram_tensor("m32", [128, 128], F32, kind="ExternalInput").ap()
    y_h = nc.dram_tensor("y", [T, DM], F32, kind="ExternalOutput").ap()
    dump_h = {}
    for tag, shape in dumps:
        dump_h[tag] = nc.dram_tensor("dbg_" + tag, shape, F32, kind="ExternalOutput").ap()

    streams = {e: [] for e in ("sync", "gpsimd", "vector", "scalar", "tensor")}
    cnt = {s: 0 for s in ("dG", "dS", "dD", "sT", "sV", "sA", "sG")}
    res = {}

    pending_dump = {}

    def emit(eng, fn, waits=(), inc=None):
        wmax = {}
        pd = pending_dump.pop(eng, None)
        waits = list(waits) + ([("dD", pd)] if pd else [])
        for s, v in waits:
            assert v <= cnt[s], f"wait {s}>={v}, emitted {cnt[s]}"
            if v > 0:
                wmax[s] = max(wmax.get(s, 0), v)
        if inc:
            s, n = inc
            cnt[s] += n
        streams[eng].append((fn, tuple(wmax.items()), inc))

    def freew(*names):
        w = []
        for n in names:
            w += res.pop(n, [])
        return w

    def reads(name, sem):
        res.setdefault(name, []).append((sem, cnt[sem]))

    with contextlib.ExitStack() as stk:
        E = stk.enter_context
        stream = E(nc.sbuf_tensor("stream", [128, nch * DM], BF16))
        xfm = E(nc.sbuf_tensor("xfm", [128, DM], BF16))
        warena = E(nc.sbuf_tensor("warena", [128, 24576], BF16))
        wg_sb = E(nc.sbuf_tensor("wg_sb", [128, 24576], BF16))
        wu_sb = E(nc.sbuf_tensor("wu_sb", [128, 24576], BF16))
        scr = E(nc.sbuf_tensor("scr", [128, 14724], BF16))
        m32_sb = E(nc.sbuf_tensor("m32_sb", [128, 128], BF16))
        id_sb = E(nc.sbuf_tensor("id_sb", [128, 128], BF16))
        cst_sb = E(nc.sbuf_tensor("cst_sb", [128, 8], F32))
        den_sb = E(nc.sbuf_tensor("den_sb", [128, 16], F32))
        rec_sb = E(nc.sbuf_tensor("rec_sb", [128, 16], F32))
        ln_sb = E(nc.sbuf_tensor("ln_sb", [128, 96], F32))

        xps = scr[:, 0:2048]
        qk_tok = scr[:, 0:512]
        ksc = scr[:, 512:768]
        qk_pad = scr[:, 768:1792]
        pu_sb = scr[:, 1792:2048]
        pk_tok = scr[:, 12164:12676]
        pk_sc = scr[:, 12676:12932]
        phik = scr[:, 2048:6416]
        phiq = scr[:, 6416:7235]
        phiqfm = scr[:, 7235:8003]
        v_sb = scr[:, 8003:9043]
        state = scr[:, 9043:12163]
        attn = scr[:, 12164:13188]
        attnfm = scr[:, 13188:14212]
        pu_s1 = scr[:, 14212:14724].bitcast(F32)
        h_tok = scr[:, 0:3072]
        hfm = scr[:, 3072:6144]
        silu_sb = scr[:, 6144:7680]
        ostage = scr[:, 7680:11776].bitcast(F32)

        psA = E(nc.psum_tensor("psA", [128, 512], F32))
        psB = E(nc.psum_tensor("psB", [128, 512], F32))
        psC = E(nc.psum_tensor("psC", [128, 512], F32))
        psD = E(nc.psum_tensor("psD", [128, 512], F32))
        psE = E(nc.psum_tensor("psE", [128, 512], F32))
        psF = E(nc.psum_tensor("psF", [128, 512], F32))
        psT = E(nc.psum_tensor("psT", [128, 1024], BF16))
        psU = E(nc.psum_tensor("psU", [128, 1024], BF16))

        sems = {s: E(nc.semaphore(s)) for s in cnt}

        wq_sb = warena[:, 0:2048]
        wk_sb = warena[:, 2048:4096]
        wv_sb = warena[:, 4096:12288]
        wo_sb = warena[:, 12288:20480]
        wd_sb = warena[:, 0:24576]

        def dma(dst, src):
            return lambda e, d=dst, s=src: e.dma_start(out=d, in_=s)

        emit("gpsimd", dma(id_sb[:, :], id_h[:, :]), inc=("dG", 16))

        emit("gpsimd", dma(wq_sb.rearrange("p (d j) -> p d j", d=8),
                           wq_h.rearrange("(d p) j -> p d j", p=128)), inc=("dG", 16))
        emit("gpsimd", dma(wk_sb.rearrange("p (d j) -> p d j", d=8),
                           wk_h.rearrange("(d p) j -> p d j", p=128)), inc=("dG", 16))
        emit("gpsimd", dma(wv_sb.rearrange("p (d j) -> p d j", d=8),
                           wv_h.rearrange("(d p) j -> p d j", p=128)), inc=("dG", 16))
        emit("gpsimd", dma(wo_sb.rearrange("p (d j) -> p d j", d=8),
                           wo_h.rearrange("(d p) j -> p d j", p=128)), inc=("dG", 16))
        W_ATTN = cnt["dG"]
        emit("gpsimd", dma(m32_sb[:, :], m32_h[:, :]), inc=("dG", 16))
        emit("sync", dma(cst_sb[:, :], cst_h[:, :]), inc=("dG", 16))
        W_CONST = cnt["dG"]

        emit("vector", lambda e: e.memset(
            phik.rearrange("p (h d) -> p h d", h=16)[:, :, 0:1], 1.0))
        emit("vector", lambda e: e.memset(
            phiq.rearrange("p (s d) -> p s d", s=3)[:, :, 0:1], 1.0))
        emit("vector", lambda e: e.memset(state[:, :], 0.0), inc=("sV", 1))
        res["state"] = [("sV", cnt["sV"])]

        emit("gpsimd", dma(wg_sb.rearrange("p (d j) -> p d j", d=8),
                           wg_h.rearrange("(d p) j -> p d j", p=128)), inc=("dG", 16))
        emit("gpsimd", dma(wu_sb.rearrange("p (d j) -> p d j", d=8),
                           wu_h.rearrange("(d p) j -> p d j", p=128)), inc=("dG", 16))
        W_GU = cnt["dG"]

        def tr(out, src, tp=None):
            return lambda e, o=out, s=src, t=tp: e.transpose(o, s, id_sb[:, :],
                                                             tile_position=t)

        def mm(out, lhsT, rhs, start, stop, tp=None):
            return lambda e, o=out, l=lhsT, r=rhs, a=start, b=stop, t=tp: e.matmul(
                o, l, r, start=a, stop=b, tile_position=t)

        def cp(eng_dst, src):
            return lambda e, d=eng_dst, s=src: e.tensor_copy(d, s)

        def out_slot(h):
            if h < 7:
                return psA, "psA", 65 * h
            if h < 14:
                return psB, "psB", 65 * (h - 7)
            return psC, "psC", 130 + 65 * (h - 14)

        addsnap = {h: ("sV", 0) for h in range(H)}

        def dump(tag, src_bf16, n_f32_cols=None):
            if tag in dump_h:
                w = [(s, cnt[s]) for s in ("sT", "sV", "sA", "sG")]
                emit("gpsimd", dma(dump_h[tag][:, :], src_bf16), waits=w, inc=("dD", 16))
                for eng in streams:
                    pending_dump[eng] = cnt["dD"]

        def scan_chunk(ci, xsrc, vcol, kv_only, xld, qk_tok=qk_tok, ksc=ksc,
                       dump_cb=None):
            w = freew("psT") + [("dG", xld), ("dG", W_CONST)]
            for dc in range(8):
                emit("tensor", tr(psT[:, 128 * dc:128 * dc + 128],
                                  xsrc[:, 128 * dc:128 * dc + 128]),
                     waits=w if dc == 0 else (),
                     inc=("sT", 1) if dc == 7 else None)
            tx = cnt["sT"]
            emit("vector", cp(xfm[:, :], psT[:, :]), waits=[("sT", tx)] + freew("xfm"),
                 inc=("sV", 1))
            reads("psT", "sV")
            xev = cnt["sV"]

            w = [("sV", xev), ("dG", W_ATTN)] + freew("psA", "psB", "psC")
            for dc in range(8):
                lhsT = xfm[:, 128 * dc:128 * dc + 128]
                if not kv_only:
                    emit("tensor", mm(psA[:, 0:256], lhsT,
                                      wq_sb[:, 256 * dc:256 * dc + 256], dc == 0, dc == 7),
                         waits=w if dc == 0 else ())
                    w = ()
                emit("tensor", mm(psA[:, 256:512], lhsT,
                                  wk_sb[:, 256 * dc:256 * dc + 256],
                                  (dc == 0) and kv_only, dc == 7),
                     waits=w if dc == 0 else ())
                w = ()
                emit("tensor", mm(psB[:, 0:512], lhsT,
                                  wv_sb[:, 1024 * dc:1024 * dc + 512], dc == 0, dc == 7))
                emit("tensor", mm(psC[:, 0:512], lhsT,
                                  wv_sb[:, 1024 * dc + 512:1024 * dc + 1024], dc == 0, dc == 7),
                     inc=("sT", 1) if dc == 7 else None)
            tproj = cnt["sT"]

            lo = 256 if kv_only else 0
            emit("vector", cp(qk_tok[:, lo:512], psA[:, lo:512]),
                 waits=[("sT", tproj)] + freew("qk"))
            emit("vector", cp(v_sb.rearrange("p (h v) -> p h v", h=16)[:, 0:8, 0:64],
                              psB[:, 0:512].rearrange("p (h v) -> p h v", h=8)),
                 waits=freew("v"))
            emit("vector", cp(v_sb.rearrange("p (h v) -> p h v", h=16)[:, 8:16, 0:64],
                              psC[:, 0:512].rearrange("p (h v) -> p h v", h=8)))
            emit("vector", cp(v_sb.rearrange("p (h v) -> p h v", h=16)[:, :, 64:65],
                              cst_sb[:, vcol:vcol + 1].unsqueeze(1).broadcast_to([128, 16, 1])))
            emit("vector", lambda e: e.tensor_scalar(
                ksc[:, 0:256], qk_tok[:, 256:512], KQSC, None, ALU.mult),
                 waits=freew("qksc"))
            emit("vector", lambda e: e.tensor_scalar(
                phik.rearrange("p (h d) -> p h d", h=16)[:, :, 1:17],
                qk_tok[:, 256:512].rearrange("p (h f) -> p h f", h=16),
                KRSC, None, ALU.mult), waits=freew("phikraw"), inc=("sV", 1))
            projev = cnt["sV"]
            reads("psA", "sV")
            reads("psB", "sV")
            reads("psC", "sV")

            for h in range(H):
                w = [("sV", projev)] + freew(f"phik{h}")
                def kq(e, h=h):
                    o = phik[:, DPHI * h + 17:DPHI * h + 273].rearrange(
                        "p (i j) -> p i j", i=16)
                    i0 = ksc[:, 16 * h:16 * h + 16].unsqueeze(2).broadcast_to([128, 16, 16])
                    i1 = qk_tok[:, 256 + 16 * h:256 + 16 * h + 16].unsqueeze(1).broadcast_to([128, 16, 16])
                    return e.tensor_tensor(o, i0, i1, ALU.mult)
                emit("gpsimd", kq, waits=w, inc=("sG", 1))
                res[f"kfm{h}"] = [("sG", cnt["sG"])]

            if kv_only:
                for h in range(H):
                    ps, bank = (psE, "psE") if h % 2 == 0 else (psF, "psF")
                    c0 = PACK * h
                    w = res[f"kfm{h}"] + freew(bank)
                    emit("tensor", mm(ps[:, 0:65], phik[:, DPHI * h:DPHI * h + 128],
                                      v_sb[:, 65 * h:65 * h + 65], True, True), waits=w)
                    emit("tensor", mm(ps[:, 65:130], phik[:, DPHI * h + 128:DPHI * h + 256],
                                      v_sb[:, 65 * h:65 * h + 65], True, True))
                    emit("tensor", mm(ps[0:17, 130:195], phik[:, DPHI * h + 256:DPHI * h + 273],
                                      v_sb[:, 65 * h:65 * h + 65], True, True),
                         inc=("sT", 1))
                    stv = cnt["sT"]
                    res[f"phik{h}"] = [("sT", stv)]
                    emit("vector", lambda e, ps=ps, c0=c0: e.tensor_tensor(
                        state[:, c0:c0 + PACK], ps[:, 0:PACK],
                        state[:, c0:c0 + PACK], ALU.add),
                         waits=[("sT", stv)] + freew("state") + [addsnap[h]],
                         inc=("sV", 1))
                    addsnap[h] = ("sV", cnt["sV"])
                    reads(bank, "sV")
                res["v"] = [("sT", cnt["sT"])]
                res["qk"] = [("sG", cnt["sG"]), ("sT", cnt["sT"])]
                res["qksc"] = [("sG", cnt["sG"])]
                res["phikraw"] = [("sG", cnt["sG"]), ("sT", cnt["sT"])]
                return

            w = [("sV", projev)] + freew("psT")
            for h in range(H):
                j, g = h % 4, h // 4
                emit("tensor", tr(psT[32 * j:32 * j + 16, 128 * g:128 * g + 128],
                                  qk_tok[:, 16 * h:16 * h + 16], (0, 32 * j)),
                     waits=w if h == 0 else ())
                w = ()
            for h in range(H):
                j, g = h % 4, h // 4
                emit("tensor", tr(psT[32 * j:32 * j + 16, 512 + 128 * g:512 + 128 * g + 128],
                                  qk_tok[:, 256 + 16 * h:256 + 16 * h + 16], (0, 32 * j)),
                     inc=("sT", 1) if h == H - 1 else None)
            tpads = cnt["sT"]
            emit("vector", cp(qk_pad[:, :], psT[:, :]), waits=[("sT", tpads)],
                 inc=("sV", 1))
            padsev = cnt["sV"]
            reads("psT", "sV")
            if dump_cb:
                dump_cb("pads0", qk_pad[:, :])

            for h in range(H):
                j, g = h % 4, h // 4
                osl, obank, ocol = out_slot(h)
                slot2 = h % 2
                slot4 = h % 3
                ubank, uname = (psC, "u_psC") if slot2 == 0 else (psD, "u_psD")
                phb, phbname = (psT, "psT") if slot2 == 0 else (psU, "psU")
                phislot = phiq[:, DPHI * slot4:DPHI * slot4 + DPHI]
                phifm = phiqfm[:, 384 * slot2:384 * slot2 + 384]
                pus1 = pu_s1[:, 128 * slot2:128 * slot2 + 128]
                pusl = pu_sb[:, 128 * slot2:128 * slot2 + 128]

                emit("vector", lambda e, h=h, sl=phislot: e.tensor_copy(
                    sl[:, 1:17], qk_tok[:, 16 * h:16 * h + 16]),
                     waits=freew(f"phiq{slot4}"))
                def qq(e, h=h, sl=phislot):
                    o = sl[:, 17:273].rearrange("p (i j) -> p i j", i=16)
                    i0 = qk_tok[:, 16 * h:16 * h + 16].unsqueeze(2).broadcast_to([128, 16, 16])
                    i1 = qk_tok[:, 16 * h:16 * h + 16].unsqueeze(1).broadcast_to([128, 16, 16])
                    return e.tensor_tensor(o, i0, i1, ALU.mult)
                emit("vector", qq, inc=("sV", 1))
                qqv = cnt["sV"]

                w = [("sV", padsev)] + freew(uname)
                emit("tensor", mm(ubank[:, 0:128],
                                  qk_pad[32 * j:32 * j + 16, 512 + 128 * g:512 + 128 * g + 128],
                                  qk_pad[32 * j:32 * j + 16, 128 * g:128 * g + 128],
                                  True, True, (32 * j, 0)), waits=w, inc=("sT", 1))
                tu = cnt["sT"]

                emit("scalar", lambda e, ub=ubank, s1=pus1: e.activation(
                    s1, ub[:, 0:128], AF.Square, bias=cst_sb[:, 2:3]),
                     waits=[("sT", tu)] + freew(f"pus1_{slot2}"), inc=("sA", 1))
                reads(uname, "sA")
                polyA = cnt["sA"]

                emit("vector", lambda e, s1=pus1, pu=pusl: e.scalar_tensor_tensor(
                    pu, s1, 16.0, m32_sb[:, :], ALU.add, ALU.mult),
                     waits=[("sA", polyA)] + freew(f"pu_{slot2}"), inc=("sV", 1))
                res[f"pus1_{slot2}"] = [("sV", cnt["sV"])]
                puV = cnt["sV"]
                if dump_cb and h == 0:
                    dump_cb("pu0", pusl)

                w = [("sV", qqv)] + freew(phbname)
                emit("tensor", tr(phb[:, 0:128], phislot[:, 0:128]), waits=w)
                emit("tensor", tr(phb[:, 128:256], phislot[:, 128:256]))
                emit("tensor", tr(phb[0:17, 256:384], phislot[:, 256:273]), inc=("sT", 1))
                res[f"phiq{slot4}"] = [("sT", cnt["sT"])]
                tphi = cnt["sT"]

                emit("scalar", lambda e, pb=phb, pf=phifm: e.copy(pf, pb[:, 0:384]),
                     waits=[("sT", tphi)] + freew(f"phifm{slot2}"), inc=("sA", 1))
                reads(phbname, "sA")
                phievA = cnt["sA"]
                if dump_cb and h == 0:
                    dump_cb("phiq0", phislot)
                    dump_cb("phifm0", phifm)

                emit("tensor", mm(osl[:, ocol:ocol + 65], pusl,
                                  v_sb[:, 65 * h:65 * h + 65], h in (0, 7, 14), False),
                     waits=[("sV", puV)], inc=("sT", 1))
                res[f"pu_{slot2}"] = [("sT", cnt["sT"])]

                c0 = PACK * h
                w = [("sA", phievA), addsnap[h]]
                emit("tensor", mm(osl[:, ocol:ocol + 65], phifm[:, 0:128],
                                  state[:, c0:c0 + 65], False, False), waits=w)
                emit("tensor", mm(osl[:, ocol:ocol + 65], phifm[:, 128:256],
                                  state[:, c0 + 65:c0 + 130], False, False))
                emit("tensor", mm(osl[:, ocol:ocol + 65], phifm[0:17, 256:384],
                                  state[0:17, c0 + 130:c0 + 195], False, True),
                     inc=("sT", 1))
                res[f"phifm{slot2}"] = [("sT", cnt["sT"])]

                ps, bank = (psE, "psE") if slot2 == 0 else (psF, "psF")
                w = res[f"kfm{h}"] + freew(bank)
                emit("tensor", mm(ps[:, 0:65], phik[:, DPHI * h:DPHI * h + 128],
                                  v_sb[:, 65 * h:65 * h + 65], True, True), waits=w)
                emit("tensor", mm(ps[:, 65:130], phik[:, DPHI * h + 128:DPHI * h + 256],
                                  v_sb[:, 65 * h:65 * h + 65], True, True))
                emit("tensor", mm(ps[0:17, 130:195], phik[:, DPHI * h + 256:DPHI * h + 273],
                                  v_sb[:, 65 * h:65 * h + 65], True, True),
                     inc=("sT", 1))
                stv = cnt["sT"]
                res[f"phik{h}"] = [("sT", stv)]
                emit("vector", lambda e, ps=ps, c0=c0: e.tensor_tensor(
                    state[:, c0:c0 + PACK], ps[:, 0:PACK],
                    state[:, c0:c0 + PACK], ALU.add),
                     waits=[("sT", stv)] + freew("state") + [addsnap[h]], inc=("sV", 1))
                addsnap[h] = ("sV", cnt["sV"])
                reads(bank, "sV")

            theads = cnt["sT"]
            res["v"] = [("sT", theads)]
            res["qk"] = [("sT", theads), ("sG", cnt["sG"]), ("sV", cnt["sV"])]
            res["qksc"] = [("sG", cnt["sG"]), ("sV", cnt["sV"])]
            res["phikraw"] = [("sG", cnt["sG"]), ("sT", theads)]

            emit("vector", cp(den_sb[:, 0:7],
                              psA[:, 0:455].rearrange("p (h v) -> p h v", h=7)[:, :, 64:65].squeeze(2)),
                 waits=[("sT", theads)] + freew("attn"))
            emit("vector", cp(den_sb[:, 7:14],
                              psB[:, 0:455].rearrange("p (h v) -> p h v", h=7)[:, :, 64:65].squeeze(2)))
            emit("vector", cp(den_sb[:, 14:16],
                              psC[:, 130:260].rearrange("p (h v) -> p h v", h=2)[:, :, 64:65].squeeze(2)),
                 inc=("sV", 1))
            genv = cnt["sV"]
            emit("vector", lambda e: e.tensor_scalar(den_sb[:, :], den_sb[:, :],
                                                     EPS, None, ALU.add),
                 waits=[("sV", genv)], inc=("sV", 1))
            epsv = cnt["sV"]
            emit("vector", lambda e: e.reciprocal(rec_sb[:, :], den_sb[:, :]),
                 waits=[("sV", epsv)], inc=("sV", 1))
            recv = cnt["sV"]
            for h2 in range(H):
                osl2, _, ocol2 = out_slot(h2)
                emit("vector", lambda e, h2=h2, osl2=osl2, ocol2=ocol2: e.tensor_scalar(
                    attn[:, 64 * h2:64 * h2 + 64], osl2[:, ocol2:ocol2 + 64],
                    rec_sb[:, h2:h2 + 1], None, ALU.mult),
                     waits=[("sV", recv)] if h2 == 0 else (),
                     inc=("sV", 1) if h2 == H - 1 else None)
            divV = cnt["sV"]
            if dump_cb:
                emit("vector", cp(attnfm.bitcast(F32)[:, 0:455], psA[:, 0:455]),
                     waits=[("sT", theads)] + freew("attnfm"), inc=("sV", 1))
                res["attnfm"] = [("sV", cnt["sV"])]
                dump_cb("o0", attnfm.bitcast(F32)[:, 0:455])
                dump_cb("den0", den_sb[:, :])
                dump_cb("rec0", rec_sb[:, :])
            res["psA"] = [("sV", divV)]
            res["psB"] = [("sV", divV)]
            res["psC"] = [("sV", divV), res.pop("u_psC", [("sV", 0)])[0]]

            w = [("sV", divV)] + freew("psT")
            for dc in range(8):
                emit("tensor", tr(psT[:, 128 * dc:128 * dc + 128],
                                  attn[:, 128 * dc:128 * dc + 128]),
                     waits=w if dc == 0 else (), inc=("sT", 1) if dc == 7 else None)
                w = ()
            tat = cnt["sT"]
            res["attn"] = [("sT", tat)]
            emit("scalar", lambda e: e.copy(attnfm[:, :], psT[:, :]),
                 waits=[("sT", tat)] + freew("attnfm"), inc=("sA", 1))
            reads("psT", "sA")
            atev = cnt["sA"]
            w = [("sA", atev)] + freew("u_psD", "psF")
            for dc in range(8):
                emit("tensor", mm(psD[:, 0:512], attnfm[:, 128 * dc:128 * dc + 128],
                                  wo_sb[:, 1024 * dc:1024 * dc + 512], dc == 0, dc == 7),
                     waits=w if dc == 0 else ())
                w = ()
                emit("tensor", mm(psF[:, 0:512], attnfm[:, 128 * dc:128 * dc + 128],
                                  wo_sb[:, 1024 * dc + 512:1024 * dc + 1024], dc == 0, dc == 7),
                     inc=("sT", 1) if dc == 7 else None)
            two = cnt["sT"]
            res["attnfm"] = [("sT", two)]
            emit("vector", lambda e, ci=ci: e.tensor_tensor(
                stream[:, DM * ci:DM * ci + 512], psD[:, 0:512],
                stream[:, DM * ci:DM * ci + 512], ALU.add), waits=[("sT", two)])
            emit("vector", lambda e, ci=ci: e.tensor_tensor(
                stream[:, DM * ci + 512:DM * ci + 1024], psF[:, 0:512],
                stream[:, DM * ci + 512:DM * ci + 1024], ALU.add), inc=("sV", 1))
            rv = cnt["sV"]
            res["psD"] = [("sV", rv)]
            res["psF"] = [("sV", rv)]
            res["wo_resid"] = [("sV", rv)]

        for ci in range(npre):
            slot = xps[:, 1024 * (ci % 2):1024 * (ci % 2) + 1024]
            emit("gpsimd", dma(slot, xp_h[C * ci:C * ci + C, :]),
                 waits=freew(f"xps{ci % 2}"), inc=("dG", 16))
            xld = cnt["dG"]
            scan_chunk(ci, slot, 1, True, xld, qk_tok=pk_tok, ksc=pk_sc)
            res[f"xps{ci % 2}"] = [("sT", cnt["sT"])]

        if dumps:
            dump("state_pre", state[:, :], PACK * H)

        for ci in range(nch):
            emit("gpsimd", dma(stream[:, DM * ci:DM * ci + DM], x_h[C * ci:C * ci + C, :]),
                 inc=("dG", 16))
            scan_chunk(ci, stream[:, DM * ci:DM * ci + DM], 0, False, cnt["dG"],
                       dump_cb=(dump if ci == 0 else None))
            if ci == 0:
                dump("qk0", qk_tok[:, :], 512)
                dump("v0", v_sb[:, :], 1040)
                dump("phik0", phik[:, :], DPHI * H)
                dump("attn0", attn[:, :], DM)
                dump("state0", state[:, :], PACK * H)
            if ci == 1:
                dump("stream1", stream[:, DM:2 * DM], DM)

        def ln_stats(ci):
            w = freew("wo_resid") if ci == 0 else []
            if ci == 0:
                w += freew("attnfm")
            emit("scalar", lambda e, ci=ci: e.activation(
                attnfm[:, :], stream[:, DM * ci:DM * ci + DM],
                AF.Square, accum_out=ln_sb[:, 16 + ci:17 + ci]),
                 waits=w, inc=("sA", 1))
            emit("vector", lambda e, ci=ci: e.reduce_sum(
                ln_sb[:, ci:ci + 1], stream[:, DM * ci:DM * ci + DM], AX),
                 waits=w, inc=("sV", 1))

        def ln_finish():
            sAs, sVs = cnt["sA"], cnt["sV"]
            emit("vector", lambda e: e.tensor_scalar(
                ln_sb[:, 32:48], ln_sb[:, 0:16], 1.0 / DM, None, ALU.mult),
                 waits=[("sA", sAs), ("sV", sVs)], inc=("sV", 1))
            meanv = cnt["sV"]
            emit("vector", lambda e: e.tensor_tensor(
                den_sb[:, 0:16], ln_sb[:, 32:48], ln_sb[:, 32:48], ALU.mult),
                 waits=[("sV", meanv)], inc=("sV", 1))
            msqv = cnt["sV"]
            emit("vector", lambda e: e.scalar_tensor_tensor(
                ln_sb[:, 64:80], ln_sb[:, 16:32], 1.0 / DM, den_sb[:, 0:16],
                ALU.mult, ALU.subtract), waits=[("sV", msqv)], inc=("sV", 1))
            vv = cnt["sV"]
            emit("scalar", lambda e: e.activation(rec_sb[:, 0:16], ln_sb[:, 64:80],
                                                  AF.Ln, bias=cst_sb[:, 3:4]), waits=[("sV", vv)])
            emit("scalar", lambda e: e.activation(ln_sb[:, 48:64], rec_sb[:, 0:16],
                                                  AF.Exp, scale=cst_sb[:, 4:5]), inc=("sA", 1))
            sa = cnt["sA"]
            emit("vector", lambda e: e.scalar_tensor_tensor(
                ln_sb[:, 64:80], ln_sb[:, 32:48], -1.0, ln_sb[:, 48:64],
                ALU.mult, ALU.mult), waits=[("sA", sa)], inc=("sV", 1))
            return cnt["sV"]

        for ci in range(nch):
            ln_stats(ci)
        vneg = ln_finish()
        if dumps:
            dump("lnsb", ln_sb[:, :], 96)
        for ci in range(nch):
            emit("vector", lambda e, ci=ci: e.tensor_scalar(
                stream[:, DM * ci:DM * ci + DM], stream[:, DM * ci:DM * ci + DM],
                ln_sb[:, 48 + ci:49 + ci], ln_sb[:, 64 + ci:65 + ci],
                ALU.mult, ALU.add),
                 waits=[("sV", vneg)] if ci == 0 else (), inc=("sV", 1))
            res[f"norm{ci}"] = [("sV", cnt["sV"])]
        if dumps:
            dump("y1", stream[:, 0:DM], DM)

        emit("gpsimd", dma(wd_sb.rearrange("p (hc j) -> p hc j", hc=24),
                           wd_h.rearrange("(hc p) j -> p hc j", p=128)),
             waits=[("sT", cnt["sT"])], inc=("dG", 16))
        W_WD = cnt["dG"]

        for ci in range(nch):
            w = res.pop(f"norm{ci}") + freew("psT")
            for dc in range(8):
                emit("tensor", tr(psT[:, 128 * dc:128 * dc + 128],
                                  stream[:, DM * ci + 128 * dc:DM * ci + 128 * dc + 128]),
                     waits=w if dc == 0 else (), inc=("sT", 1) if dc == 7 else None)
                w = ()
            ty = cnt["sT"]
            emit("vector", cp(xfm[:, :], psT[:, :]),
                 waits=[("sT", ty)] + freew("xfm"), inc=("sV", 1))
            reads("psT", "sV")
            y1ev = cnt["sV"]

            hm = []
            for half in range(2):
                w = [("sV", y1ev), ("dG", W_GU)] + freew(
                    "psA", "psB", "psC", "psD", "psE", "psF")
                for dc in range(8):
                    lhsT = xfm[:, 128 * dc:128 * dc + 128]
                    for k, ps in enumerate((psA, psB, psC)):
                        emit("tensor", mm(ps[:, 0:512], lhsT,
                                          wg_sb[:, 3072 * dc + 1536 * half + 512 * k:
                                                3072 * dc + 1536 * half + 512 * k + 512],
                                          dc == 0, dc == 7),
                             waits=w if (dc == 0 and k == 0) else ())
                        w = ()
                    for k, ps in enumerate((psD, psE, psF)):
                        emit("tensor", mm(ps[:, 0:512], lhsT,
                                          wu_sb[:, 3072 * dc + 1536 * half + 512 * k:
                                                3072 * dc + 1536 * half + 512 * k + 512],
                                          dc == 0, dc == 7),
                             inc=("sT", 1) if (dc == 7 and k == 2) else None)
                tgu = cnt["sT"]
                for k, (psg, psu2) in enumerate(((psA, psD), (psB, psE), (psC, psF))):
                    emit("scalar", lambda e, k=k, psg=psg: e.activation(
                        silu_sb[:, 512 * k:512 * k + 512], psg[:, 0:512], AF.Silu,
                        scale=cst_sb[:, 5:6]),
                         waits=[("sT", tgu)] + (freew("silu") if k == 0 else []),
                         inc=("sA", 1))
                    sk = cnt["sA"]
                    gname = ("psA", "psB", "psC")[k]
                    reads(gname, "sA")
                    emit("vector", lambda e, k=k, half=half, psu2=psu2: e.tensor_tensor(
                        h_tok[:, 1536 * half + 512 * k:1536 * half + 512 * k + 512],
                        silu_sb[:, 512 * k:512 * k + 512], psu2[:, 0:512], ALU.mult),
                         waits=[("sA", sk)] + (freew("htok") if (half == 0 and k == 0) else []),
                         inc=("sV", 1))
                    uname = ("psD", "psE", "psF")[k]
                    reads(uname, "sV")
                hm.append(cnt["sV"])
                res["silu"] = [("sV", cnt["sV"])]

            hevs = []
            for grp in range(3):
                phb, phn = (psT, "psT") if grp % 2 == 0 else (psU, "psU")
                w = [("sV", hm[1])] + freew(phn)
                for i in range(8):
                    hc = 8 * grp + i
                    emit("tensor", tr(phb[:, 128 * i:128 * i + 128],
                                      h_tok[:, 128 * hc:128 * hc + 128]),
                         waits=w if i == 0 else (), inc=("sT", 1) if i == 7 else None)
                    w = ()
                tg = cnt["sT"]
                emit("scalar", lambda e, phb=phb, grp=grp: e.copy(
                    hfm[:, 1024 * grp:1024 * grp + 1024], phb[:, :]),
                     waits=[("sT", tg)] + (freew("hfm") if grp == 0 else []),
                     inc=("sA", 1))
                reads(phn, "sA")
                hevs.append(cnt["sA"])

            w = [("sA", hevs[2]), ("dG", W_WD)] + freew("psA", "psB")
            for hc in range(24):
                emit("tensor", mm(psA[:, 0:512], hfm[:, 128 * hc:128 * hc + 128],
                                  wd_sb[:, 1024 * hc:1024 * hc + 512], hc == 0, hc == 23),
                     waits=w if hc == 0 else ())
                w = ()
                emit("tensor", mm(psB[:, 0:512], hfm[:, 128 * hc:128 * hc + 128],
                                  wd_sb[:, 1024 * hc + 512:1024 * hc + 1024], hc == 0, hc == 23),
                     inc=("sT", 1) if hc == 23 else None)
            td = cnt["sT"]
            res["hfm"] = [("sT", td)]
            res["htok"] = [("sT", td)]
            emit("vector", lambda e, ci=ci: e.scalar_tensor_tensor(
                stream[:, DM * ci:DM * ci + 512], psA[:, 0:512], FFSC,
                stream[:, DM * ci:DM * ci + 512], ALU.mult, ALU.add),
                 waits=[("sT", td)])
            emit("vector", lambda e, ci=ci: e.scalar_tensor_tensor(
                stream[:, DM * ci + 512:DM * ci + 1024], psB[:, 0:512], FFSC,
                stream[:, DM * ci + 512:DM * ci + 1024], ALU.mult, ALU.add),
                 inc=("sV", 1))
            zr = cnt["sV"]
            res["psA"] = [("sV", zr)]
            res["psB"] = [("sV", zr)]
            res[f"z{ci}"] = [("sV", zr)]

        if dumps:
            dump("z0", stream[:, 0:DM], DM)

        for ci in range(nch):
            w = res.pop(f"z{ci}")
            emit("scalar", lambda e, ci=ci: e.activation(
                attnfm[:, :], stream[:, DM * ci:DM * ci + DM],
                AF.Square, accum_out=ln_sb[:, 16 + ci:17 + ci]),
                 waits=w, inc=("sA", 1))
            emit("vector", lambda e, ci=ci: e.reduce_sum(
                ln_sb[:, ci:ci + 1], stream[:, DM * ci:DM * ci + DM], AX),
                 waits=w, inc=("sV", 1))
        vneg2 = ln_finish()
        for ci in range(nch):
            stg = ostage[:, 1024 * (ci % 2):1024 * (ci % 2) + 1024]
            w = [("sV", vneg2)] if ci == 0 else []
            if ci >= 2:
                w.append(("dS", 16 * (ci - 1)))
            emit("vector", lambda e, ci=ci, stg=stg: e.tensor_scalar(
                stg, stream[:, DM * ci:DM * ci + DM],
                ln_sb[:, 48 + ci:49 + ci], ln_sb[:, 64 + ci:65 + ci],
                ALU.mult, ALU.add),
                 waits=w, inc=("sV", 1))
            nv = cnt["sV"]
            emit("scalar", lambda e, ci=ci, stg=stg: e.dma_start(
                out=y_h[C * ci:C * ci + C, :], in_=stg),
                 waits=[("sV", nv)], inc=("dS", 16))

        final_dS = cnt["dS"]
        final_dD = cnt["dD"]
        emit("scalar", lambda e: e.nop(), waits=[("dS", final_dS)])
        emit("sync", lambda e: e.nop(),
             waits=[("dS", final_dS)] + ([("dD", final_dD)] if final_dD else []))
        emit("gpsimd", lambda e: e.nop(), waits=[("dS", final_dS)])

        with nc.Block() as block:
            def runner(eng_name):
                def go(e):
                    for fn, waits, inc in streams[eng_name]:
                        for s, v in waits:
                            e.wait_ge(sems[s], v)
                        i = fn(e)
                        if inc:
                            i.then_inc(sems[inc[0]], inc[1])
                return go
            block.sync(runner("sync"))
            block.gpsimd(runner("gpsimd"))
            block.vector(runner("vector"))
            block.scalar(runner("scalar"))
            block.tensor(runner("tensor"))

    return nc


def make_inputs_for_core(core, x_full, w):
    b, half = core // 2, core % 2
    x_main = np.ascontiguousarray(x_full[b, half * T:(half + 1) * T])
    if half == 1:
        x_pre = np.ascontiguousarray(x_full[b, 0:T])
        pmask = 1.0
    else:
        x_pre = np.zeros((T, DM), np.float32)
        pmask = 0.0
    cst = np.zeros((128, 8), np.float32)
    cst[:, 0] = 1.0
    cst[:, 1] = pmask
    cst[:, 2] = 4.0
    cst[:, 3] = LNEPS
    cst[:, 4] = -0.5
    cst[:, 5] = 1.0 / 127.0
    m = {"x": x_main, "xp": x_pre, "cst": cst}
    m.update(w)
    return m


def shared_weights(inputs):
    ident = np.eye(128, dtype=np.float32)
    m32 = (np.triu(np.ones((128, 128), np.float32)) / 32.0).astype(np.float32)
    return {
        "wq": np.ascontiguousarray(inputs["L0_attn_Wq"], np.float32),
        "wk": np.ascontiguousarray(inputs["L0_attn_Wk"], np.float32),
        "wv": np.ascontiguousarray(inputs["L0_attn_Wv"], np.float32),
        "wo": np.ascontiguousarray(inputs["L0_attn_Wo"], np.float32),
        "wg": np.ascontiguousarray(inputs["L0_ffn_w_gate"].T),
        "wu": np.ascontiguousarray(inputs["L0_ffn_w_up"].T),
        "wd": np.ascontiguousarray(inputs["L0_ffn_w_down"].T),
        "ident": ident,
        "m32": m32,
    }


def kernel(L0_attn_hidden_states, L0_attn_Wq, L0_attn_Wk, L0_attn_Wv,
           L0_attn_Wo, L0_ffn_w_gate, L0_ffn_w_up, L0_ffn_w_down):
    inputs = {
        "L0_attn_hidden_states": np.asarray(L0_attn_hidden_states),
        "L0_attn_Wq": np.asarray(L0_attn_Wq), "L0_attn_Wk": np.asarray(L0_attn_Wk),
        "L0_attn_Wv": np.asarray(L0_attn_Wv), "L0_attn_Wo": np.asarray(L0_attn_Wo),
        "L0_ffn_w_gate": np.asarray(L0_ffn_w_gate),
        "L0_ffn_w_up": np.asarray(L0_ffn_w_up),
        "L0_ffn_w_down": np.asarray(L0_ffn_w_down),
    }
    x = inputs["L0_attn_hidden_states"]
    w = shared_weights(inputs)
    nc = build_nc()
    in_maps = [make_inputs_for_core(c, x, w) for c in range(8)]
    res = run_bass_kernel_spmd(nc, in_maps, core_ids=list(range(8)))
    out = np.empty((4, 4096, DM), np.float32)
    for c in range(8):
        b, half = c // 2, c % 2
        out[b, half * T:(half + 1) * T] = res.results[c]["y"]
    return out


# version 03
# speedup vs baseline: 1.0123x; 1.0123x over previous
import contextlib
import numpy as np
import concourse.bass as bass
import concourse.mybir as mybir
from concourse.bass_utils import run_bass_kernel_spmd

F32 = mybir.dt.float32
BF16 = mybir.dt.bfloat16
I32 = mybir.dt.int32
AF = mybir.ActivationFunctionType
ALU = mybir.AluOpType
AX = mybir.AxisListType.X

C = 128
H = 16
DM = 1024
DFF = 3072
T = 2048
DPHI = 273
PACK = 195
EPS = 1e-9
LNEPS = 1e-5
KQSC = 1.0 / 32.0
KRSC = 0.25
FFSC = float(1.0 / (127.0 * 127.0))


def build_nc(nch=T // C, npre=T // C, dumps=(), pus1_sbuf=False, late_div=False,
             attnfm_sep=False, rec_sep=False, phiq3=True):
    nc = bass.Bass("TRN2", target_bir_lowering=False)
    x_h = nc.dram_tensor("x", [T, DM], F32, kind="ExternalInput").ap()
    xt_h = nc.dram_tensor("xt", [DM, T], F32, kind="ExternalInput").ap()
    xpt_h = nc.dram_tensor("xpt", [DM, T], F32, kind="ExternalInput").ap()
    cst_h = nc.dram_tensor("cst", [128, 8], F32, kind="ExternalInput").ap()
    wq_h = nc.dram_tensor("wq", [DM, 256], F32, kind="ExternalInput").ap()
    wk_h = nc.dram_tensor("wk", [DM, 256], F32, kind="ExternalInput").ap()
    wv_h = nc.dram_tensor("wv", [DM, DM], F32, kind="ExternalInput").ap()
    wo_h = nc.dram_tensor("wo", [DM, DM], F32, kind="ExternalInput").ap()
    wg_h = nc.dram_tensor("wg", [DM, DFF], I32, kind="ExternalInput").ap()
    wu_h = nc.dram_tensor("wu", [DM, DFF], I32, kind="ExternalInput").ap()
    wd_h = nc.dram_tensor("wd", [DFF, DM], I32, kind="ExternalInput").ap()
    id_h = nc.dram_tensor("ident", [128, 128], F32, kind="ExternalInput").ap()
    m32_h = nc.dram_tensor("m32", [128, 128], F32, kind="ExternalInput").ap()
    y_h = nc.dram_tensor("y", [T, DM], F32, kind="ExternalOutput").ap()
    dump_h = {}
    for tag, shape in dumps:
        dump_h[tag] = nc.dram_tensor("dbg_" + tag, shape, F32, kind="ExternalOutput").ap()

    streams = {e: [] for e in ("sync", "gpsimd", "vector", "scalar", "tensor")}
    cnt = {s: 0 for s in ("dG", "dS", "dD", "sT", "sV", "sA", "sG")}
    res = {}

    pending_dump = {}

    def emit(eng, fn, waits=(), inc=None):
        wmax = {}
        pd = pending_dump.pop(eng, None)
        waits = list(waits) + ([("dD", pd)] if pd else [])
        for s, v in waits:
            assert v <= cnt[s], f"wait {s}>={v}, emitted {cnt[s]}"
            if v > 0:
                wmax[s] = max(wmax.get(s, 0), v)
        if inc:
            s, n = inc
            cnt[s] += n
        streams[eng].append((fn, tuple(wmax.items()), inc))

    def freew(*names):
        w = []
        for n in names:
            w += res.pop(n, [])
        return w

    def reads(name, sem):
        res.setdefault(name, []).append((sem, cnt[sem]))

    with contextlib.ExitStack() as stk:
        E = stk.enter_context
        stream = E(nc.sbuf_tensor("stream", [128, nch * DM], BF16))
        xfm = E(nc.sbuf_tensor("xfm", [128, 2 * DM], BF16))
        warena = E(nc.sbuf_tensor("warena", [128, 24576], BF16))
        wg_sb = E(nc.sbuf_tensor("wg_sb", [128, 24576], BF16))
        wu_sb = E(nc.sbuf_tensor("wu_sb", [128, 24576], BF16))
        scr = E(nc.sbuf_tensor("scr", [128, 13188], BF16))
        m32_sb = E(nc.sbuf_tensor("m32_sb", [128, 128], BF16))
        id_sb = E(nc.sbuf_tensor("id_sb", [128, 128], BF16))
        cst_sb = E(nc.sbuf_tensor("cst_sb", [128, 8], F32))
        den_sb = E(nc.sbuf_tensor("den_sb", [128, 16], F32))
        ln_sb = E(nc.sbuf_tensor("ln_sb", [128, 80], F32))
        pu_s1t = E(nc.sbuf_tensor("pu_s1t", [128, 256], F32)) if pus1_sbuf else None
        attnfm_t = E(nc.sbuf_tensor("attnfm_t", [128, 1024], BF16)) if attnfm_sep else None
        rec_t = E(nc.sbuf_tensor("rec_t", [128, 16], F32)) if rec_sep else None

        rec_sb = ln_sb[:, 64:80]
        if rec_sep:
            rec_sb = rec_t

        xps = scr[:, 0:2048]
        qk_tok = scr[:, 0:512]
        ksc = scr[:, 512:768]
        qk_pad = scr[:, 768:1792]
        pu_sb = scr[:, 1792:2048]
        pk_tok = scr[:, 12164:12676]
        pk_sc = scr[:, 12676:12932]
        phik = scr[:, 2048:6416]
        phiq = scr[:, 6416:7235]
        NQSL = 3
        phiqfm = scr[:, 7235:8003]
        v_sb = scr[:, 8003:9043]
        state = scr[:, 9043:12163]
        attn = scr[:, 12164:13188]
        attnfm = scr[:, 768:1792]
        if attnfm_sep:
            attnfm = attnfm_t
        h_tok = scr[:, 0:3072]
        hfm = scr[:, 3072:6144]
        silu_sb = scr[:, 6144:7680]
        ostage = scr[:, 7680:11776].bitcast(F32)

        psA = E(nc.psum_tensor("psA", [128, 512], F32))
        psB = E(nc.psum_tensor("psB", [128, 512], F32))
        psC = E(nc.psum_tensor("psC", [128, 512], F32))
        psD = E(nc.psum_tensor("psD", [128, 512], F32))
        psE = E(nc.psum_tensor("psE", [128, 512], F32))
        psF = E(nc.psum_tensor("psF", [128, 512], F32))
        psT = E(nc.psum_tensor("psT", [128, 1024], BF16))
        psU = E(nc.psum_tensor("psU", [128, 1024], BF16))

        sems = {s: E(nc.semaphore(s)) for s in cnt}

        wq_sb = warena[:, 0:2048]
        wk_sb = warena[:, 2048:4096]
        wv_sb = warena[:, 4096:12288]
        wo_sb = warena[:, 12288:20480]
        wd_sb = warena[:, 0:24576]

        def dma(dst, src):
            return lambda e, d=dst, s=src: e.dma_start(out=d, in_=s)

        emit("gpsimd", dma(id_sb[:, :], id_h[:, :]), inc=("dG", 16))

        emit("gpsimd", dma(wq_sb.rearrange("p (d j) -> p d j", d=8),
                           wq_h.rearrange("(d p) j -> p d j", p=128)), inc=("dG", 16))
        emit("gpsimd", dma(wk_sb.rearrange("p (d j) -> p d j", d=8),
                           wk_h.rearrange("(d p) j -> p d j", p=128)), inc=("dG", 16))
        emit("gpsimd", dma(wv_sb.rearrange("p (d j) -> p d j", d=8),
                           wv_h.rearrange("(d p) j -> p d j", p=128)), inc=("dG", 16))
        emit("gpsimd", dma(wo_sb.rearrange("p (d j) -> p d j", d=8),
                           wo_h.rearrange("(d p) j -> p d j", p=128)), inc=("dG", 16))
        W_ATTN = cnt["dG"]
        emit("gpsimd", dma(m32_sb[:, :], m32_h[:, :]), inc=("dG", 16))
        emit("sync", dma(cst_sb[:, :], cst_h[:, :]), inc=("dG", 16))
        W_CONST = cnt["dG"]

        emit("vector", lambda e: e.memset(
            phik.rearrange("p (h d) -> p h d", h=16)[:, :, 0:1], 1.0))
        emit("vector", lambda e: e.memset(
            phiq.rearrange("p (s d) -> p s d", s=3)[:, :, 0:1], 1.0))
        emit("vector", lambda e: e.memset(state[:, :], 0.0), inc=("sV", 1))
        res["state"] = [("sV", cnt["sV"])]

        emit("gpsimd", dma(wg_sb.rearrange("p (d j) -> p d j", d=8),
                           wg_h.rearrange("(d p) j -> p d j", p=128)), inc=("dG", 16))
        emit("gpsimd", dma(wu_sb.rearrange("p (d j) -> p d j", d=8),
                           wu_h.rearrange("(d p) j -> p d j", p=128)), inc=("dG", 16))
        W_GU = cnt["dG"]

        def tr(out, src, tp=None):
            return lambda e, o=out, s=src, t=tp: e.transpose(o, s, id_sb[:, :],
                                                             tile_position=t)

        def mm(out, lhsT, rhs, start, stop, tp=None):
            return lambda e, o=out, l=lhsT, r=rhs, a=start, b=stop, t=tp: e.matmul(
                o, l, r, start=a, stop=b, tile_position=t)

        def cp(eng_dst, src):
            return lambda e, d=eng_dst, s=src: e.tensor_copy(d, s)

        def out_slot(h):
            if h < 7:
                return psA, "psA", 65 * h
            if h < 14:
                return psB, "psB", 65 * (h - 7)
            return psC, "psC", 130 + 65 * (h - 14)

        addsnap = {h: ("sV", 0) for h in range(H)}

        def dump(tag, src_bf16, n_f32_cols=None):
            if tag in dump_h:
                w = [(s, cnt[s]) for s in ("sT", "sV", "sA", "sG")]
                emit("gpsimd", dma(dump_h[tag][:, :], src_bf16), waits=w, inc=("dD", 16))
                for eng in streams:
                    pending_dump[eng] = cnt["dD"]

        def scan_chunk(ci, xslot, vcol, kv_only, xld, qk_tok=qk_tok, ksc=ksc,
                       dump_cb=None):
            xf = xfm[:, DM * xslot:DM * xslot + DM]
            w = [("dG", xld), ("dG", W_CONST), ("dG", W_ATTN)] + freew(
                "psA", "psB", "psC")
            for dc in range(8):
                lhsT = xf[:, 128 * dc:128 * dc + 128]
                if not kv_only:
                    emit("tensor", mm(psA[:, 0:256], lhsT,
                                      wq_sb[:, 256 * dc:256 * dc + 256], dc == 0, dc == 7),
                         waits=w if dc == 0 else ())
                    w = ()
                emit("tensor", mm(psA[:, 256:512], lhsT,
                                  wk_sb[:, 256 * dc:256 * dc + 256],
                                  (dc == 0) and kv_only, dc == 7),
                     waits=w if dc == 0 else ())
                w = ()
                emit("tensor", mm(psB[:, 0:512], lhsT,
                                  wv_sb[:, 1024 * dc:1024 * dc + 512], dc == 0, dc == 7))
                emit("tensor", mm(psC[:, 0:512], lhsT,
                                  wv_sb[:, 1024 * dc + 512:1024 * dc + 1024], dc == 0, dc == 7),
                     inc=("sT", 1) if dc == 7 else None)
            tproj = cnt["sT"]

            lo = 256 if kv_only else 0
            emit("vector", cp(qk_tok[:, lo:512], psA[:, lo:512]),
                 waits=[("sT", tproj)] + freew("qk"))
            emit("vector", cp(v_sb.rearrange("p (h v) -> p h v", h=16)[:, 0:8, 0:64],
                              psB[:, 0:512].rearrange("p (h v) -> p h v", h=8)),
                 waits=freew("v"))
            emit("vector", cp(v_sb.rearrange("p (h v) -> p h v", h=16)[:, 8:16, 0:64],
                              psC[:, 0:512].rearrange("p (h v) -> p h v", h=8)))
            emit("vector", cp(v_sb.rearrange("p (h v) -> p h v", h=16)[:, :, 64:65],
                              cst_sb[:, vcol:vcol + 1].unsqueeze(1).broadcast_to([128, 16, 1])))
            emit("vector", lambda e: e.tensor_scalar(
                ksc[:, 0:256], qk_tok[:, 256:512], KQSC, None, ALU.mult),
                 waits=freew("qksc"))
            emit("vector", lambda e: e.tensor_scalar(
                phik.rearrange("p (h d) -> p h d", h=16)[:, :, 1:17],
                qk_tok[:, 256:512].rearrange("p (h f) -> p h f", h=16),
                KRSC, None, ALU.mult), waits=freew("phikraw"), inc=("sV", 1))
            projev = cnt["sV"]
            reads("psA", "sV")
            reads("psB", "sV")
            reads("psC", "sV")

            for h in range(H):
                w = [("sV", projev)] + freew(f"phik{h}")
                def kq(e, h=h):
                    o = phik[:, DPHI * h + 17:DPHI * h + 273].rearrange(
                        "p (i j) -> p i j", i=16)
                    i0 = ksc[:, 16 * h:16 * h + 16].unsqueeze(2).broadcast_to([128, 16, 16])
                    i1 = qk_tok[:, 256 + 16 * h:256 + 16 * h + 16].unsqueeze(1).broadcast_to([128, 16, 16])
                    return e.tensor_tensor(o, i0, i1, ALU.mult)
                emit("gpsimd", kq, waits=w, inc=("sG", 1))
                res[f"kfm{h}"] = [("sG", cnt["sG"])]

            if kv_only:
                for h in range(H):
                    ps, bank = (psE, "psE") if h % 2 == 0 else (psF, "psF")
                    c0 = PACK * h
                    w = res[f"kfm{h}"] + freew(bank)
                    emit("tensor", mm(ps[:, 0:65], phik[:, DPHI * h:DPHI * h + 128],
                                      v_sb[:, 65 * h:65 * h + 65], True, True), waits=w)
                    emit("tensor", mm(ps[:, 65:130], phik[:, DPHI * h + 128:DPHI * h + 256],
                                      v_sb[:, 65 * h:65 * h + 65], True, True))
                    emit("tensor", mm(ps[0:17, 130:195], phik[:, DPHI * h + 256:DPHI * h + 273],
                                      v_sb[:, 65 * h:65 * h + 65], True, True),
                         inc=("sT", 1))
                    stv = cnt["sT"]
                    res[f"phik{h}"] = [("sT", stv)]
                    emit("vector", lambda e, ps=ps, c0=c0: e.tensor_tensor(
                        state[:, c0:c0 + PACK], ps[:, 0:PACK],
                        state[:, c0:c0 + PACK], ALU.add),
                         waits=[("sT", stv)] + freew("state") + [addsnap[h]],
                         inc=("sV", 1))
                    addsnap[h] = ("sV", cnt["sV"])
                    reads(bank, "sV")
                res["v"] = [("sT", cnt["sT"])]
                res["qk"] = [("sG", cnt["sG"]), ("sT", cnt["sT"])]
                res["qksc"] = [("sG", cnt["sG"])]
                res["phikraw"] = [("sG", cnt["sG"]), ("sT", cnt["sT"])]
                return

            w = [("sV", projev)] + freew("psT")
            for h in range(H):
                j, g = h % 4, h // 4
                emit("tensor", tr(psT[32 * j:32 * j + 16, 128 * g:128 * g + 128],
                                  qk_tok[:, 16 * h:16 * h + 16], (0, 32 * j)),
                     waits=w if h == 0 else ())
                w = ()
            for h in range(H):
                j, g = h % 4, h // 4
                emit("tensor", tr(psT[32 * j:32 * j + 16, 512 + 128 * g:512 + 128 * g + 128],
                                  qk_tok[:, 256 + 16 * h:256 + 16 * h + 16], (0, 32 * j)),
                     inc=("sT", 1) if h == H - 1 else None)
            tpads = cnt["sT"]
            emit("vector", cp(qk_pad[:, :], psT[:, :]),
                 waits=[("sT", tpads)] + freew("attnfm"), inc=("sV", 1))
            padsev = cnt["sV"]
            reads("psT", "sV")
            if dump_cb:
                dump_cb("pads0", qk_pad[:, :])

            for h in range(H):
                j, g = h % 4, h // 4
                osl, obank, ocol = out_slot(h)
                slot2 = h % 2
                slot4 = h % NQSL
                ubank, uname = (psC, "psC") if slot2 == 0 else (psD, "psD")
                phb, phbname = (psT, "psT") if slot2 == 0 else (psU, "psU")
                phislot = phiq[:, DPHI * slot4:DPHI * slot4 + DPHI]
                phifm = phiqfm[:, 384 * slot2:384 * slot2 + 384]
                pus1 = pu_s1t[:, 128 * slot2:128 * slot2 + 128] if pus1_sbuf else ubank[:, 384:512]
                pusl = pu_sb[:, 128 * slot2:128 * slot2 + 128]

                emit("vector", lambda e, h=h, sl=phislot: e.tensor_copy(
                    sl[:, 1:17], qk_tok[:, 16 * h:16 * h + 16]),
                     waits=freew(f"phiq{slot4}"))
                def qq(e, h=h, sl=phislot):
                    o = sl[:, 17:273].rearrange("p (i j) -> p i j", i=16)
                    i0 = qk_tok[:, 16 * h:16 * h + 16].unsqueeze(2).broadcast_to([128, 16, 16])
                    i1 = qk_tok[:, 16 * h:16 * h + 16].unsqueeze(1).broadcast_to([128, 16, 16])
                    return e.tensor_tensor(o, i0, i1, ALU.mult)
                emit("vector", qq, inc=("sV", 1))
                qqv = cnt["sV"]

                w = [("sV", padsev)] + freew(uname)
                emit("tensor", mm(ubank[:, 0:128],
                                  qk_pad[32 * j:32 * j + 16, 512 + 128 * g:512 + 128 * g + 128],
                                  qk_pad[32 * j:32 * j + 16, 128 * g:128 * g + 128],
                                  True, True, (32 * j, 0)), waits=w, inc=("sT", 1))
                tu = cnt["sT"]

                emit("scalar", lambda e, ub=ubank, s1=pus1: e.activation(
                    s1, ub[:, 0:128], AF.Square, bias=cst_sb[:, 2:3]),
                     waits=[("sT", tu)] + freew(f"pus1_{slot2}"), inc=("sA", 1))
                reads(uname, "sA")
                polyA = cnt["sA"]

                emit("vector", lambda e, s1=pus1, pu=pusl: e.scalar_tensor_tensor(
                    pu, s1, 16.0, m32_sb[:, :], ALU.add, ALU.mult),
                     waits=[("sA", polyA)] + freew(f"pu_{slot2}"), inc=("sV", 1))
                res[f"pus1_{slot2}"] = [("sV", cnt["sV"])]
                reads(uname, "sV")
                puV = cnt["sV"]
                if dump_cb and h == 0:
                    dump_cb("pu0", pusl)
                if dump_cb and h == 1:
                    emit("vector", cp(attn.bitcast(F32)[:, 0:128], pus1),
                         waits=[("sV", puV)], inc=("sV", 1))
                    dump_cb("s1h1", attn.bitcast(F32)[:, 0:128])
                    dump_cb("pu1", pusl)

                w = [("sV", qqv)] + freew(phbname)
                emit("tensor", tr(phb[:, 0:128], phislot[:, 0:128]), waits=w)
                emit("tensor", tr(phb[:, 128:256], phislot[:, 128:256]))
                emit("tensor", tr(phb[0:17, 256:384], phislot[:, 256:273]), inc=("sT", 1))
                res[f"phiq{slot4}"] = [("sT", cnt["sT"])]
                tphi = cnt["sT"]

                emit("scalar", lambda e, pb=phb, pf=phifm: e.copy(pf, pb[:, 0:384]),
                     waits=[("sT", tphi)] + freew(f"phifm{slot2}"), inc=("sA", 1))
                reads(phbname, "sA")
                phievA = cnt["sA"]
                if dump_cb and h == 0:
                    dump_cb("phiq0", phislot)
                    dump_cb("phifm0", phifm)

                emit("tensor", mm(osl[:, ocol:ocol + 65], pusl,
                                  v_sb[:, 65 * h:65 * h + 65], h in (0, 7, 14), False),
                     waits=[("sV", puV)], inc=("sT", 1))
                res[f"pu_{slot2}"] = [("sT", cnt["sT"])]

                c0 = PACK * h
                w = [("sA", phievA), addsnap[h]]
                emit("tensor", mm(osl[:, ocol:ocol + 65], phifm[:, 0:128],
                                  state[:, c0:c0 + 65], False, False), waits=w)
                emit("tensor", mm(osl[:, ocol:ocol + 65], phifm[:, 128:256],
                                  state[:, c0 + 65:c0 + 130], False, False))
                emit("tensor", mm(osl[:, ocol:ocol + 65], phifm[0:17, 256:384],
                                  state[0:17, c0 + 130:c0 + 195], False, True),
                     inc=("sT", 1))
                res[f"phifm{slot2}"] = [("sT", cnt["sT"])]

                ps, bank = (psE, "psE") if slot2 == 0 else (psF, "psF")
                w = res[f"kfm{h}"] + freew(bank)
                emit("tensor", mm(ps[:, 0:65], phik[:, DPHI * h:DPHI * h + 128],
                                  v_sb[:, 65 * h:65 * h + 65], True, True), waits=w)
                emit("tensor", mm(ps[:, 65:130], phik[:, DPHI * h + 128:DPHI * h + 256],
                                  v_sb[:, 65 * h:65 * h + 65], True, True))
                emit("tensor", mm(ps[0:17, 130:195], phik[:, DPHI * h + 256:DPHI * h + 273],
                                  v_sb[:, 65 * h:65 * h + 65], True, True),
                     inc=("sT", 1))
                stv = cnt["sT"]
                res[f"phik{h}"] = [("sT", stv)]
                emit("vector", lambda e, ps=ps, c0=c0: e.tensor_tensor(
                    state[:, c0:c0 + PACK], ps[:, 0:PACK],
                    state[:, c0:c0 + PACK], ALU.add),
                     waits=[("sT", stv)] + freew("state") + [addsnap[h]], inc=("sV", 1))
                addsnap[h] = ("sV", cnt["sV"])
                reads(bank, "sV")

                div_heads = (15,) if late_div else (6, 13, 15)
                if h in div_heads:
                  for h, glo, gbank, gname in ([(6, 0, psA, "psA"), (13, 7, psB, "psB"),
                                                (15, 14, psC, "psC")] if late_div else
                                               [(h, {6: 0, 13: 7, 15: 14}[h],
                                                 {6: psA, 13: psB, 15: psC}[h],
                                                 {6: "psA", 13: "psB", 15: "psC"}[h])]):
                      nh = h + 1 - glo
                      w = [("sT", stv)]
                      if h == 6:
                          w += freew("attn")
                      if gname == "psC":
                          dsrc = psC[:, 130:260].rearrange("p (h v) -> p h v", h=2)[:, :, 64:65].squeeze(2)
                      else:
                          dsrc = gbank[:, 0:455].rearrange("p (h v) -> p h v", h=7)[:, :, 64:65].squeeze(2)
                      emit("vector", cp(den_sb[:, glo:glo + nh], dsrc),
                           waits=w, inc=("sV", 1))
                      genv = cnt["sV"]
                      emit("vector", lambda e, glo=glo, nh=nh: e.tensor_scalar(
                          den_sb[:, glo:glo + nh], den_sb[:, glo:glo + nh],
                          EPS, None, ALU.add), waits=[("sV", genv)], inc=("sV", 1))
                      epsv = cnt["sV"]
                      emit("vector", lambda e, glo=glo, nh=nh: e.reciprocal(
                          rec_sb[:, glo:glo + nh], den_sb[:, glo:glo + nh]),
                           waits=[("sV", epsv)], inc=("sV", 1))
                      recv = cnt["sV"]
                      for h2 in range(glo, h + 1):
                          osl2, _, ocol2 = out_slot(h2)
                          emit("vector", lambda e, h2=h2, osl2=osl2, ocol2=ocol2: e.tensor_scalar(
                              attn[:, 64 * h2:64 * h2 + 64], osl2[:, ocol2:ocol2 + 64],
                              rec_sb[:, h2:h2 + 1], None, ALU.mult),
                               waits=[("sV", recv)] if h2 == glo else (),
                               inc=("sV", 1) if h2 == h else None)
                      reads(gname, "sV")

            divV = cnt["sV"]
            theads = cnt["sT"]
            res["v"] = [("sT", theads)]
            res["qk"] = [("sT", theads), ("sG", cnt["sG"]), ("sV", cnt["sV"])]
            res["qksc"] = [("sG", cnt["sG"]), ("sV", cnt["sV"])]
            res["phikraw"] = [("sG", cnt["sG"]), ("sT", theads)]

            if dump_cb:
                emit("vector", cp(attnfm.bitcast(F32)[:, 0:455], psA[:, 0:455]),
                     waits=[("sT", theads)] + freew("attnfm"), inc=("sV", 1))
                res["attnfm"] = [("sV", cnt["sV"])]
                dump_cb("o0", attnfm.bitcast(F32)[:, 0:455])
                dump_cb("den0", den_sb[:, :])
                dump_cb("rec0", rec_sb[:, :])

            w = [("sV", divV)] + freew("psT")
            for dc in range(8):
                emit("tensor", tr(psT[:, 128 * dc:128 * dc + 128],
                                  attn[:, 128 * dc:128 * dc + 128]),
                     waits=w if dc == 0 else (), inc=("sT", 1) if dc == 7 else None)
                w = ()
            tat = cnt["sT"]
            res["attn"] = [("sT", tat)]
            emit("scalar", lambda e: e.copy(attnfm[:, :], psT[:, :]),
                 waits=[("sT", tat)] + freew("attnfm"), inc=("sA", 1))
            reads("psT", "sA")
            atev = cnt["sA"]
            w = [("sA", atev)] + freew("psD", "psF")
            for dc in range(8):
                emit("tensor", mm(psD[:, 0:512], attnfm[:, 128 * dc:128 * dc + 128],
                                  wo_sb[:, 1024 * dc:1024 * dc + 512], dc == 0, dc == 7),
                     waits=w if dc == 0 else ())
                w = ()
                emit("tensor", mm(psF[:, 0:512], attnfm[:, 128 * dc:128 * dc + 128],
                                  wo_sb[:, 1024 * dc + 512:1024 * dc + 1024], dc == 0, dc == 7),
                     inc=("sT", 1) if dc == 7 else None)
            two = cnt["sT"]
            res["attnfm"] = [("sT", two)]
            emit("vector", lambda e, ci=ci: e.tensor_tensor(
                stream[:, DM * ci:DM * ci + 512], psD[:, 0:512],
                stream[:, DM * ci:DM * ci + 512], ALU.add), waits=[("sT", two)])
            emit("vector", lambda e, ci=ci: e.tensor_tensor(
                stream[:, DM * ci + 512:DM * ci + 1024], psF[:, 0:512],
                stream[:, DM * ci + 512:DM * ci + 1024], ALU.add), inc=("sV", 1))
            rv = cnt["sV"]
            reads("psD", "sV")
            reads("psF", "sV")
            res["wo_resid"] = [("sV", rv)]

        def load_xfm(src_t, ci, slot):
            emit("gpsimd", dma(
                xfm[:, DM * slot:DM * slot + DM].rearrange("p (dc t) -> p dc t", dc=8),
                src_t.rearrange("(dc p) t -> p dc t", p=128)[:, :, C * ci:C * ci + C]),
                 waits=freew(f"xfm{slot}"), inc=("dG", 16))
            return cnt["dG"]

        for ci in range(npre):
            xld = load_xfm(xpt_h, ci, ci % 2)
            scan_chunk(ci, ci % 2, 1, True, xld, qk_tok=pk_tok, ksc=pk_sc)
            res[f"xfm{ci % 2}"] = [("sT", cnt["sT"])]

        if dumps:
            dump("state_pre", state[:, :], PACK * H)

        for ci in range(nch):
            emit("gpsimd", dma(stream[:, DM * ci:DM * ci + DM], x_h[C * ci:C * ci + C, :]),
                 inc=("dG", 16))
            xld = load_xfm(xt_h, ci, ci % 2)
            scan_chunk(ci, ci % 2, 0, False, xld,
                       dump_cb=(dump if ci == 0 else None))
            res[f"xfm{ci % 2}"] = [("sT", cnt["sT"])]
            if ci == 0:
                dump("qk0", qk_tok[:, :], 512)
                dump("v0", v_sb[:, :], 1040)
                dump("phik0", phik[:, :], DPHI * H)
                dump("attn0", attn[:, :], DM)
                dump("state0", state[:, :], PACK * H)
            if ci == 1:
                dump("stream1", stream[:, DM:2 * DM], DM)

        def ln_stats(ci):
            w = freew("wo_resid") if ci == 0 else []
            if ci == 0:
                w += freew("attnfm")
            emit("scalar", lambda e, ci=ci: e.activation(
                attnfm[:, :], stream[:, DM * ci:DM * ci + DM],
                AF.Square, accum_out=ln_sb[:, 16 + ci:17 + ci]),
                 waits=w, inc=("sA", 1))
            emit("vector", lambda e, ci=ci: e.reduce_sum(
                ln_sb[:, ci:ci + 1], stream[:, DM * ci:DM * ci + DM], AX),
                 waits=w, inc=("sV", 1))

        def ln_finish():
            sAs, sVs = cnt["sA"], cnt["sV"]
            emit("vector", lambda e: e.tensor_scalar(
                ln_sb[:, 32:48], ln_sb[:, 0:16], 1.0 / DM, None, ALU.mult),
                 waits=[("sA", sAs), ("sV", sVs)], inc=("sV", 1))
            meanv = cnt["sV"]
            emit("vector", lambda e: e.tensor_tensor(
                den_sb[:, 0:16], ln_sb[:, 32:48], ln_sb[:, 32:48], ALU.mult),
                 waits=[("sV", meanv)], inc=("sV", 1))
            msqv = cnt["sV"]
            emit("vector", lambda e: e.scalar_tensor_tensor(
                ln_sb[:, 64:80], ln_sb[:, 16:32], 1.0 / DM, den_sb[:, 0:16],
                ALU.mult, ALU.subtract), waits=[("sV", msqv)], inc=("sV", 1))
            vv = cnt["sV"]
            emit("scalar", lambda e: e.activation(rec_sb[:, 0:16], ln_sb[:, 64:80],
                                                  AF.Ln, bias=cst_sb[:, 3:4]), waits=[("sV", vv)])
            emit("scalar", lambda e: e.activation(ln_sb[:, 48:64], rec_sb[:, 0:16],
                                                  AF.Exp, scale=cst_sb[:, 4:5]), inc=("sA", 1))
            sa = cnt["sA"]
            emit("vector", lambda e: e.scalar_tensor_tensor(
                ln_sb[:, 64:80], ln_sb[:, 32:48], -1.0, ln_sb[:, 48:64],
                ALU.mult, ALU.mult), waits=[("sA", sa)], inc=("sV", 1))
            return cnt["sV"]

        for ci in range(nch):
            ln_stats(ci)
        vneg = ln_finish()
        if dumps:
            dump("lnsb", ln_sb[:, :], 96)
        for ci in range(nch):
            emit("vector", lambda e, ci=ci: e.tensor_scalar(
                stream[:, DM * ci:DM * ci + DM], stream[:, DM * ci:DM * ci + DM],
                ln_sb[:, 48 + ci:49 + ci], ln_sb[:, 64 + ci:65 + ci],
                ALU.mult, ALU.add),
                 waits=[("sV", vneg)] if ci == 0 else (), inc=("sV", 1))
            res[f"norm{ci}"] = [("sV", cnt["sV"])]
        if dumps:
            dump("y1", stream[:, 0:DM], DM)

        emit("gpsimd", dma(wd_sb.rearrange("p (hc j) -> p hc j", hc=24),
                           wd_h.rearrange("(hc p) j -> p hc j", p=128)),
             waits=[("sT", cnt["sT"])], inc=("dG", 16))
        W_WD = cnt["dG"]

        for ci in range(nch):
            w = res.pop(f"norm{ci}") + freew("psT")
            for dc in range(8):
                emit("tensor", tr(psT[:, 128 * dc:128 * dc + 128],
                                  stream[:, DM * ci + 128 * dc:DM * ci + 128 * dc + 128]),
                     waits=w if dc == 0 else (), inc=("sT", 1) if dc == 7 else None)
                w = ()
            ty = cnt["sT"]
            emit("vector", cp(xfm[:, 0:DM], psT[:, :]),
                 waits=[("sT", ty)] + freew("xfm0") + freew("xfm1"), inc=("sV", 1))
            reads("psT", "sV")
            y1ev = cnt["sV"]

            hm = []
            for half in range(2):
                w = [("sV", y1ev), ("dG", W_GU)] + freew(
                    "psA", "psB", "psC", "psD", "psE", "psF")
                for dc in range(8):
                    lhsT = xfm[:, 128 * dc:128 * dc + 128]
                    for k, ps in enumerate((psA, psB, psC)):
                        emit("tensor", mm(ps[:, 0:512], lhsT,
                                          wg_sb[:, 3072 * dc + 1536 * half + 512 * k:
                                                3072 * dc + 1536 * half + 512 * k + 512],
                                          dc == 0, dc == 7),
                             waits=w if (dc == 0 and k == 0) else ())
                        w = ()
                    for k, ps in enumerate((psD, psE, psF)):
                        emit("tensor", mm(ps[:, 0:512], lhsT,
                                          wu_sb[:, 3072 * dc + 1536 * half + 512 * k:
                                                3072 * dc + 1536 * half + 512 * k + 512],
                                          dc == 0, dc == 7),
                             inc=("sT", 1) if (dc == 7 and k == 2) else None)
                tgu = cnt["sT"]
                for k, (psg, psu2) in enumerate(((psA, psD), (psB, psE), (psC, psF))):
                    emit("scalar", lambda e, k=k, psg=psg: e.activation(
                        silu_sb[:, 512 * k:512 * k + 512], psg[:, 0:512], AF.Silu,
                        scale=cst_sb[:, 5:6]),
                         waits=[("sT", tgu)] + (freew("silu") if k == 0 else []),
                         inc=("sA", 1))
                    sk = cnt["sA"]
                    gname = ("psA", "psB", "psC")[k]
                    reads(gname, "sA")
                    emit("vector", lambda e, k=k, half=half, psu2=psu2: e.tensor_tensor(
                        h_tok[:, 1536 * half + 512 * k:1536 * half + 512 * k + 512],
                        silu_sb[:, 512 * k:512 * k + 512], psu2[:, 0:512], ALU.mult),
                         waits=[("sA", sk)] + (freew("htok") if (half == 0 and k == 0) else []),
                         inc=("sV", 1))
                    uname = ("psD", "psE", "psF")[k]
                    reads(uname, "sV")
                hm.append(cnt["sV"])
                res["silu"] = [("sV", cnt["sV"])]

            hevs = []
            for grp in range(3):
                phb, phn = (psT, "psT") if grp % 2 == 0 else (psU, "psU")
                w = [("sV", hm[1])] + freew(phn)
                for i in range(8):
                    hc = 8 * grp + i
                    emit("tensor", tr(phb[:, 128 * i:128 * i + 128],
                                      h_tok[:, 128 * hc:128 * hc + 128]),
                         waits=w if i == 0 else (), inc=("sT", 1) if i == 7 else None)
                    w = ()
                tg = cnt["sT"]
                emit("scalar", lambda e, phb=phb, grp=grp: e.copy(
                    hfm[:, 1024 * grp:1024 * grp + 1024], phb[:, :]),
                     waits=[("sT", tg)] + (freew("hfm") if grp == 0 else []),
                     inc=("sA", 1))
                reads(phn, "sA")
                hevs.append(cnt["sA"])

            w = [("sA", hevs[2]), ("dG", W_WD)] + freew("psA", "psB")
            for hc in range(24):
                emit("tensor", mm(psA[:, 0:512], hfm[:, 128 * hc:128 * hc + 128],
                                  wd_sb[:, 1024 * hc:1024 * hc + 512], hc == 0, hc == 23),
                     waits=w if hc == 0 else ())
                w = ()
                emit("tensor", mm(psB[:, 0:512], hfm[:, 128 * hc:128 * hc + 128],
                                  wd_sb[:, 1024 * hc + 512:1024 * hc + 1024], hc == 0, hc == 23),
                     inc=("sT", 1) if hc == 23 else None)
            td = cnt["sT"]
            res["hfm"] = [("sT", td)]
            res["htok"] = [("sT", td)]
            emit("vector", lambda e, ci=ci: e.scalar_tensor_tensor(
                stream[:, DM * ci:DM * ci + 512], psA[:, 0:512], FFSC,
                stream[:, DM * ci:DM * ci + 512], ALU.mult, ALU.add),
                 waits=[("sT", td)])
            emit("vector", lambda e, ci=ci: e.scalar_tensor_tensor(
                stream[:, DM * ci + 512:DM * ci + 1024], psB[:, 0:512], FFSC,
                stream[:, DM * ci + 512:DM * ci + 1024], ALU.mult, ALU.add),
                 inc=("sV", 1))
            zr = cnt["sV"]
            res["psA"] = [("sV", zr)]
            res["psB"] = [("sV", zr)]
            res[f"z{ci}"] = [("sV", zr)]

        if dumps:
            dump("z0", stream[:, 0:DM], DM)

        for ci in range(nch):
            w = res.pop(f"z{ci}")
            emit("scalar", lambda e, ci=ci: e.activation(
                attnfm[:, :], stream[:, DM * ci:DM * ci + DM],
                AF.Square, accum_out=ln_sb[:, 16 + ci:17 + ci]),
                 waits=w, inc=("sA", 1))
            emit("vector", lambda e, ci=ci: e.reduce_sum(
                ln_sb[:, ci:ci + 1], stream[:, DM * ci:DM * ci + DM], AX),
                 waits=w, inc=("sV", 1))
        vneg2 = ln_finish()
        for ci in range(nch):
            stg = ostage[:, 1024 * (ci % 2):1024 * (ci % 2) + 1024]
            w = [("sV", vneg2)] if ci == 0 else []
            if ci >= 2:
                w.append(("dS", 16 * (ci - 1)))
            emit("vector", lambda e, ci=ci, stg=stg: e.tensor_scalar(
                stg, stream[:, DM * ci:DM * ci + DM],
                ln_sb[:, 48 + ci:49 + ci], ln_sb[:, 64 + ci:65 + ci],
                ALU.mult, ALU.add),
                 waits=w, inc=("sV", 1))
            nv = cnt["sV"]
            emit("scalar", lambda e, ci=ci, stg=stg: e.dma_start(
                out=y_h[C * ci:C * ci + C, :], in_=stg),
                 waits=[("sV", nv)], inc=("dS", 16))

        final_dS = cnt["dS"]
        final_dD = cnt["dD"]
        emit("scalar", lambda e: e.nop(), waits=[("dS", final_dS)])
        emit("sync", lambda e: e.nop(),
             waits=[("dS", final_dS)] + ([("dD", final_dD)] if final_dD else []))
        emit("gpsimd", lambda e: e.nop(), waits=[("dS", final_dS)])

        with nc.Block() as block:
            def runner(eng_name):
                def go(e):
                    for fn, waits, inc in streams[eng_name]:
                        for s, v in waits:
                            e.wait_ge(sems[s], v)
                        i = fn(e)
                        if inc:
                            i.then_inc(sems[inc[0]], inc[1])
                return go
            block.sync(runner("sync"))
            block.gpsimd(runner("gpsimd"))
            block.vector(runner("vector"))
            block.scalar(runner("scalar"))
            block.tensor(runner("tensor"))

    return nc


def make_inputs_for_core(core, x_full, w):
    b, half = core // 2, core % 2
    x_main = np.ascontiguousarray(x_full[b, half * T:(half + 1) * T])
    x_t = np.ascontiguousarray(x_main.T)
    if half == 1:
        x_pt = np.ascontiguousarray(x_full[b, 0:T].T)
        pmask = 1.0
    else:
        x_pt = np.zeros((DM, T), np.float32)
        pmask = 0.0
    cst = np.zeros((128, 8), np.float32)
    cst[:, 0] = 1.0
    cst[:, 1] = pmask
    cst[:, 2] = 4.0
    cst[:, 3] = LNEPS
    cst[:, 4] = -0.5
    cst[:, 5] = 1.0 / 127.0
    m = {"x": x_main, "xt": x_t, "xpt": x_pt, "cst": cst}
    m.update(w)
    return m


def shared_weights(inputs):
    ident = np.eye(128, dtype=np.float32)
    m32 = (np.triu(np.ones((128, 128), np.float32)) / 32.0).astype(np.float32)
    return {
        "wq": np.ascontiguousarray(inputs["L0_attn_Wq"], np.float32),
        "wk": np.ascontiguousarray(inputs["L0_attn_Wk"], np.float32),
        "wv": np.ascontiguousarray(inputs["L0_attn_Wv"], np.float32),
        "wo": np.ascontiguousarray(inputs["L0_attn_Wo"], np.float32),
        "wg": np.ascontiguousarray(inputs["L0_ffn_w_gate"].T),
        "wu": np.ascontiguousarray(inputs["L0_ffn_w_up"].T),
        "wd": np.ascontiguousarray(inputs["L0_ffn_w_down"].T),
        "ident": ident,
        "m32": m32,
    }


def kernel(L0_attn_hidden_states, L0_attn_Wq, L0_attn_Wk, L0_attn_Wv,
           L0_attn_Wo, L0_ffn_w_gate, L0_ffn_w_up, L0_ffn_w_down):
    inputs = {
        "L0_attn_hidden_states": np.asarray(L0_attn_hidden_states),
        "L0_attn_Wq": np.asarray(L0_attn_Wq), "L0_attn_Wk": np.asarray(L0_attn_Wk),
        "L0_attn_Wv": np.asarray(L0_attn_Wv), "L0_attn_Wo": np.asarray(L0_attn_Wo),
        "L0_ffn_w_gate": np.asarray(L0_ffn_w_gate),
        "L0_ffn_w_up": np.asarray(L0_ffn_w_up),
        "L0_ffn_w_down": np.asarray(L0_ffn_w_down),
    }
    x = inputs["L0_attn_hidden_states"]
    w = shared_weights(inputs)
    nc = build_nc()
    in_maps = [make_inputs_for_core(c, x, w) for c in range(8)]
    res = run_bass_kernel_spmd(nc, in_maps, core_ids=list(range(8)))
    out = np.empty((4, 4096, DM), np.float32)
    for c in range(8):
        b, half = c // 2, c % 2
        out[b, half * T:(half + 1) * T] = res.results[c]["y"]
    return out


# version 06
# speedup vs baseline: 1.0542x; 1.0414x over previous
import contextlib
import numpy as np
import concourse.bass as bass
import concourse.mybir as mybir
from concourse.bass_utils import run_bass_kernel_spmd

F32 = mybir.dt.float32
BF16 = mybir.dt.bfloat16
I32 = mybir.dt.int32
AF = mybir.ActivationFunctionType
ALU = mybir.AluOpType
AX = mybir.AxisListType.X

C = 128
H = 16
DM = 1024
DFF = 3072
T = 2048
DPHI = 273
PACK = 195
EPS = 1e-9
LNEPS = 1e-5
KQSC = 1.0 / 32.0
KRSC = 0.25
FFSC = float(1.0 / (127.0 * 127.0))


def build_nc(nch=T // C, npre=T // C, dumps=(), pus1_sbuf=False, late_div=False,
             attnfm_sep=False, rec_sep=False, phiq3=True):
    nc = bass.Bass("TRN2", target_bir_lowering=False)
    x_h = nc.dram_tensor("x", [T, DM], F32, kind="ExternalInput").ap()
    xt_h = nc.dram_tensor("xt", [DM, T], F32, kind="ExternalInput").ap()
    xpt_h = nc.dram_tensor("xpt", [DM, T], F32, kind="ExternalInput").ap()
    cst_h = nc.dram_tensor("cst", [128, 8], F32, kind="ExternalInput").ap()
    wq_h = nc.dram_tensor("wq", [DM, 256], F32, kind="ExternalInput").ap()
    wk_h = nc.dram_tensor("wk", [DM, 256], F32, kind="ExternalInput").ap()
    wv_h = nc.dram_tensor("wv", [DM, DM], F32, kind="ExternalInput").ap()
    wo_h = nc.dram_tensor("wo", [DM, DM], F32, kind="ExternalInput").ap()
    wg_h = nc.dram_tensor("wg", [DM, DFF], I32, kind="ExternalInput").ap()
    wu_h = nc.dram_tensor("wu", [DM, DFF], I32, kind="ExternalInput").ap()
    wd_h = nc.dram_tensor("wd", [DFF, DM], I32, kind="ExternalInput").ap()
    id_h = nc.dram_tensor("ident", [128, 128], F32, kind="ExternalInput").ap()
    m32_h = nc.dram_tensor("m32", [128, 128], F32, kind="ExternalInput").ap()
    y_h = nc.dram_tensor("y", [T, DM], F32, kind="ExternalOutput").ap()
    dump_h = {}
    for tag, shape in dumps:
        dump_h[tag] = nc.dram_tensor("dbg_" + tag, shape, F32, kind="ExternalOutput").ap()

    streams = {e: [] for e in ("sync", "gpsimd", "vector", "scalar", "tensor")}
    cnt = {s: 0 for s in ("dG", "dS", "dD", "sT", "sV", "sA", "sG")}
    res = {}

    pending_dump = {}

    def emit(eng, fn, waits=(), inc=None):
        wmax = {}
        pd = pending_dump.pop(eng, None)
        waits = list(waits) + ([("dD", pd)] if pd else [])
        for s, v in waits:
            assert v <= cnt[s], f"wait {s}>={v}, emitted {cnt[s]}"
            if v > 0:
                wmax[s] = max(wmax.get(s, 0), v)
        if inc:
            s, n = inc
            cnt[s] += n
        streams[eng].append((fn, tuple(wmax.items()), inc))

    def freew(*names):
        w = []
        for n in names:
            w += res.pop(n, [])
        return w

    def reads(name, sem):
        res.setdefault(name, []).append((sem, cnt[sem]))

    with contextlib.ExitStack() as stk:
        E = stk.enter_context
        stream = E(nc.sbuf_tensor("stream", [128, nch * DM], BF16))
        xfm = E(nc.sbuf_tensor("xfm", [128, 2 * DM], BF16))
        warena = E(nc.sbuf_tensor("warena", [128, 24576], BF16))
        wg_sb = E(nc.sbuf_tensor("wg_sb", [128, 24576], BF16))
        wu_sb = E(nc.sbuf_tensor("wu_sb", [128, 24576], BF16))
        scr = E(nc.sbuf_tensor("scr", [128, 13188], BF16))
        m32_sb = E(nc.sbuf_tensor("m32_sb", [128, 128], BF16))
        id_sb = E(nc.sbuf_tensor("id_sb", [128, 128], BF16))
        cst_sb = E(nc.sbuf_tensor("cst_sb", [128, 8], F32))
        den_sb = E(nc.sbuf_tensor("den_sb", [128, 16], F32))
        ln_sb = E(nc.sbuf_tensor("ln_sb", [128, 80], F32))
        pu_s1t = E(nc.sbuf_tensor("pu_s1t", [128, 256], F32)) if pus1_sbuf else None
        attnfm_t = E(nc.sbuf_tensor("attnfm_t", [128, 1024], BF16)) if attnfm_sep else None
        rec_t = E(nc.sbuf_tensor("rec_t", [128, 16], F32)) if rec_sep else None

        rec_sb = ln_sb[:, 64:80]
        if rec_sep:
            rec_sb = rec_t

        xps = scr[:, 0:2048]
        qk_tok = scr[:, 0:512]
        ksc = scr[:, 512:768]
        qk_pad = scr[:, 768:1792]
        pu_sb = scr[:, 1792:2048]
        pk_tok = scr[:, 12164:12676]
        pk_sc = scr[:, 12676:12932]
        phik = scr[:, 2048:6416]
        phiq = scr[:, 6416:7235]
        NQSL = 3
        phiqfm = scr[:, 7235:8003]
        v_sb = scr[:, 8003:9043]
        state = scr[:, 9043:12163]
        attn = scr[:, 12164:13188]
        attnfm = scr[:, 768:1792]
        if attnfm_sep:
            attnfm = attnfm_t
        h_tok = scr[:, 0:3072]
        hfm = scr[:, 3072:6144]
        silu_sb = scr[:, 6144:7680]
        ostage = scr[:, 7680:11776].bitcast(F32)

        psA = E(nc.psum_tensor("psA", [128, 512], F32))
        psB = E(nc.psum_tensor("psB", [128, 512], F32))
        psC = E(nc.psum_tensor("psC", [128, 512], F32))
        psD = E(nc.psum_tensor("psD", [128, 512], F32))
        psE = E(nc.psum_tensor("psE", [128, 512], F32))
        psF = E(nc.psum_tensor("psF", [128, 512], F32))
        psT = E(nc.psum_tensor("psT", [128, 1024], BF16))
        psU = E(nc.psum_tensor("psU", [128, 1024], BF16))

        sems = {s: E(nc.semaphore(s)) for s in cnt}

        wq_sb = warena[:, 0:2048]
        wk_sb = warena[:, 2048:4096]
        wv_sb = warena[:, 4096:12288]
        wo_sb = warena[:, 12288:20480]
        wd_sb = warena[:, 0:24576]

        def dma(dst, src):
            return lambda e, d=dst, s=src: e.dma_start(out=d, in_=s)

        emit("gpsimd", dma(id_sb[:, :], id_h[:, :]), inc=("dG", 16))

        emit("gpsimd", dma(wq_sb.rearrange("p (d j) -> p d j", d=8),
                           wq_h.rearrange("(d p) j -> p d j", p=128)), inc=("dG", 16))
        emit("gpsimd", dma(wk_sb.rearrange("p (d j) -> p d j", d=8),
                           wk_h.rearrange("(d p) j -> p d j", p=128)), inc=("dG", 16))
        emit("gpsimd", dma(wv_sb.rearrange("p (d j) -> p d j", d=8),
                           wv_h.rearrange("(d p) j -> p d j", p=128)), inc=("dG", 16))
        emit("gpsimd", dma(wo_sb.rearrange("p (d j) -> p d j", d=8),
                           wo_h.rearrange("(d p) j -> p d j", p=128)), inc=("dG", 16))
        W_ATTN = cnt["dG"]
        emit("gpsimd", dma(m32_sb[:, :], m32_h[:, :]), inc=("dG", 16))
        emit("sync", dma(cst_sb[:, :], cst_h[:, :]), inc=("dG", 16))
        W_CONST = cnt["dG"]

        emit("vector", lambda e: e.memset(
            phik.rearrange("p (h d) -> p h d", h=16)[:, :, 0:1], 1.0))
        emit("vector", lambda e: e.memset(
            phiq.rearrange("p (s d) -> p s d", s=3)[:, :, 0:1], 1.0))
        emit("vector", lambda e: e.memset(state[:, :], 0.0), inc=("sV", 1))
        res["state"] = [("sV", cnt["sV"])]

        emit("gpsimd", dma(wg_sb.rearrange("p (d j) -> p d j", d=8),
                           wg_h.rearrange("(d p) j -> p d j", p=128)), inc=("dG", 16))
        emit("gpsimd", dma(wu_sb.rearrange("p (d j) -> p d j", d=8),
                           wu_h.rearrange("(d p) j -> p d j", p=128)), inc=("dG", 16))
        W_GU = cnt["dG"]

        def tr(out, src, tp=None):
            return lambda e, o=out, s=src, t=tp: e.transpose(o, s, id_sb[:, :],
                                                             tile_position=t)

        def mm(out, lhsT, rhs, start, stop, tp=None):
            return lambda e, o=out, l=lhsT, r=rhs, a=start, b=stop, t=tp: e.matmul(
                o, l, r, start=a, stop=b, tile_position=t)

        def cp(eng_dst, src):
            return lambda e, d=eng_dst, s=src: e.tensor_copy(d, s)

        def out_slot(h):
            if h < 7:
                return psA, "psA", 65 * h
            if h < 14:
                return psB, "psB", 65 * (h - 7)
            return psC, "psC", 130 + 65 * (h - 14)

        addsnap = {h: ("sV", 0) for h in range(H)}

        def dump(tag, src_bf16, n_f32_cols=None):
            if tag in dump_h:
                w = [(s, cnt[s]) for s in ("sT", "sV", "sA", "sG")]
                emit("gpsimd", dma(dump_h[tag][:, :], src_bf16), waits=w, inc=("dD", 16))
                for eng in streams:
                    pending_dump[eng] = cnt["dD"]

        def scan_chunk(ci, xslot, vcol, kv_only, xld, qk_tok=qk_tok, ksc=ksc,
                       dump_cb=None, after_proj=None, after_pads=None, x_ld=0):
            xf = xfm[:, DM * xslot:DM * xslot + DM]
            w = [("dG", xld), ("dG", W_CONST), ("dG", W_ATTN)] + freew(
                "psA", "psB", "psC")
            for dc in range(8):
                lhsT = xf[:, 128 * dc:128 * dc + 128]
                if not kv_only:
                    emit("tensor", mm(psA[:, 0:256], lhsT,
                                      wq_sb[:, 256 * dc:256 * dc + 256], dc == 0, dc == 7),
                         waits=w if dc == 0 else ())
                    w = ()
                emit("tensor", mm(psA[:, 256:512], lhsT,
                                  wk_sb[:, 256 * dc:256 * dc + 256],
                                  (dc == 0) and kv_only, dc == 7),
                     waits=w if dc == 0 else ())
                w = ()
                emit("tensor", mm(psB[:, 0:512], lhsT,
                                  wv_sb[:, 1024 * dc:1024 * dc + 512], dc == 0, dc == 7))
                emit("tensor", mm(psC[:, 0:512], lhsT,
                                  wv_sb[:, 1024 * dc + 512:1024 * dc + 1024], dc == 0, dc == 7),
                     inc=("sT", 1) if dc == 7 else None)
            tproj = cnt["sT"]

            lo = 256 if kv_only else 0
            emit("vector", cp(qk_tok[:, lo:512], psA[:, lo:512]),
                 waits=[("sT", tproj)] + freew("qk"))
            emit("vector", lambda e: e.tensor_scalar(
                ksc[:, 0:256], qk_tok[:, 256:512], KQSC, None, ALU.mult),
                 waits=freew("qksc"))
            emit("vector", lambda e: e.tensor_scalar(
                phik.rearrange("p (h d) -> p h d", h=16)[:, :, 1:17],
                qk_tok[:, 256:512].rearrange("p (h f) -> p h f", h=16),
                KRSC, None, ALU.mult), waits=freew("phikraw"), inc=("sV", 1))
            kev = cnt["sV"]
            emit("vector", cp(v_sb.rearrange("p (h v) -> p h v", h=16)[:, 0:8, 0:64],
                              psB[:, 0:512].rearrange("p (h v) -> p h v", h=8)),
                 waits=freew("v"))
            emit("vector", cp(v_sb.rearrange("p (h v) -> p h v", h=16)[:, 8:16, 0:64],
                              psC[:, 0:512].rearrange("p (h v) -> p h v", h=8)))
            emit("vector", cp(v_sb.rearrange("p (h v) -> p h v", h=16)[:, :, 64:65],
                              cst_sb[:, vcol:vcol + 1].unsqueeze(1).broadcast_to([128, 16, 1])),
                 inc=("sV", 1))
            projev = cnt["sV"]
            reads("psA", "sV")
            reads("psB", "sV")
            reads("psC", "sV")

            for h in range(H):
                w = [("sV", kev)] + freew(f"phik{h}")
                def kq(e, h=h, qk_tok=qk_tok, ksc=ksc):
                    o = phik[:, DPHI * h + 17:DPHI * h + 273].rearrange(
                        "p (i j) -> p i j", i=16)
                    i0 = ksc[:, 16 * h:16 * h + 16].unsqueeze(2).broadcast_to([128, 16, 16])
                    i1 = qk_tok[:, 256 + 16 * h:256 + 16 * h + 16].unsqueeze(1).broadcast_to([128, 16, 16])
                    return e.tensor_tensor(o, i0, i1, ALU.mult)
                eng = "vector" if (kv_only and h % 3 == 2) else "gpsimd"
                sem = "sV" if eng == "vector" else "sG"
                emit(eng, kq, waits=w, inc=(sem, 1))
                res[f"kfm{h}"] = [(sem, cnt[sem])]

            if kv_only:
                for h in range(H):
                    ps, bank = (psE, "psE") if h % 2 == 0 else (psF, "psF")
                    c0 = PACK * h
                    w = res[f"kfm{h}"] + freew(bank)
                    if h == 0:
                        w.append(("sV", projev))
                    emit("tensor", mm(ps[:, 0:65], phik[:, DPHI * h:DPHI * h + 128],
                                      v_sb[:, 65 * h:65 * h + 65], True, True), waits=w)
                    emit("tensor", mm(ps[:, 65:130], phik[:, DPHI * h + 128:DPHI * h + 256],
                                      v_sb[:, 65 * h:65 * h + 65], True, True))
                    emit("tensor", mm(ps[0:17, 130:195], phik[:, DPHI * h + 256:DPHI * h + 273],
                                      v_sb[:, 65 * h:65 * h + 65], True, True),
                         inc=("sT", 1))
                    stv = cnt["sT"]
                    res[f"phik{h}"] = [("sT", stv)]
                    emit("vector", lambda e, ps=ps, c0=c0: e.tensor_tensor(
                        state[:, c0:c0 + PACK], ps[:, 0:PACK],
                        state[:, c0:c0 + PACK], ALU.add),
                         waits=[("sT", stv)] + freew("state") + [addsnap[h]],
                         inc=("sV", 1))
                    addsnap[h] = ("sV", cnt["sV"])
                    reads(bank, "sV")
                res["v"] = [("sT", cnt["sT"])]
                res["qk"] = [("sG", cnt["sG"]), ("sT", cnt["sT"])]
                res["qksc"] = [("sG", cnt["sG"])]
                res["phikraw"] = [("sG", cnt["sG"]), ("sT", cnt["sT"])]
                return

            if after_proj:
                after_proj()
            w = [("sV", projev)] + freew("psT")
            for h in range(H):
                j, g = h % 4, h // 4
                emit("tensor", tr(psT[32 * j:32 * j + 16, 128 * g:128 * g + 128],
                                  qk_tok[:, 16 * h:16 * h + 16], (0, 32 * j)),
                     waits=w if h == 0 else ())
                w = ()
            for h in range(H):
                j, g = h % 4, h // 4
                emit("tensor", tr(psT[32 * j:32 * j + 16, 512 + 128 * g:512 + 128 * g + 128],
                                  qk_tok[:, 256 + 16 * h:256 + 16 * h + 16], (0, 32 * j)),
                     inc=("sT", 1) if h == H - 1 else None)
            tpads = cnt["sT"]
            emit("vector", cp(qk_pad[:, :], psT[:, :]),
                 waits=[("sT", tpads)] + freew("attnfm"), inc=("sV", 1))
            padsev = cnt["sV"]
            reads("psT", "sV")
            if dump_cb:
                dump_cb("pads0", qk_pad[:, :])
            if after_pads:
                after_pads()

            for h in range(H):
                j, g = h % 4, h // 4
                osl, obank, ocol = out_slot(h)
                slot2 = h % 2
                slot4 = h % NQSL
                ubank, uname = (psC, "psC") if slot2 == 0 else (psD, "psD")
                phb, phbname = (psT, "psT") if slot2 == 0 else (psU, "psU")
                phislot = phiq[:, DPHI * slot4:DPHI * slot4 + DPHI]
                phifm = phiqfm[:, 384 * slot2:384 * slot2 + 384]
                pus1 = pu_s1t[:, 128 * slot2:128 * slot2 + 128] if pus1_sbuf else ubank[:, 384:512]
                pusl = pu_sb[:, 128 * slot2:128 * slot2 + 128]

                emit("vector", lambda e, h=h, sl=phislot: e.tensor_copy(
                    sl[:, 1:17], qk_tok[:, 16 * h:16 * h + 16]),
                     waits=freew(f"phiq{slot4}"))
                def qq(e, h=h, sl=phislot):
                    o = sl[:, 17:273].rearrange("p (i j) -> p i j", i=16)
                    i0 = qk_tok[:, 16 * h:16 * h + 16].unsqueeze(2).broadcast_to([128, 16, 16])
                    i1 = qk_tok[:, 16 * h:16 * h + 16].unsqueeze(1).broadcast_to([128, 16, 16])
                    return e.tensor_tensor(o, i0, i1, ALU.mult)
                emit("vector", qq, inc=("sV", 1))
                qqv = cnt["sV"]

                w = [("sV", padsev)] + freew(uname)
                emit("tensor", mm(ubank[:, 0:128],
                                  qk_pad[32 * j:32 * j + 16, 512 + 128 * g:512 + 128 * g + 128],
                                  qk_pad[32 * j:32 * j + 16, 128 * g:128 * g + 128],
                                  True, True, (32 * j, 0)), waits=w, inc=("sT", 1))
                tu = cnt["sT"]

                emit("scalar", lambda e, ub=ubank, s1=pus1: e.activation(
                    s1, ub[:, 0:128], AF.Square, bias=cst_sb[:, 2:3]),
                     waits=[("sT", tu)] + freew(f"pus1_{slot2}"), inc=("sA", 1))
                reads(uname, "sA")
                polyA = cnt["sA"]

                emit("vector", lambda e, s1=pus1, pu=pusl: e.scalar_tensor_tensor(
                    pu, s1, 16.0, m32_sb[:, :], ALU.add, ALU.mult),
                     waits=[("sA", polyA)] + freew(f"pu_{slot2}"), inc=("sV", 1))
                res[f"pus1_{slot2}"] = [("sV", cnt["sV"])]
                reads(uname, "sV")
                puV = cnt["sV"]
                if dump_cb and h == 0:
                    dump_cb("pu0", pusl)
                if dump_cb and h == 1:
                    emit("vector", cp(attn.bitcast(F32)[:, 0:128], pus1),
                         waits=[("sV", puV)], inc=("sV", 1))
                    dump_cb("s1h1", attn.bitcast(F32)[:, 0:128])
                    dump_cb("pu1", pusl)

                w = [("sV", qqv)] + freew(phbname)
                emit("tensor", tr(phb[:, 0:128], phislot[:, 0:128]), waits=w)
                emit("tensor", tr(phb[:, 128:256], phislot[:, 128:256]))
                emit("tensor", tr(phb[0:17, 256:384], phislot[:, 256:273]), inc=("sT", 1))
                res[f"phiq{slot4}"] = [("sT", cnt["sT"])]
                tphi = cnt["sT"]

                emit("scalar", lambda e, pb=phb, pf=phifm: e.copy(pf, pb[:, 0:384]),
                     waits=[("sT", tphi)] + freew(f"phifm{slot2}"), inc=("sA", 1))
                reads(phbname, "sA")
                phievA = cnt["sA"]
                if dump_cb and h == 0:
                    dump_cb("phiq0", phislot)
                    dump_cb("phifm0", phifm)

                emit("tensor", mm(osl[:, ocol:ocol + 65], pusl,
                                  v_sb[:, 65 * h:65 * h + 65], h in (0, 7, 14), False),
                     waits=[("sV", puV)], inc=("sT", 1))
                res[f"pu_{slot2}"] = [("sT", cnt["sT"])]

                c0 = PACK * h
                w = [("sA", phievA), addsnap[h]]
                emit("tensor", mm(osl[:, ocol:ocol + 65], phifm[:, 0:128],
                                  state[:, c0:c0 + 65], False, False), waits=w)
                emit("tensor", mm(osl[:, ocol:ocol + 65], phifm[:, 128:256],
                                  state[:, c0 + 65:c0 + 130], False, False))
                emit("tensor", mm(osl[:, ocol:ocol + 65], phifm[0:17, 256:384],
                                  state[0:17, c0 + 130:c0 + 195], False, True),
                     inc=("sT", 1))
                res[f"phifm{slot2}"] = [("sT", cnt["sT"])]

                ps, bank = (psE, "psE") if slot2 == 0 else (psF, "psF")
                w = res[f"kfm{h}"] + freew(bank)
                emit("tensor", mm(ps[:, 0:65], phik[:, DPHI * h:DPHI * h + 128],
                                  v_sb[:, 65 * h:65 * h + 65], True, True), waits=w)
                emit("tensor", mm(ps[:, 65:130], phik[:, DPHI * h + 128:DPHI * h + 256],
                                  v_sb[:, 65 * h:65 * h + 65], True, True))
                emit("tensor", mm(ps[0:17, 130:195], phik[:, DPHI * h + 256:DPHI * h + 273],
                                  v_sb[:, 65 * h:65 * h + 65], True, True),
                     inc=("sT", 1))
                stv = cnt["sT"]
                res[f"phik{h}"] = [("sT", stv)]
                emit("vector", lambda e, ps=ps, c0=c0: e.tensor_tensor(
                    state[:, c0:c0 + PACK], ps[:, 0:PACK],
                    state[:, c0:c0 + PACK], ALU.add),
                     waits=[("sT", stv)] + freew("state") + [addsnap[h]], inc=("sV", 1))
                addsnap[h] = ("sV", cnt["sV"])
                reads(bank, "sV")

                div_heads = (15,) if late_div else (6, 13, 15)
                if h in div_heads:
                  for h, glo, gbank, gname in ([(6, 0, psA, "psA"), (13, 7, psB, "psB"),
                                                (15, 14, psC, "psC")] if late_div else
                                               [(h, {6: 0, 13: 7, 15: 14}[h],
                                                 {6: psA, 13: psB, 15: psC}[h],
                                                 {6: "psA", 13: "psB", 15: "psC"}[h])]):
                      nh = h + 1 - glo
                      w = [("sT", stv)]
                      if h == 6:
                          w += freew("attn")
                      if gname == "psC":
                          dsrc = psC[:, 130:260].rearrange("p (h v) -> p h v", h=2)[:, :, 64:65].squeeze(2)
                      else:
                          dsrc = gbank[:, 0:455].rearrange("p (h v) -> p h v", h=7)[:, :, 64:65].squeeze(2)
                      emit("vector", cp(den_sb[:, glo:glo + nh], dsrc),
                           waits=w, inc=("sV", 1))
                      genv = cnt["sV"]
                      emit("vector", lambda e, glo=glo, nh=nh: e.tensor_scalar(
                          den_sb[:, glo:glo + nh], den_sb[:, glo:glo + nh],
                          EPS, None, ALU.add), waits=[("sV", genv)], inc=("sV", 1))
                      epsv = cnt["sV"]
                      emit("vector", lambda e, glo=glo, nh=nh: e.reciprocal(
                          rec_sb[:, glo:glo + nh], den_sb[:, glo:glo + nh]),
                           waits=[("sV", epsv)], inc=("sV", 1))
                      recv = cnt["sV"]
                      for h2 in range(glo, h + 1):
                          osl2, _, ocol2 = out_slot(h2)
                          emit("vector", lambda e, h2=h2, osl2=osl2, ocol2=ocol2: e.tensor_scalar(
                              attn[:, 64 * h2:64 * h2 + 64], osl2[:, ocol2:ocol2 + 64],
                              rec_sb[:, h2:h2 + 1], None, ALU.mult),
                               waits=[("sV", recv)] if h2 == glo else (),
                               inc=("sV", 1) if h2 == h else None)
                      reads(gname, "sV")

            divV = cnt["sV"]
            theads = cnt["sT"]
            res["v"] = [("sT", theads)]
            res["qk"] = [("sT", theads), ("sG", cnt["sG"]), ("sV", cnt["sV"])]
            res["qksc"] = [("sG", cnt["sG"]), ("sV", cnt["sV"])]
            res["phikraw"] = [("sG", cnt["sG"]), ("sT", theads)]

            if dump_cb:
                emit("vector", cp(attnfm.bitcast(F32)[:, 0:455], psA[:, 0:455]),
                     waits=[("sT", theads)] + freew("attnfm"), inc=("sV", 1))
                res["attnfm"] = [("sV", cnt["sV"])]
                dump_cb("o0", attnfm.bitcast(F32)[:, 0:455])
                dump_cb("den0", den_sb[:, :])
                dump_cb("rec0", rec_sb[:, :])

            def tail_a(divV=divV):
                w = [("sV", divV)] + freew("psT")
                for dc in range(8):
                    emit("tensor", tr(psT[:, 128 * dc:128 * dc + 128],
                                      attn[:, 128 * dc:128 * dc + 128]),
                         waits=w if dc == 0 else (), inc=("sT", 1) if dc == 7 else None)
                tat = cnt["sT"]
                reads("attn", "sT")
                emit("scalar", lambda e: e.copy(attnfm[:, :], psT[:, :]),
                     waits=[("sT", tat)] + freew("attnfm"), inc=("sA", 1))
                reads("psT", "sA")

            def tail_b(ci=ci):
                atev = cnt["sA"]
                w = [("sA", atev)] + freew("psD", "psF")
                for dc in range(8):
                    emit("tensor", mm(psD[:, 0:512], attnfm[:, 128 * dc:128 * dc + 128],
                                      wo_sb[:, 1024 * dc:1024 * dc + 512], dc == 0, dc == 7),
                         waits=w if dc == 0 else ())
                    emit("tensor", mm(psF[:, 0:512], attnfm[:, 128 * dc:128 * dc + 128],
                                      wo_sb[:, 1024 * dc + 512:1024 * dc + 1024], dc == 0, dc == 7),
                         inc=("sT", 1) if dc == 7 else None)
                two = cnt["sT"]
                res["attnfm"] = [("sT", two)]
                emit("vector", lambda e: e.tensor_tensor(
                    stream[:, DM * ci:DM * ci + 512], psD[:, 0:512],
                    stream[:, DM * ci:DM * ci + 512], ALU.add),
                     waits=[("sT", two), ("dG", x_ld)])
                emit("vector", lambda e: e.tensor_tensor(
                    stream[:, DM * ci + 512:DM * ci + 1024], psF[:, 0:512],
                    stream[:, DM * ci + 512:DM * ci + 1024], ALU.add), inc=("sV", 1))
                rv = cnt["sV"]
                reads("psD", "sV")
                reads("psF", "sV")
                res["wo_resid"] = [("sV", rv)]
                return rv
            return tail_a, tail_b

        def load_xfm(src_t, ci, slot):
            emit("gpsimd", dma(
                xfm[:, DM * slot:DM * slot + DM].rearrange("p (dc t) -> p dc t", dc=8),
                src_t.rearrange("(dc p) t -> p dc t", p=128)[:, :, C * ci:C * ci + C]),
                 waits=freew(f"xfm{slot}"), inc=("dG", 16))
            return cnt["dG"]

        for ci in range(npre):
            xld = load_xfm(xpt_h, ci, ci % 2)
            scan_chunk(ci, ci % 2, 1, True, xld, qk_tok=pk_tok, ksc=pk_sc)
            res[f"xfm{ci % 2}"] = [("sT", cnt["sT"])]

        if dumps:
            dump("state_pre", state[:, :], PACK * H)

        def ln_stats(ci, resid_sv):
            w = [("sV", resid_sv)]
            emit("scalar", lambda e, ci=ci: e.activation(
                attn[:, :], stream[:, DM * ci:DM * ci + DM],
                AF.Square, accum_out=ln_sb[:, 16 + ci:17 + ci]),
                 waits=w, inc=("sA", 1))
            reads("attn", "sA")
            emit("vector", lambda e, ci=ci: e.reduce_sum(
                ln_sb[:, ci:ci + 1], stream[:, DM * ci:DM * ci + DM], AX),
                 waits=w, inc=("sV", 1))


        pend = [None, None]
        no_pipe = False
        for ci in range(nch):
            emit("gpsimd", dma(stream[:, DM * ci:DM * ci + DM], x_h[C * ci:C * ci + C, :]),
                 inc=("dG", 16))
            x_ld = cnt["dG"]
            xld = load_xfm(xt_h, ci, ci % 2)
            if pend[0] is not None:
                pend[0]()
                pend[0] = None
            def tailb_only(ci=ci):
                if pend[1] is not None:
                    rv = pend[1]()
                    ln_stats(ci - 1, rv)
                    if ci - 1 == 1:
                        dump("stream1", stream[:, DM:2 * DM], DM)
                    pend[1] = None
            tails = scan_chunk(ci, ci % 2, 0, False, xld,
                               dump_cb=(dump if ci == 0 else None),
                               after_proj=tailb_only,
                               after_pads=None, x_ld=x_ld)
            pend = list(tails)
            res[f"xfm{ci % 2}"] = [("sT", cnt["sT"])]
            if ci == 0:
                dump("qk0", qk_tok[:, :], 512)
                dump("v0", v_sb[:, :], 1040)
                dump("phik0", phik[:, :], DPHI * H)
                dump("attn0", attn[:, :], DM)
                dump("state0", state[:, :], PACK * H)
        pend[0]()
        rv = pend[1]()
        ln_stats(nch - 1, rv)

        def ln_finish():
            sAs, sVs = cnt["sA"], cnt["sV"]
            emit("vector", lambda e: e.tensor_scalar(
                ln_sb[:, 32:48], ln_sb[:, 0:16], 1.0 / DM, None, ALU.mult),
                 waits=[("sA", sAs), ("sV", sVs)], inc=("sV", 1))
            meanv = cnt["sV"]
            emit("vector", lambda e: e.tensor_tensor(
                den_sb[:, 0:16], ln_sb[:, 32:48], ln_sb[:, 32:48], ALU.mult),
                 waits=[("sV", meanv)], inc=("sV", 1))
            msqv = cnt["sV"]
            emit("vector", lambda e: e.scalar_tensor_tensor(
                ln_sb[:, 64:80], ln_sb[:, 16:32], 1.0 / DM, den_sb[:, 0:16],
                ALU.mult, ALU.subtract), waits=[("sV", msqv)], inc=("sV", 1))
            vv = cnt["sV"]
            emit("scalar", lambda e: e.activation(rec_sb[:, 0:16], ln_sb[:, 64:80],
                                                  AF.Ln, bias=cst_sb[:, 3:4]), waits=[("sV", vv)])
            emit("scalar", lambda e: e.activation(ln_sb[:, 48:64], rec_sb[:, 0:16],
                                                  AF.Exp, scale=cst_sb[:, 4:5]), inc=("sA", 1))
            sa = cnt["sA"]
            emit("vector", lambda e: e.scalar_tensor_tensor(
                ln_sb[:, 64:80], ln_sb[:, 32:48], -1.0, ln_sb[:, 48:64],
                ALU.mult, ALU.mult), waits=[("sA", sa)], inc=("sV", 1))
            return cnt["sV"]

        vneg = ln_finish()
        if dumps:
            dump("lnsb", ln_sb[:, :], 96)
        for ci in range(nch):
            emit("vector", lambda e, ci=ci: e.tensor_scalar(
                stream[:, DM * ci:DM * ci + DM], stream[:, DM * ci:DM * ci + DM],
                ln_sb[:, 48 + ci:49 + ci], ln_sb[:, 64 + ci:65 + ci],
                ALU.mult, ALU.add),
                 waits=[("sV", vneg)] if ci == 0 else (), inc=("sV", 1))
            res[f"norm{ci}"] = [("sV", cnt["sV"])]
        if dumps:
            dump("y1", stream[:, 0:DM], DM)

        emit("gpsimd", dma(wd_sb.rearrange("p (hc j) -> p hc j", hc=24),
                           wd_h.rearrange("(hc p) j -> p hc j", p=128)),
             waits=[("sT", cnt["sT"])], inc=("dG", 16))
        W_WD = cnt["dG"]

        for ci in range(nch):
            w = res.pop(f"norm{ci}") + freew("psT")
            for dc in range(8):
                emit("tensor", tr(psT[:, 128 * dc:128 * dc + 128],
                                  stream[:, DM * ci + 128 * dc:DM * ci + 128 * dc + 128]),
                     waits=w if dc == 0 else (), inc=("sT", 1) if dc == 7 else None)
                w = ()
            ty = cnt["sT"]
            emit("vector", cp(xfm[:, 0:DM], psT[:, :]),
                 waits=[("sT", ty)] + freew("xfm0") + freew("xfm1"), inc=("sV", 1))
            reads("psT", "sV")
            y1ev = cnt["sV"]

            hm = []
            for half in range(2):
                w = [("sV", y1ev), ("dG", W_GU)] + freew(
                    "psA", "psB", "psC", "psD", "psE", "psF")
                for dc in range(8):
                    lhsT = xfm[:, 128 * dc:128 * dc + 128]
                    for k, ps in enumerate((psA, psB, psC)):
                        emit("tensor", mm(ps[:, 0:512], lhsT,
                                          wg_sb[:, 3072 * dc + 1536 * half + 512 * k:
                                                3072 * dc + 1536 * half + 512 * k + 512],
                                          dc == 0, dc == 7),
                             waits=w if (dc == 0 and k == 0) else ())
                        w = ()
                    for k, ps in enumerate((psD, psE, psF)):
                        emit("tensor", mm(ps[:, 0:512], lhsT,
                                          wu_sb[:, 3072 * dc + 1536 * half + 512 * k:
                                                3072 * dc + 1536 * half + 512 * k + 512],
                                          dc == 0, dc == 7),
                             inc=("sT", 1) if (dc == 7 and k == 2) else None)
                tgu = cnt["sT"]
                for k, (psg, psu2) in enumerate(((psA, psD), (psB, psE), (psC, psF))):
                    emit("scalar", lambda e, k=k, psg=psg: e.activation(
                        silu_sb[:, 512 * k:512 * k + 512], psg[:, 0:512], AF.Silu,
                        scale=cst_sb[:, 5:6]),
                         waits=[("sT", tgu)] + (freew("silu") if k == 0 else []),
                         inc=("sA", 1))
                    sk = cnt["sA"]
                    gname = ("psA", "psB", "psC")[k]
                    reads(gname, "sA")
                    emit("vector", lambda e, k=k, half=half, psu2=psu2: e.tensor_tensor(
                        h_tok[:, 1536 * half + 512 * k:1536 * half + 512 * k + 512],
                        silu_sb[:, 512 * k:512 * k + 512], psu2[:, 0:512], ALU.mult),
                         waits=[("sA", sk)] + (freew("htok") if (half == 0 and k == 0) else []),
                         inc=("sV", 1))
                    uname = ("psD", "psE", "psF")[k]
                    reads(uname, "sV")
                hm.append(cnt["sV"])
                res["silu"] = [("sV", cnt["sV"])]

            hevs = []
            for grp in range(3):
                phb, phn = (psT, "psT") if grp % 2 == 0 else (psU, "psU")
                w = [("sV", hm[1])] + freew(phn)
                for i in range(8):
                    hc = 8 * grp + i
                    emit("tensor", tr(phb[:, 128 * i:128 * i + 128],
                                      h_tok[:, 128 * hc:128 * hc + 128]),
                         waits=w if i == 0 else (), inc=("sT", 1) if i == 7 else None)
                    w = ()
                tg = cnt["sT"]
                emit("scalar", lambda e, phb=phb, grp=grp: e.copy(
                    hfm[:, 1024 * grp:1024 * grp + 1024], phb[:, :]),
                     waits=[("sT", tg)] + (freew("hfm") if grp == 0 else []),
                     inc=("sA", 1))
                reads(phn, "sA")
                hevs.append(cnt["sA"])

            w = [("sA", hevs[2]), ("dG", W_WD)] + freew("psA", "psB")
            for hc in range(24):
                emit("tensor", mm(psA[:, 0:512], hfm[:, 128 * hc:128 * hc + 128],
                                  wd_sb[:, 1024 * hc:1024 * hc + 512], hc == 0, hc == 23),
                     waits=w if hc == 0 else ())
                w = ()
                emit("tensor", mm(psB[:, 0:512], hfm[:, 128 * hc:128 * hc + 128],
                                  wd_sb[:, 1024 * hc + 512:1024 * hc + 1024], hc == 0, hc == 23),
                     inc=("sT", 1) if hc == 23 else None)
            td = cnt["sT"]
            res["hfm"] = [("sT", td)]
            res["htok"] = [("sT", td)]
            emit("vector", lambda e, ci=ci: e.scalar_tensor_tensor(
                stream[:, DM * ci:DM * ci + 512], psA[:, 0:512], FFSC,
                stream[:, DM * ci:DM * ci + 512], ALU.mult, ALU.add),
                 waits=[("sT", td)])
            emit("vector", lambda e, ci=ci: e.scalar_tensor_tensor(
                stream[:, DM * ci + 512:DM * ci + 1024], psB[:, 0:512], FFSC,
                stream[:, DM * ci + 512:DM * ci + 1024], ALU.mult, ALU.add),
                 inc=("sV", 1))
            zr = cnt["sV"]
            res["psA"] = [("sV", zr)]
            res["psB"] = [("sV", zr)]
            res[f"z{ci}"] = [("sV", zr)]
            ln_stats(ci, zr)

        if dumps:
            dump("z0", stream[:, 0:DM], DM)

        vneg2 = ln_finish()
        for ci in range(nch):
            stg = ostage[:, 1024 * (ci % 2):1024 * (ci % 2) + 1024]
            w = [("sV", vneg2)] if ci == 0 else []
            if ci >= 2:
                w.append(("dS", 16 * (ci - 1)))
            emit("vector", lambda e, ci=ci, stg=stg: e.tensor_scalar(
                stg, stream[:, DM * ci:DM * ci + DM],
                ln_sb[:, 48 + ci:49 + ci], ln_sb[:, 64 + ci:65 + ci],
                ALU.mult, ALU.add),
                 waits=w, inc=("sV", 1))
            nv = cnt["sV"]
            emit("scalar", lambda e, ci=ci, stg=stg: e.dma_start(
                out=y_h[C * ci:C * ci + C, :], in_=stg),
                 waits=[("sV", nv)], inc=("dS", 16))

        final_dS = cnt["dS"]
        final_dD = cnt["dD"]
        emit("scalar", lambda e: e.nop(), waits=[("dS", final_dS)])
        emit("sync", lambda e: e.nop(),
             waits=[("dS", final_dS)] + ([("dD", final_dD)] if final_dD else []))
        emit("gpsimd", lambda e: e.nop(), waits=[("dS", final_dS)])

        with nc.Block() as block:
            def runner(eng_name):
                def go(e):
                    for fn, waits, inc in streams[eng_name]:
                        for s, v in waits:
                            e.wait_ge(sems[s], v)
                        i = fn(e)
                        if inc:
                            i.then_inc(sems[inc[0]], inc[1])
                return go
            block.sync(runner("sync"))
            block.gpsimd(runner("gpsimd"))
            block.vector(runner("vector"))
            block.scalar(runner("scalar"))
            block.tensor(runner("tensor"))

    return nc


def make_inputs_for_core(core, x_full, w):
    b, half = core // 2, core % 2
    x_main = np.ascontiguousarray(x_full[b, half * T:(half + 1) * T])
    x_t = np.ascontiguousarray(x_main.T)
    if half == 1:
        x_pt = np.ascontiguousarray(x_full[b, 0:T].T)
        pmask = 1.0
    else:
        x_pt = np.zeros((DM, T), np.float32)
        pmask = 0.0
    cst = np.zeros((128, 8), np.float32)
    cst[:, 0] = 1.0
    cst[:, 1] = pmask
    cst[:, 2] = 4.0
    cst[:, 3] = LNEPS
    cst[:, 4] = -0.5
    cst[:, 5] = 1.0 / 127.0
    m = {"x": x_main, "xt": x_t, "xpt": x_pt, "cst": cst}
    m.update(w)
    return m


def shared_weights(inputs):
    ident = np.eye(128, dtype=np.float32)
    m32 = (np.triu(np.ones((128, 128), np.float32)) / 32.0).astype(np.float32)
    return {
        "wq": np.ascontiguousarray(inputs["L0_attn_Wq"], np.float32),
        "wk": np.ascontiguousarray(inputs["L0_attn_Wk"], np.float32),
        "wv": np.ascontiguousarray(inputs["L0_attn_Wv"], np.float32),
        "wo": np.ascontiguousarray(inputs["L0_attn_Wo"], np.float32),
        "wg": np.ascontiguousarray(inputs["L0_ffn_w_gate"].T),
        "wu": np.ascontiguousarray(inputs["L0_ffn_w_up"].T),
        "wd": np.ascontiguousarray(inputs["L0_ffn_w_down"].T),
        "ident": ident,
        "m32": m32,
    }


def kernel(L0_attn_hidden_states, L0_attn_Wq, L0_attn_Wk, L0_attn_Wv,
           L0_attn_Wo, L0_ffn_w_gate, L0_ffn_w_up, L0_ffn_w_down):
    inputs = {
        "L0_attn_hidden_states": np.asarray(L0_attn_hidden_states),
        "L0_attn_Wq": np.asarray(L0_attn_Wq), "L0_attn_Wk": np.asarray(L0_attn_Wk),
        "L0_attn_Wv": np.asarray(L0_attn_Wv), "L0_attn_Wo": np.asarray(L0_attn_Wo),
        "L0_ffn_w_gate": np.asarray(L0_ffn_w_gate),
        "L0_ffn_w_up": np.asarray(L0_ffn_w_up),
        "L0_ffn_w_down": np.asarray(L0_ffn_w_down),
    }
    x = inputs["L0_attn_hidden_states"]
    w = shared_weights(inputs)
    nc = build_nc()
    in_maps = [make_inputs_for_core(c, x, w) for c in range(8)]
    res = run_bass_kernel_spmd(nc, in_maps, core_ids=list(range(8)))
    out = np.empty((4, 4096, DM), np.float32)
    for c in range(8):
        b, half = c // 2, c % 2
        out[b, half * T:(half + 1) * T] = res.results[c]["y"]
    return out


# version 07
# speedup vs baseline: 1.0611x; 1.0065x over previous
import contextlib
import numpy as np
import concourse.bass as bass
import concourse.mybir as mybir
from concourse.bass_utils import run_bass_kernel_spmd

F32 = mybir.dt.float32
BF16 = mybir.dt.bfloat16
I32 = mybir.dt.int32
AF = mybir.ActivationFunctionType
ALU = mybir.AluOpType
AX = mybir.AxisListType.X

C = 128
H = 16
DM = 1024
DFF = 3072
T = 2048
DPHI = 273
PACK = 195
EPS = 1e-9
LNEPS = 1e-5
KQSC = 1.0 / 32.0
KRSC = 0.25
FFSC = float(1.0 / (127.0 * 127.0))


def build_nc(nch=T // C, npre=T // C, dumps=(), pus1_sbuf=False, late_div=False,
             attnfm_sep=False, rec_sep=False, phiq3=True):
    nc = bass.Bass("TRN2", target_bir_lowering=False)
    x_h = nc.dram_tensor("x", [T, DM], F32, kind="ExternalInput").ap()
    xt_h = nc.dram_tensor("xt", [DM, T], F32, kind="ExternalInput").ap()
    xpt_h = nc.dram_tensor("xpt", [DM, T], F32, kind="ExternalInput").ap()
    cst_h = nc.dram_tensor("cst", [128, 8], F32, kind="ExternalInput").ap()
    wq_h = nc.dram_tensor("wq", [DM, 256], F32, kind="ExternalInput").ap()
    wk_h = nc.dram_tensor("wk", [DM, 256], F32, kind="ExternalInput").ap()
    wv_h = nc.dram_tensor("wv", [DM, DM], F32, kind="ExternalInput").ap()
    wo_h = nc.dram_tensor("wo", [DM, DM], F32, kind="ExternalInput").ap()
    wg_h = nc.dram_tensor("wg", [DM, DFF], I32, kind="ExternalInput").ap()
    wu_h = nc.dram_tensor("wu", [DM, DFF], I32, kind="ExternalInput").ap()
    wd_h = nc.dram_tensor("wd", [DFF, DM], I32, kind="ExternalInput").ap()
    id_h = nc.dram_tensor("ident", [128, 128], F32, kind="ExternalInput").ap()
    m32_h = nc.dram_tensor("m32", [128, 128], F32, kind="ExternalInput").ap()
    y_h = nc.dram_tensor("y", [T, DM], F32, kind="ExternalOutput").ap()
    dump_h = {}
    for tag, shape in dumps:
        dump_h[tag] = nc.dram_tensor("dbg_" + tag, shape, F32, kind="ExternalOutput").ap()

    streams = {e: [] for e in ("sync", "gpsimd", "vector", "scalar", "tensor")}
    cnt = {s: 0 for s in ("dG", "dS", "dD", "sT", "sV", "sA", "sG")}
    res = {}

    pending_dump = {}

    def emit(eng, fn, waits=(), inc=None):
        wmax = {}
        pd = pending_dump.pop(eng, None)
        waits = list(waits) + ([("dD", pd)] if pd else [])
        for s, v in waits:
            assert v <= cnt[s], f"wait {s}>={v}, emitted {cnt[s]}"
            if v > 0:
                wmax[s] = max(wmax.get(s, 0), v)
        if inc:
            s, n = inc
            cnt[s] += n
        streams[eng].append((fn, tuple(wmax.items()), inc))

    def freew(*names):
        w = []
        for n in names:
            w += res.pop(n, [])
        return w

    def reads(name, sem):
        res.setdefault(name, []).append((sem, cnt[sem]))

    with contextlib.ExitStack() as stk:
        E = stk.enter_context
        stream = E(nc.sbuf_tensor("stream", [128, nch * DM], BF16))
        xfm = E(nc.sbuf_tensor("xfm", [128, 2 * DM], BF16))
        warena = E(nc.sbuf_tensor("warena", [128, 24576], BF16))
        wg_sb = E(nc.sbuf_tensor("wg_sb", [128, 24576], BF16))
        wu_sb = E(nc.sbuf_tensor("wu_sb", [128, 24576], BF16))
        scr = E(nc.sbuf_tensor("scr", [128, 13188], BF16))
        m32_sb = E(nc.sbuf_tensor("m32_sb", [128, 128], BF16))
        id_sb = E(nc.sbuf_tensor("id_sb", [128, 128], BF16))
        cst_sb = E(nc.sbuf_tensor("cst_sb", [128, 8], F32))
        den_sb = E(nc.sbuf_tensor("den_sb", [128, 16], F32))
        ln_sb = E(nc.sbuf_tensor("ln_sb", [128, 80], F32))
        pu_s1t = E(nc.sbuf_tensor("pu_s1t", [128, 256], F32)) if pus1_sbuf else None
        attnfm_t = E(nc.sbuf_tensor("attnfm_t", [128, 1024], BF16)) if attnfm_sep else None
        rec_t = E(nc.sbuf_tensor("rec_t", [128, 16], F32)) if rec_sep else None

        rec_sb = ln_sb[:, 64:80]
        if rec_sep:
            rec_sb = rec_t

        xps = scr[:, 0:2048]
        qk_tok = scr[:, 0:512]
        ksc = scr[:, 512:768]
        qk_pad = scr[:, 768:1792]
        pu_sb = scr[:, 1792:2048]
        pk_tok = scr[:, 12164:12676]
        pk_sc = scr[:, 12676:12932]
        phik = scr[:, 2048:6416]
        phiq = scr[:, 6416:7235]
        NQSL = 3
        phiqfm = scr[:, 7235:8003]
        v_sb = scr[:, 8003:9043]
        state = scr[:, 9043:12163]
        attn = scr[:, 12164:13188]
        attnfm = scr[:, 768:1792]
        if attnfm_sep:
            attnfm = attnfm_t
        h_tok = scr[:, 0:3072]
        hfm = scr[:, 3072:6144]
        silu_sb = scr[:, 6144:7680]
        ostage = scr[:, 7680:11776].bitcast(F32)

        psA = E(nc.psum_tensor("psA", [128, 512], F32))
        psB = E(nc.psum_tensor("psB", [128, 512], F32))
        psC = E(nc.psum_tensor("psC", [128, 512], F32))
        psD = E(nc.psum_tensor("psD", [128, 512], F32))
        psE = E(nc.psum_tensor("psE", [128, 512], F32))
        psF = E(nc.psum_tensor("psF", [128, 512], F32))
        psT = E(nc.psum_tensor("psT", [128, 1024], BF16))
        psU = E(nc.psum_tensor("psU", [128, 1024], BF16))

        sems = {s: E(nc.semaphore(s)) for s in cnt}

        wq_sb = warena[:, 0:2048]
        wk_sb = warena[:, 2048:4096]
        wv_sb = warena[:, 4096:12288]
        wo_sb = warena[:, 12288:20480]
        wd_sb = warena[:, 0:24576]

        def dma(dst, src):
            return lambda e, d=dst, s=src: e.dma_start(out=d, in_=s)

        emit("gpsimd", dma(id_sb[:, :], id_h[:, :]), inc=("dG", 16))

        emit("gpsimd", dma(wq_sb.rearrange("p (d j) -> p d j", d=8),
                           wq_h.rearrange("(d p) j -> p d j", p=128)), inc=("dG", 16))
        emit("gpsimd", dma(wk_sb.rearrange("p (d j) -> p d j", d=8),
                           wk_h.rearrange("(d p) j -> p d j", p=128)), inc=("dG", 16))
        emit("gpsimd", dma(wv_sb.rearrange("p (d j) -> p d j", d=8),
                           wv_h.rearrange("(d p) j -> p d j", p=128)), inc=("dG", 16))
        emit("gpsimd", dma(wo_sb.rearrange("p (d j) -> p d j", d=8),
                           wo_h.rearrange("(d p) j -> p d j", p=128)), inc=("dG", 16))
        W_ATTN = cnt["dG"]
        emit("gpsimd", dma(m32_sb[:, :], m32_h[:, :]), inc=("dG", 16))
        emit("sync", dma(cst_sb[:, :], cst_h[:, :]), inc=("dG", 16))
        W_CONST = cnt["dG"]

        emit("vector", lambda e: e.memset(
            phik.rearrange("p (h d) -> p h d", h=16)[:, :, 0:1], 1.0))
        emit("vector", lambda e: e.memset(
            phiq.rearrange("p (s d) -> p s d", s=3)[:, :, 0:1], 1.0))
        emit("vector", lambda e: e.memset(state[:, :], 0.0), inc=("sV", 1))
        res["state"] = [("sV", cnt["sV"])]

        emit("gpsimd", dma(wg_sb.rearrange("p (d j) -> p d j", d=8),
                           wg_h.rearrange("(d p) j -> p d j", p=128)), inc=("dG", 16))
        emit("gpsimd", dma(wu_sb.rearrange("p (d j) -> p d j", d=8),
                           wu_h.rearrange("(d p) j -> p d j", p=128)), inc=("dG", 16))
        W_GU = cnt["dG"]

        def tr(out, src, tp=None):
            return lambda e, o=out, s=src, t=tp: e.transpose(o, s, id_sb[:, :],
                                                             tile_position=t)

        def mm(out, lhsT, rhs, start, stop, tp=None):
            return lambda e, o=out, l=lhsT, r=rhs, a=start, b=stop, t=tp: e.matmul(
                o, l, r, start=a, stop=b, tile_position=t)

        def cp(eng_dst, src):
            return lambda e, d=eng_dst, s=src: e.tensor_copy(d, s)

        def out_slot(h):
            if h < 7:
                return psA, "psA", 65 * h
            if h < 14:
                return psB, "psB", 65 * (h - 7)
            return psC, "psC", 130 + 65 * (h - 14)

        addsnap = {h: ("sV", 0) for h in range(H)}

        def dump(tag, src_bf16, n_f32_cols=None):
            if tag in dump_h:
                w = [(s, cnt[s]) for s in ("sT", "sV", "sA", "sG")]
                emit("gpsimd", dma(dump_h[tag][:, :], src_bf16), waits=w, inc=("dD", 16))
                for eng in streams:
                    pending_dump[eng] = cnt["dD"]

        def scan_chunk(ci, xslot, vcol, kv_only, xld, qk_tok=qk_tok, ksc=ksc,
                       dump_cb=None, after_proj=None, after_pads=None, x_ld=0):
            xf = xfm[:, DM * xslot:DM * xslot + DM]
            w = [("dG", xld), ("dG", W_CONST), ("dG", W_ATTN)] + freew(
                "psA", "psB", "psC")
            for dc in range(8):
                lhsT = xf[:, 128 * dc:128 * dc + 128]
                if not kv_only:
                    emit("tensor", mm(psA[:, 0:256], lhsT,
                                      wq_sb[:, 256 * dc:256 * dc + 256], dc == 0, dc == 7),
                         waits=w if dc == 0 else ())
                    w = ()
                emit("tensor", mm(psA[:, 256:512], lhsT,
                                  wk_sb[:, 256 * dc:256 * dc + 256],
                                  (dc == 0) and kv_only, dc == 7),
                     waits=w if dc == 0 else ())
                w = ()
                emit("tensor", mm(psB[:, 0:512], lhsT,
                                  wv_sb[:, 1024 * dc:1024 * dc + 512], dc == 0, dc == 7))
                emit("tensor", mm(psC[:, 0:512], lhsT,
                                  wv_sb[:, 1024 * dc + 512:1024 * dc + 1024], dc == 0, dc == 7),
                     inc=("sT", 1) if dc == 7 else None)
            tproj = cnt["sT"]

            lo = 256 if kv_only else 0
            emit("vector", cp(qk_tok[:, lo:512], psA[:, lo:512]),
                 waits=[("sT", tproj)] + freew("qk"))
            emit("vector", lambda e: e.tensor_scalar(
                ksc[:, 0:256], qk_tok[:, 256:512], KQSC, None, ALU.mult),
                 waits=freew("qksc"))
            emit("vector", lambda e: e.tensor_scalar(
                phik.rearrange("p (h d) -> p h d", h=16)[:, :, 1:17],
                qk_tok[:, 256:512].rearrange("p (h f) -> p h f", h=16),
                KRSC, None, ALU.mult), waits=freew("phikraw"), inc=("sV", 1))
            kev = cnt["sV"]
            emit("vector", cp(v_sb.rearrange("p (h v) -> p h v", h=16)[:, 0:8, 0:64],
                              psB[:, 0:512].rearrange("p (h v) -> p h v", h=8)),
                 waits=freew("v"))
            emit("vector", cp(v_sb.rearrange("p (h v) -> p h v", h=16)[:, 8:16, 0:64],
                              psC[:, 0:512].rearrange("p (h v) -> p h v", h=8)))
            emit("vector", cp(v_sb.rearrange("p (h v) -> p h v", h=16)[:, :, 64:65],
                              cst_sb[:, vcol:vcol + 1].unsqueeze(1).broadcast_to([128, 16, 1])),
                 inc=("sV", 1))
            projev = cnt["sV"]
            reads("psA", "sV")
            reads("psB", "sV")
            reads("psC", "sV")

            for h in range(H):
                w = [("sV", kev)] + freew(f"phik{h}")
                def kq(e, h=h, qk_tok=qk_tok, ksc=ksc):
                    o = phik[:, DPHI * h + 17:DPHI * h + 273].rearrange(
                        "p (i j) -> p i j", i=16)
                    i0 = ksc[:, 16 * h:16 * h + 16].unsqueeze(2).broadcast_to([128, 16, 16])
                    i1 = qk_tok[:, 256 + 16 * h:256 + 16 * h + 16].unsqueeze(1).broadcast_to([128, 16, 16])
                    return e.tensor_tensor(o, i0, i1, ALU.mult)
                eng = "vector" if (kv_only and h % 8 == 7) else "gpsimd"
                sem = "sV" if eng == "vector" else "sG"
                emit(eng, kq, waits=w, inc=(sem, 1))
                res[f"kfm{h}"] = [(sem, cnt[sem])]

            if kv_only:
                for h in range(H):
                    ps, bank = (psE, "psE") if h % 2 == 0 else (psF, "psF")
                    c0 = PACK * h
                    w = res[f"kfm{h}"] + freew(bank)
                    if h == 0:
                        w.append(("sV", projev))
                    emit("tensor", mm(ps[:, 0:65], phik[:, DPHI * h:DPHI * h + 128],
                                      v_sb[:, 65 * h:65 * h + 65], True, True), waits=w)
                    emit("tensor", mm(ps[:, 65:130], phik[:, DPHI * h + 128:DPHI * h + 256],
                                      v_sb[:, 65 * h:65 * h + 65], True, True))
                    emit("tensor", mm(ps[0:17, 130:195], phik[:, DPHI * h + 256:DPHI * h + 273],
                                      v_sb[:, 65 * h:65 * h + 65], True, True),
                         inc=("sT", 1))
                    stv = cnt["sT"]
                    res[f"phik{h}"] = [("sT", stv)]
                    emit("vector", lambda e, ps=ps, c0=c0: e.tensor_tensor(
                        state[:, c0:c0 + PACK], ps[:, 0:PACK],
                        state[:, c0:c0 + PACK], ALU.add),
                         waits=[("sT", stv)] + freew("state") + [addsnap[h]],
                         inc=("sV", 1))
                    addsnap[h] = ("sV", cnt["sV"])
                    reads(bank, "sV")
                res["v"] = [("sT", cnt["sT"])]
                res["qk"] = [("sG", cnt["sG"]), ("sT", cnt["sT"])]
                res["qksc"] = [("sG", cnt["sG"])]
                res["phikraw"] = [("sG", cnt["sG"]), ("sT", cnt["sT"])]
                return

            if after_proj:
                after_proj()
            w = [("sV", projev)] + freew("psT")
            for h in range(H):
                j, g = h % 4, h // 4
                emit("tensor", tr(psT[32 * j:32 * j + 16, 128 * g:128 * g + 128],
                                  qk_tok[:, 16 * h:16 * h + 16], (0, 32 * j)),
                     waits=w if h == 0 else ())
                w = ()
            for h in range(H):
                j, g = h % 4, h // 4
                emit("tensor", tr(psT[32 * j:32 * j + 16, 512 + 128 * g:512 + 128 * g + 128],
                                  qk_tok[:, 256 + 16 * h:256 + 16 * h + 16], (0, 32 * j)),
                     inc=("sT", 1) if h == H - 1 else None)
            tpads = cnt["sT"]
            emit("vector", cp(qk_pad[:, :], psT[:, :]),
                 waits=[("sT", tpads)] + freew("attnfm"), inc=("sV", 1))
            padsev = cnt["sV"]
            reads("psT", "sV")
            if dump_cb:
                dump_cb("pads0", qk_pad[:, :])
            if after_pads:
                after_pads()

            for h in range(H):
                j, g = h % 4, h // 4
                osl, obank, ocol = out_slot(h)
                slot2 = h % 2
                slot4 = h % NQSL
                ubank, uname = (psC, "psC") if slot2 == 0 else (psD, "psD")
                phb, phbname = (psT, "psT") if slot2 == 0 else (psU, "psU")
                phislot = phiq[:, DPHI * slot4:DPHI * slot4 + DPHI]
                phifm = phiqfm[:, 384 * slot2:384 * slot2 + 384]
                pus1 = pu_s1t[:, 128 * slot2:128 * slot2 + 128] if pus1_sbuf else ubank[:, 384:512]
                pusl = pu_sb[:, 128 * slot2:128 * slot2 + 128]

                emit("vector", lambda e, h=h, sl=phislot: e.tensor_copy(
                    sl[:, 1:17], qk_tok[:, 16 * h:16 * h + 16]),
                     waits=freew(f"phiq{slot4}"))
                def qq(e, h=h, sl=phislot):
                    o = sl[:, 17:273].rearrange("p (i j) -> p i j", i=16)
                    i0 = qk_tok[:, 16 * h:16 * h + 16].unsqueeze(2).broadcast_to([128, 16, 16])
                    i1 = qk_tok[:, 16 * h:16 * h + 16].unsqueeze(1).broadcast_to([128, 16, 16])
                    return e.tensor_tensor(o, i0, i1, ALU.mult)
                emit("vector", qq, inc=("sV", 1))
                qqv = cnt["sV"]

                w = [("sV", padsev)] + freew(uname)
                emit("tensor", mm(ubank[:, 0:128],
                                  qk_pad[32 * j:32 * j + 16, 512 + 128 * g:512 + 128 * g + 128],
                                  qk_pad[32 * j:32 * j + 16, 128 * g:128 * g + 128],
                                  True, True, (32 * j, 0)), waits=w, inc=("sT", 1))
                tu = cnt["sT"]

                emit("scalar", lambda e, ub=ubank, s1=pus1: e.activation(
                    s1, ub[:, 0:128], AF.Square, bias=cst_sb[:, 2:3]),
                     waits=[("sT", tu)] + freew(f"pus1_{slot2}"), inc=("sA", 1))
                reads(uname, "sA")
                polyA = cnt["sA"]

                emit("vector", lambda e, s1=pus1, pu=pusl: e.scalar_tensor_tensor(
                    pu, s1, 16.0, m32_sb[:, :], ALU.add, ALU.mult),
                     waits=[("sA", polyA)] + freew(f"pu_{slot2}"), inc=("sV", 1))
                res[f"pus1_{slot2}"] = [("sV", cnt["sV"])]
                reads(uname, "sV")
                puV = cnt["sV"]
                if dump_cb and h == 0:
                    dump_cb("pu0", pusl)
                if dump_cb and h == 1:
                    emit("vector", cp(attn.bitcast(F32)[:, 0:128], pus1),
                         waits=[("sV", puV)], inc=("sV", 1))
                    dump_cb("s1h1", attn.bitcast(F32)[:, 0:128])
                    dump_cb("pu1", pusl)

                w = [("sV", qqv)] + freew(phbname)
                emit("tensor", tr(phb[:, 0:128], phislot[:, 0:128]), waits=w)
                emit("tensor", tr(phb[:, 128:256], phislot[:, 128:256]))
                emit("tensor", tr(phb[0:17, 256:384], phislot[:, 256:273]), inc=("sT", 1))
                res[f"phiq{slot4}"] = [("sT", cnt["sT"])]
                tphi = cnt["sT"]

                emit("scalar", lambda e, pb=phb, pf=phifm: e.copy(pf, pb[:, 0:384]),
                     waits=[("sT", tphi)] + freew(f"phifm{slot2}"), inc=("sA", 1))
                reads(phbname, "sA")
                phievA = cnt["sA"]
                if dump_cb and h == 0:
                    dump_cb("phiq0", phislot)
                    dump_cb("phifm0", phifm)

                emit("tensor", mm(osl[:, ocol:ocol + 65], pusl,
                                  v_sb[:, 65 * h:65 * h + 65], h in (0, 7, 14), False),
                     waits=[("sV", puV)], inc=("sT", 1))
                res[f"pu_{slot2}"] = [("sT", cnt["sT"])]

                c0 = PACK * h
                w = [("sA", phievA), addsnap[h]]
                emit("tensor", mm(osl[:, ocol:ocol + 65], phifm[:, 0:128],
                                  state[:, c0:c0 + 65], False, False), waits=w)
                emit("tensor", mm(osl[:, ocol:ocol + 65], phifm[:, 128:256],
                                  state[:, c0 + 65:c0 + 130], False, False))
                emit("tensor", mm(osl[:, ocol:ocol + 65], phifm[0:17, 256:384],
                                  state[0:17, c0 + 130:c0 + 195], False, True),
                     inc=("sT", 1))
                res[f"phifm{slot2}"] = [("sT", cnt["sT"])]

                ps, bank = (psE, "psE") if slot2 == 0 else (psF, "psF")
                w = res[f"kfm{h}"] + freew(bank)
                emit("tensor", mm(ps[:, 0:65], phik[:, DPHI * h:DPHI * h + 128],
                                  v_sb[:, 65 * h:65 * h + 65], True, True), waits=w)
                emit("tensor", mm(ps[:, 65:130], phik[:, DPHI * h + 128:DPHI * h + 256],
                                  v_sb[:, 65 * h:65 * h + 65], True, True))
                emit("tensor", mm(ps[0:17, 130:195], phik[:, DPHI * h + 256:DPHI * h + 273],
                                  v_sb[:, 65 * h:65 * h + 65], True, True),
                     inc=("sT", 1))
                stv = cnt["sT"]
                res[f"phik{h}"] = [("sT", stv)]
                emit("vector", lambda e, ps=ps, c0=c0: e.tensor_tensor(
                    state[:, c0:c0 + PACK], ps[:, 0:PACK],
                    state[:, c0:c0 + PACK], ALU.add),
                     waits=[("sT", stv)] + freew("state") + [addsnap[h]], inc=("sV", 1))
                addsnap[h] = ("sV", cnt["sV"])
                reads(bank, "sV")

                div_heads = (15,) if late_div else (6, 13, 15)
                if h in div_heads:
                  for h, glo, gbank, gname in ([(6, 0, psA, "psA"), (13, 7, psB, "psB"),
                                                (15, 14, psC, "psC")] if late_div else
                                               [(h, {6: 0, 13: 7, 15: 14}[h],
                                                 {6: psA, 13: psB, 15: psC}[h],
                                                 {6: "psA", 13: "psB", 15: "psC"}[h])]):
                      nh = h + 1 - glo
                      w = [("sT", stv)]
                      if h == 6:
                          w += freew("attn")
                      if gname == "psC":
                          dsrc = psC[:, 130:260].rearrange("p (h v) -> p h v", h=2)[:, :, 64:65].squeeze(2)
                      else:
                          dsrc = gbank[:, 0:455].rearrange("p (h v) -> p h v", h=7)[:, :, 64:65].squeeze(2)
                      emit("vector", cp(den_sb[:, glo:glo + nh], dsrc),
                           waits=w, inc=("sV", 1))
                      genv = cnt["sV"]
                      emit("vector", lambda e, glo=glo, nh=nh: e.tensor_scalar(
                          den_sb[:, glo:glo + nh], den_sb[:, glo:glo + nh],
                          EPS, None, ALU.add), waits=[("sV", genv)], inc=("sV", 1))
                      epsv = cnt["sV"]
                      emit("vector", lambda e, glo=glo, nh=nh: e.reciprocal(
                          rec_sb[:, glo:glo + nh], den_sb[:, glo:glo + nh]),
                           waits=[("sV", epsv)], inc=("sV", 1))
                      recv = cnt["sV"]
                      for h2 in range(glo, h + 1):
                          osl2, _, ocol2 = out_slot(h2)
                          emit("vector", lambda e, h2=h2, osl2=osl2, ocol2=ocol2: e.tensor_scalar(
                              attn[:, 64 * h2:64 * h2 + 64], osl2[:, ocol2:ocol2 + 64],
                              rec_sb[:, h2:h2 + 1], None, ALU.mult),
                               waits=[("sV", recv)] if h2 == glo else (),
                               inc=("sV", 1) if h2 == h else None)
                      reads(gname, "sV")

            divV = cnt["sV"]
            theads = cnt["sT"]
            res["v"] = [("sT", theads)]
            res["qk"] = [("sT", theads), ("sG", cnt["sG"]), ("sV", cnt["sV"])]
            res["qksc"] = [("sG", cnt["sG"]), ("sV", cnt["sV"])]
            res["phikraw"] = [("sG", cnt["sG"]), ("sT", theads)]

            if dump_cb:
                emit("vector", cp(attnfm.bitcast(F32)[:, 0:455], psA[:, 0:455]),
                     waits=[("sT", theads)] + freew("attnfm"), inc=("sV", 1))
                res["attnfm"] = [("sV", cnt["sV"])]
                dump_cb("o0", attnfm.bitcast(F32)[:, 0:455])
                dump_cb("den0", den_sb[:, :])
                dump_cb("rec0", rec_sb[:, :])

            def tail_a(divV=divV):
                w = [("sV", divV)] + freew("psT")
                for dc in range(8):
                    emit("tensor", tr(psT[:, 128 * dc:128 * dc + 128],
                                      attn[:, 128 * dc:128 * dc + 128]),
                         waits=w if dc == 0 else (), inc=("sT", 1) if dc == 7 else None)
                tat = cnt["sT"]
                reads("attn", "sT")
                emit("scalar", lambda e: e.copy(attnfm[:, :], psT[:, :]),
                     waits=[("sT", tat)] + freew("attnfm"), inc=("sA", 1))
                reads("psT", "sA")

            def tail_b(ci=ci):
                atev = cnt["sA"]
                w = [("sA", atev)] + freew("psD", "psF")
                for dc in range(8):
                    emit("tensor", mm(psD[:, 0:512], attnfm[:, 128 * dc:128 * dc + 128],
                                      wo_sb[:, 1024 * dc:1024 * dc + 512], dc == 0, dc == 7),
                         waits=w if dc == 0 else ())
                    emit("tensor", mm(psF[:, 0:512], attnfm[:, 128 * dc:128 * dc + 128],
                                      wo_sb[:, 1024 * dc + 512:1024 * dc + 1024], dc == 0, dc == 7),
                         inc=("sT", 1) if dc == 7 else None)
                two = cnt["sT"]
                res["attnfm"] = [("sT", two)]
                emit("vector", lambda e: e.tensor_tensor(
                    stream[:, DM * ci:DM * ci + 512], psD[:, 0:512],
                    stream[:, DM * ci:DM * ci + 512], ALU.add),
                     waits=[("sT", two), ("dG", x_ld)])
                emit("vector", lambda e: e.tensor_tensor(
                    stream[:, DM * ci + 512:DM * ci + 1024], psF[:, 0:512],
                    stream[:, DM * ci + 512:DM * ci + 1024], ALU.add), inc=("sV", 1))
                rv = cnt["sV"]
                reads("psD", "sV")
                reads("psF", "sV")
                res["wo_resid"] = [("sV", rv)]
                return rv
            return tail_a, tail_b

        def load_xfm(src_t, ci, slot):
            emit("gpsimd", dma(
                xfm[:, DM * slot:DM * slot + DM].rearrange("p (dc t) -> p dc t", dc=8),
                src_t.rearrange("(dc p) t -> p dc t", p=128)[:, :, C * ci:C * ci + C]),
                 waits=freew(f"xfm{slot}"), inc=("dG", 16))
            return cnt["dG"]

        for ci in range(npre):
            xld = load_xfm(xpt_h, ci, ci % 2)
            scan_chunk(ci, ci % 2, 1, True, xld, qk_tok=pk_tok, ksc=pk_sc)
            res[f"xfm{ci % 2}"] = [("sT", cnt["sT"])]

        if dumps:
            dump("state_pre", state[:, :], PACK * H)

        def ln_stats(ci, resid_sv):
            w = [("sV", resid_sv)]
            emit("scalar", lambda e, ci=ci: e.activation(
                attn[:, :], stream[:, DM * ci:DM * ci + DM],
                AF.Square, accum_out=ln_sb[:, 16 + ci:17 + ci]),
                 waits=w, inc=("sA", 1))
            reads("attn", "sA")
            emit("vector", lambda e, ci=ci: e.reduce_sum(
                ln_sb[:, ci:ci + 1], stream[:, DM * ci:DM * ci + DM], AX),
                 waits=w, inc=("sV", 1))


        pend = [None, None]
        no_pipe = False
        for ci in range(nch):
            emit("gpsimd", dma(stream[:, DM * ci:DM * ci + DM], x_h[C * ci:C * ci + C, :]),
                 inc=("dG", 16))
            x_ld = cnt["dG"]
            xld = load_xfm(xt_h, ci, ci % 2)
            if pend[0] is not None:
                pend[0]()
                pend[0] = None
            def tailb_only(ci=ci):
                if pend[1] is not None:
                    rv = pend[1]()
                    ln_stats(ci - 1, rv)
                    if ci - 1 == 1:
                        dump("stream1", stream[:, DM:2 * DM], DM)
                    pend[1] = None
            tails = scan_chunk(ci, ci % 2, 0, False, xld,
                               dump_cb=(dump if ci == 0 else None),
                               after_proj=tailb_only,
                               after_pads=None, x_ld=x_ld)
            pend = list(tails)
            res[f"xfm{ci % 2}"] = [("sT", cnt["sT"])]
            if ci == 0:
                dump("qk0", qk_tok[:, :], 512)
                dump("v0", v_sb[:, :], 1040)
                dump("phik0", phik[:, :], DPHI * H)
                dump("attn0", attn[:, :], DM)
                dump("state0", state[:, :], PACK * H)
        pend[0]()
        rv = pend[1]()
        ln_stats(nch - 1, rv)

        def ln_finish():
            sAs, sVs = cnt["sA"], cnt["sV"]
            emit("vector", lambda e: e.tensor_scalar(
                ln_sb[:, 32:48], ln_sb[:, 0:16], 1.0 / DM, None, ALU.mult),
                 waits=[("sA", sAs), ("sV", sVs)], inc=("sV", 1))
            meanv = cnt["sV"]
            emit("vector", lambda e: e.tensor_tensor(
                den_sb[:, 0:16], ln_sb[:, 32:48], ln_sb[:, 32:48], ALU.mult),
                 waits=[("sV", meanv)], inc=("sV", 1))
            msqv = cnt["sV"]
            emit("vector", lambda e: e.scalar_tensor_tensor(
                ln_sb[:, 64:80], ln_sb[:, 16:32], 1.0 / DM, den_sb[:, 0:16],
                ALU.mult, ALU.subtract), waits=[("sV", msqv)], inc=("sV", 1))
            vv = cnt["sV"]
            emit("scalar", lambda e: e.activation(rec_sb[:, 0:16], ln_sb[:, 64:80],
                                                  AF.Ln, bias=cst_sb[:, 3:4]), waits=[("sV", vv)])
            emit("scalar", lambda e: e.activation(ln_sb[:, 48:64], rec_sb[:, 0:16],
                                                  AF.Exp, scale=cst_sb[:, 4:5]), inc=("sA", 1))
            sa = cnt["sA"]
            emit("vector", lambda e: e.scalar_tensor_tensor(
                ln_sb[:, 64:80], ln_sb[:, 32:48], -1.0, ln_sb[:, 48:64],
                ALU.mult, ALU.mult), waits=[("sA", sa)], inc=("sV", 1))
            return cnt["sV"]

        vneg = ln_finish()
        if dumps:
            dump("lnsb", ln_sb[:, :], 96)
        for ci in range(nch):
            emit("vector", lambda e, ci=ci: e.tensor_scalar(
                stream[:, DM * ci:DM * ci + DM], stream[:, DM * ci:DM * ci + DM],
                ln_sb[:, 48 + ci:49 + ci], ln_sb[:, 64 + ci:65 + ci],
                ALU.mult, ALU.add),
                 waits=[("sV", vneg)] if ci == 0 else (), inc=("sV", 1))
            res[f"norm{ci}"] = [("sV", cnt["sV"])]
        if dumps:
            dump("y1", stream[:, 0:DM], DM)

        emit("gpsimd", dma(wd_sb.rearrange("p (hc j) -> p hc j", hc=24),
                           wd_h.rearrange("(hc p) j -> p hc j", p=128)),
             waits=[("sT", cnt["sT"])], inc=("dG", 16))
        W_WD = cnt["dG"]

        for ci in range(nch):
            w = res.pop(f"norm{ci}") + freew("psT")
            for dc in range(8):
                emit("tensor", tr(psT[:, 128 * dc:128 * dc + 128],
                                  stream[:, DM * ci + 128 * dc:DM * ci + 128 * dc + 128]),
                     waits=w if dc == 0 else (), inc=("sT", 1) if dc == 7 else None)
                w = ()
            ty = cnt["sT"]
            emit("vector", cp(xfm[:, 0:DM], psT[:, :]),
                 waits=[("sT", ty)] + freew("xfm0") + freew("xfm1"), inc=("sV", 1))
            reads("psT", "sV")
            y1ev = cnt["sV"]

            hm = []
            for half in range(2):
                w = [("sV", y1ev), ("dG", W_GU)] + freew(
                    "psA", "psB", "psC", "psD", "psE", "psF")
                for dc in range(8):
                    lhsT = xfm[:, 128 * dc:128 * dc + 128]
                    for k, ps in enumerate((psA, psB, psC)):
                        emit("tensor", mm(ps[:, 0:512], lhsT,
                                          wg_sb[:, 3072 * dc + 1536 * half + 512 * k:
                                                3072 * dc + 1536 * half + 512 * k + 512],
                                          dc == 0, dc == 7),
                             waits=w if (dc == 0 and k == 0) else ())
                        w = ()
                    for k, ps in enumerate((psD, psE, psF)):
                        emit("tensor", mm(ps[:, 0:512], lhsT,
                                          wu_sb[:, 3072 * dc + 1536 * half + 512 * k:
                                                3072 * dc + 1536 * half + 512 * k + 512],
                                          dc == 0, dc == 7),
                             inc=("sT", 1) if (dc == 7 and k == 2) else None)
                tgu = cnt["sT"]
                for k, (psg, psu2) in enumerate(((psA, psD), (psB, psE), (psC, psF))):
                    emit("scalar", lambda e, k=k, psg=psg: e.activation(
                        silu_sb[:, 512 * k:512 * k + 512], psg[:, 0:512], AF.Silu,
                        scale=cst_sb[:, 5:6]),
                         waits=[("sT", tgu)] + (freew("silu") if k == 0 else []),
                         inc=("sA", 1))
                    sk = cnt["sA"]
                    gname = ("psA", "psB", "psC")[k]
                    reads(gname, "sA")
                    emit("vector", lambda e, k=k, half=half, psu2=psu2: e.tensor_tensor(
                        h_tok[:, 1536 * half + 512 * k:1536 * half + 512 * k + 512],
                        silu_sb[:, 512 * k:512 * k + 512], psu2[:, 0:512], ALU.mult),
                         waits=[("sA", sk)] + (freew("htok") if (half == 0 and k == 0) else []),
                         inc=("sV", 1))
                    uname = ("psD", "psE", "psF")[k]
                    reads(uname, "sV")
                hm.append(cnt["sV"])
                res["silu"] = [("sV", cnt["sV"])]

            hevs = []
            for grp in range(3):
                phb, phn = (psT, "psT") if grp % 2 == 0 else (psU, "psU")
                w = [("sV", hm[0] if grp == 0 else hm[1])] + freew(phn)
                for i in range(8):
                    hc = 8 * grp + i
                    emit("tensor", tr(phb[:, 128 * i:128 * i + 128],
                                      h_tok[:, 128 * hc:128 * hc + 128]),
                         waits=w if i == 0 else (), inc=("sT", 1) if i == 7 else None)
                    w = ()
                tg = cnt["sT"]
                emit("scalar", lambda e, phb=phb, grp=grp: e.copy(
                    hfm[:, 1024 * grp:1024 * grp + 1024], phb[:, :]),
                     waits=[("sT", tg)] + (freew("hfm") if grp == 0 else []),
                     inc=("sA", 1))
                reads(phn, "sA")
                hevs.append(cnt["sA"])

            w = [("dG", W_WD)] + freew("psA", "psB")
            for hc in range(24):
                if hc % 8 == 0:
                    w = w + [("sA", hevs[hc // 8])]
                emit("tensor", mm(psA[:, 0:512], hfm[:, 128 * hc:128 * hc + 128],
                                  wd_sb[:, 1024 * hc:1024 * hc + 512], hc == 0, hc == 23),
                     waits=w)
                w = []
                emit("tensor", mm(psB[:, 0:512], hfm[:, 128 * hc:128 * hc + 128],
                                  wd_sb[:, 1024 * hc + 512:1024 * hc + 1024], hc == 0, hc == 23),
                     inc=("sT", 1) if hc == 23 else None)
            td = cnt["sT"]
            res["hfm"] = [("sT", td)]
            res["htok"] = [("sT", td)]
            emit("vector", lambda e, ci=ci: e.scalar_tensor_tensor(
                stream[:, DM * ci:DM * ci + 512], psA[:, 0:512], FFSC,
                stream[:, DM * ci:DM * ci + 512], ALU.mult, ALU.add),
                 waits=[("sT", td)])
            emit("vector", lambda e, ci=ci: e.scalar_tensor_tensor(
                stream[:, DM * ci + 512:DM * ci + 1024], psB[:, 0:512], FFSC,
                stream[:, DM * ci + 512:DM * ci + 1024], ALU.mult, ALU.add),
                 inc=("sV", 1))
            zr = cnt["sV"]
            res["psA"] = [("sV", zr)]
            res["psB"] = [("sV", zr)]
            res[f"z{ci}"] = [("sV", zr)]
            ln_stats(ci, zr)

        if dumps:
            dump("z0", stream[:, 0:DM], DM)

        vneg2 = ln_finish()
        for ci in range(nch):
            stg = ostage[:, 1024 * (ci % 2):1024 * (ci % 2) + 1024]
            w = [("sV", vneg2)] if ci == 0 else []
            if ci >= 2:
                w.append(("dS", 16 * (ci - 1)))
            emit("vector", lambda e, ci=ci, stg=stg: e.tensor_scalar(
                stg, stream[:, DM * ci:DM * ci + DM],
                ln_sb[:, 48 + ci:49 + ci], ln_sb[:, 64 + ci:65 + ci],
                ALU.mult, ALU.add),
                 waits=w, inc=("sV", 1))
            nv = cnt["sV"]
            emit("scalar", lambda e, ci=ci, stg=stg: e.dma_start(
                out=y_h[C * ci:C * ci + C, :], in_=stg),
                 waits=[("sV", nv)], inc=("dS", 16))

        final_dS = cnt["dS"]
        final_dD = cnt["dD"]
        emit("scalar", lambda e: e.nop(), waits=[("dS", final_dS)])
        emit("sync", lambda e: e.nop(),
             waits=[("dS", final_dS)] + ([("dD", final_dD)] if final_dD else []))
        emit("gpsimd", lambda e: e.nop(), waits=[("dS", final_dS)])

        with nc.Block() as block:
            def runner(eng_name):
                def go(e):
                    for fn, waits, inc in streams[eng_name]:
                        for s, v in waits:
                            e.wait_ge(sems[s], v)
                        i = fn(e)
                        if inc:
                            i.then_inc(sems[inc[0]], inc[1])
                return go
            block.sync(runner("sync"))
            block.gpsimd(runner("gpsimd"))
            block.vector(runner("vector"))
            block.scalar(runner("scalar"))
            block.tensor(runner("tensor"))

    return nc


def make_inputs_for_core(core, x_full, w):
    b, half = core // 2, core % 2
    x_main = np.ascontiguousarray(x_full[b, half * T:(half + 1) * T])
    x_t = np.ascontiguousarray(x_main.T)
    if half == 1:
        x_pt = np.ascontiguousarray(x_full[b, 0:T].T)
        pmask = 1.0
    else:
        x_pt = np.zeros((DM, T), np.float32)
        pmask = 0.0
    cst = np.zeros((128, 8), np.float32)
    cst[:, 0] = 1.0
    cst[:, 1] = pmask
    cst[:, 2] = 4.0
    cst[:, 3] = LNEPS
    cst[:, 4] = -0.5
    cst[:, 5] = 1.0 / 127.0
    m = {"x": x_main, "xt": x_t, "xpt": x_pt, "cst": cst}
    m.update(w)
    return m


def shared_weights(inputs):
    ident = np.eye(128, dtype=np.float32)
    m32 = (np.triu(np.ones((128, 128), np.float32)) / 32.0).astype(np.float32)
    return {
        "wq": np.ascontiguousarray(inputs["L0_attn_Wq"], np.float32),
        "wk": np.ascontiguousarray(inputs["L0_attn_Wk"], np.float32),
        "wv": np.ascontiguousarray(inputs["L0_attn_Wv"], np.float32),
        "wo": np.ascontiguousarray(inputs["L0_attn_Wo"], np.float32),
        "wg": np.ascontiguousarray(inputs["L0_ffn_w_gate"].T),
        "wu": np.ascontiguousarray(inputs["L0_ffn_w_up"].T),
        "wd": np.ascontiguousarray(inputs["L0_ffn_w_down"].T),
        "ident": ident,
        "m32": m32,
    }


def kernel(L0_attn_hidden_states, L0_attn_Wq, L0_attn_Wk, L0_attn_Wv,
           L0_attn_Wo, L0_ffn_w_gate, L0_ffn_w_up, L0_ffn_w_down):
    inputs = {
        "L0_attn_hidden_states": np.asarray(L0_attn_hidden_states),
        "L0_attn_Wq": np.asarray(L0_attn_Wq), "L0_attn_Wk": np.asarray(L0_attn_Wk),
        "L0_attn_Wv": np.asarray(L0_attn_Wv), "L0_attn_Wo": np.asarray(L0_attn_Wo),
        "L0_ffn_w_gate": np.asarray(L0_ffn_w_gate),
        "L0_ffn_w_up": np.asarray(L0_ffn_w_up),
        "L0_ffn_w_down": np.asarray(L0_ffn_w_down),
    }
    x = inputs["L0_attn_hidden_states"]
    w = shared_weights(inputs)
    nc = build_nc()
    in_maps = [make_inputs_for_core(c, x, w) for c in range(8)]
    res = run_bass_kernel_spmd(nc, in_maps, core_ids=list(range(8)))
    out = np.empty((4, 4096, DM), np.float32)
    for c in range(8):
        b, half = c // 2, c % 2
        out[b, half * T:(half + 1) * T] = res.results[c]["y"]
    return out


# version 08
# speedup vs baseline: 1.2478x; 1.1760x over previous
import contextlib
import numpy as np
import concourse.bass as bass
import concourse.mybir as mybir
from concourse.bass_utils import run_bass_kernel_spmd

F32 = mybir.dt.float32
BF16 = mybir.dt.bfloat16
I32 = mybir.dt.int32
AF = mybir.ActivationFunctionType
ALU = mybir.AluOpType
AX = mybir.AxisListType.X

C = 128
H = 16
DM = 1024
DFF = 3072
T = 2048
DPHI = 273
PACK = 195
EPS = 1e-9
LNEPS = 1e-5
KQSC = 1.0 / 32.0
KRSC = 0.25
FFSC = float(1.0 / (127.0 * 127.0))


def build_nc(nch=T // C, npre=T // C, dumps=(), pus1_sbuf=False, late_div=False,
             attnfm_sep=False, rec_sep=False, phiq3=True):
    nc = bass.Bass("TRN2", target_bir_lowering=False)
    x_h = nc.dram_tensor("x", [T, DM], F32, kind="ExternalInput").ap()
    xt_h = nc.dram_tensor("xt", [DM, T], F32, kind="ExternalInput").ap()
    xpt_h = nc.dram_tensor("xpt", [DM, T], F32, kind="ExternalInput").ap()
    cst_h = nc.dram_tensor("cst", [128, 8], F32, kind="ExternalInput").ap()
    wq_h = nc.dram_tensor("wq", [DM, 256], F32, kind="ExternalInput").ap()
    wk_h = nc.dram_tensor("wk", [DM, 256], F32, kind="ExternalInput").ap()
    wv_h = nc.dram_tensor("wv", [DM, DM], F32, kind="ExternalInput").ap()
    wo_h = nc.dram_tensor("wo", [DM, DM], F32, kind="ExternalInput").ap()
    wg_h = nc.dram_tensor("wg", [DM, DFF], I32, kind="ExternalInput").ap()
    wu_h = nc.dram_tensor("wu", [DM, DFF], I32, kind="ExternalInput").ap()
    wd_h = nc.dram_tensor("wd", [DFF, DM], I32, kind="ExternalInput").ap()
    id_h = nc.dram_tensor("ident", [128, 128], F32, kind="ExternalInput").ap()
    m32_h = nc.dram_tensor("m32", [128, 128], F32, kind="ExternalInput").ap()
    y_h = nc.dram_tensor("y", [T, DM], F32, kind="ExternalOutput").ap()
    dump_h = {}
    for tag, shape in dumps:
        dump_h[tag] = nc.dram_tensor("dbg_" + tag, shape, F32, kind="ExternalOutput").ap()

    streams = {e: [] for e in ("sync", "gpsimd", "vector", "scalar", "tensor")}
    cnt = {s: 0 for s in ("dG", "dS", "dD", "sT", "sV", "sA", "sG")}
    res = {}

    pending_dump = {}

    def emit(eng, fn, waits=(), inc=None):
        wmax = {}
        pd = pending_dump.pop(eng, None)
        waits = list(waits) + ([("dD", pd)] if pd else [])
        for s, v in waits:
            assert v <= cnt[s], f"wait {s}>={v}, emitted {cnt[s]}"
            if v > 0:
                wmax[s] = max(wmax.get(s, 0), v)
        if inc:
            s, n = inc
            cnt[s] += n
        streams[eng].append((fn, tuple(wmax.items()), inc))

    def freew(*names):
        w = []
        for n in names:
            w += res.pop(n, [])
        return w

    def reads(name, sem):
        res.setdefault(name, []).append((sem, cnt[sem]))

    with contextlib.ExitStack() as stk:
        E = stk.enter_context
        stream = E(nc.sbuf_tensor("stream", [128, nch * DM], BF16))
        xfm = E(nc.sbuf_tensor("xfm", [128, 2 * DM], BF16))
        warena = E(nc.sbuf_tensor("warena", [128, 24576], BF16))
        wg_sb = E(nc.sbuf_tensor("wg_sb", [128, 24576], BF16))
        wu_sb = E(nc.sbuf_tensor("wu_sb", [128, 24576], BF16))
        scr = E(nc.sbuf_tensor("scr", [128, 13188], BF16))
        m32_sb = E(nc.sbuf_tensor("m32_sb", [128, 128], BF16))
        id_sb = E(nc.sbuf_tensor("id_sb", [128, 128], BF16))
        cst_sb = E(nc.sbuf_tensor("cst_sb", [128, 8], F32))
        den_sb = E(nc.sbuf_tensor("den_sb", [128, 16], F32))
        ln_sb = E(nc.sbuf_tensor("ln_sb", [128, 80], F32))
        pu_s1t = E(nc.sbuf_tensor("pu_s1t", [128, 512], BF16))
        attnfm_t = E(nc.sbuf_tensor("attnfm_t", [128, 1024], BF16)) if attnfm_sep else None
        rec_t = E(nc.sbuf_tensor("rec_t", [128, 16], F32)) if rec_sep else None

        rec_sb = ln_sb[:, 64:80]
        if rec_sep:
            rec_sb = rec_t

        xps = scr[:, 0:2048]
        qk_tok = scr[:, 0:512]
        ksc = scr[:, 512:768]
        qk_pad = scr[:, 768:1792]
        pu_sb = scr[:, 1792:2048]
        pk_tok = scr[:, 12164:12676]
        pk_sc = scr[:, 12676:12932]
        phik = scr[:, 2048:6416]
        phiq = scr[:, 6416:7235]
        NQSL = 3
        phiqfm = scr[:, 7235:8003]
        v_sb = scr[:, 8003:9043]
        state = scr[:, 9043:12163]
        attn = scr[:, 12164:13188]
        attnfm = scr[:, 768:1792]
        if attnfm_sep:
            attnfm = attnfm_t
        h_tok = scr[:, 0:3072]
        hfm = scr[:, 3072:6144]
        silu_sb = scr[:, 6144:7680]
        ostage = scr[:, 7680:11776].bitcast(F32)

        psA = E(nc.psum_tensor("psA", [128, 512], F32))
        psB = E(nc.psum_tensor("psB", [128, 512], F32))
        psC = E(nc.psum_tensor("psC", [128, 512], F32))
        psD = E(nc.psum_tensor("psD", [128, 512], F32))
        psE = E(nc.psum_tensor("psE", [128, 512], F32))
        psF = E(nc.psum_tensor("psF", [128, 512], F32))
        psT = E(nc.psum_tensor("psT", [128, 1024], BF16))
        psU = E(nc.psum_tensor("psU", [128, 1024], BF16))

        sems = {s: E(nc.semaphore(s)) for s in cnt}

        wq_sb = warena[:, 0:2048]
        wk_sb = warena[:, 2048:4096]
        wv_sb = warena[:, 4096:12288]
        wo_sb = warena[:, 12288:20480]
        wd_sb = warena[:, 0:24576]

        def dma(dst, src):
            return lambda e, d=dst, s=src: e.dma_start(out=d, in_=s)

        emit("gpsimd", dma(id_sb[:, :], id_h[:, :]), inc=("dG", 16))

        emit("gpsimd", dma(wq_sb.rearrange("p (d j) -> p d j", d=8),
                           wq_h.rearrange("(d p) j -> p d j", p=128)), inc=("dG", 16))
        emit("gpsimd", dma(wk_sb.rearrange("p (d j) -> p d j", d=8),
                           wk_h.rearrange("(d p) j -> p d j", p=128)), inc=("dG", 16))
        emit("gpsimd", dma(wv_sb.rearrange("p (d j) -> p d j", d=8),
                           wv_h.rearrange("(d p) j -> p d j", p=128)), inc=("dG", 16))
        emit("gpsimd", dma(wo_sb.rearrange("p (d j) -> p d j", d=8),
                           wo_h.rearrange("(d p) j -> p d j", p=128)), inc=("dG", 16))
        W_ATTN = cnt["dG"]
        emit("gpsimd", dma(m32_sb[:, :], m32_h[:, :]), inc=("dG", 16))
        emit("sync", dma(cst_sb[:, :], cst_h[:, :]), inc=("dG", 16))
        W_CONST = cnt["dG"]

        emit("vector", lambda e: e.memset(
            phik.rearrange("p (h d) -> p h d", h=16)[:, :, 0:1], 1.0))
        emit("vector", lambda e: e.memset(
            phiq.rearrange("p (s d) -> p s d", s=3)[:, :, 0:1], 1.0))
        emit("vector", lambda e: e.memset(state[:, :], 0.0), inc=("sV", 1))
        res["state"] = [("sV", cnt["sV"])]

        emit("gpsimd", dma(wg_sb.rearrange("p (d j) -> p d j", d=8),
                           wg_h.rearrange("(d p) j -> p d j", p=128)), inc=("dG", 16))
        emit("gpsimd", dma(wu_sb.rearrange("p (d j) -> p d j", d=8),
                           wu_h.rearrange("(d p) j -> p d j", p=128)), inc=("dG", 16))
        W_GU = cnt["dG"]

        def tr(out, src, tp=None):
            return lambda e, o=out, s=src, t=tp: e.transpose(o, s, id_sb[:, :],
                                                             tile_position=t)

        def mm(out, lhsT, rhs, start, stop, tp=None):
            return lambda e, o=out, l=lhsT, r=rhs, a=start, b=stop, t=tp: e.matmul(
                o, l, r, start=a, stop=b, tile_position=t)

        def cp(eng_dst, src):
            return lambda e, d=eng_dst, s=src: e.tensor_copy(d, s)

        def out_slot(h):
            if h < 7:
                return psA, "psA", 65 * h
            if h < 14:
                return psB, "psB", 65 * (h - 7)
            return psC, "psC", 130 + 65 * (h - 14)

        addsnap = {h: ("sV", 0) for h in range(H)}

        def dump(tag, src_bf16, n_f32_cols=None):
            if tag in dump_h:
                w = [(s, cnt[s]) for s in ("sT", "sV", "sA", "sG")]
                emit("gpsimd", dma(dump_h[tag][:, :], src_bf16), waits=w, inc=("dD", 16))
                for eng in streams:
                    pending_dump[eng] = cnt["dD"]

        def scan_chunk(ci, xslot, vcol, kv_only, xld, qk_tok=qk_tok, ksc=ksc,
                       dump_cb=None, after_proj=None, after_pads=None, x_ld=0):
            xf = xfm[:, DM * xslot:DM * xslot + DM]
            w = [("dG", xld), ("dG", W_CONST), ("dG", W_ATTN)] + freew(
                "psA", "psB", "psC")
            for dc in range(8):
                lhsT = xf[:, 128 * dc:128 * dc + 128]
                if not kv_only:
                    emit("tensor", mm(psA[:, 0:256], lhsT,
                                      wq_sb[:, 256 * dc:256 * dc + 256], dc == 0, dc == 7),
                         waits=w if dc == 0 else ())
                    w = ()
                emit("tensor", mm(psA[:, 256:512], lhsT,
                                  wk_sb[:, 256 * dc:256 * dc + 256],
                                  (dc == 0) and kv_only, dc == 7),
                     waits=w if dc == 0 else ())
                w = ()
                emit("tensor", mm(psB[:, 0:512], lhsT,
                                  wv_sb[:, 1024 * dc:1024 * dc + 512], dc == 0, dc == 7))
                emit("tensor", mm(psC[:, 0:512], lhsT,
                                  wv_sb[:, 1024 * dc + 512:1024 * dc + 1024], dc == 0, dc == 7),
                     inc=("sT", 1) if dc == 7 else None)
            tproj = cnt["sT"]

            lo = 256 if kv_only else 0
            emit("vector", cp(qk_tok[:, lo:512], psA[:, lo:512]),
                 waits=[("sT", tproj)] + freew("qk"))
            emit("vector", lambda e: e.tensor_scalar(
                ksc[:, 0:256], qk_tok[:, 256:512], KQSC, None, ALU.mult),
                 waits=freew("qksc"))
            emit("vector", lambda e: e.tensor_scalar(
                phik.rearrange("p (h d) -> p h d", h=16)[:, :, 1:17],
                qk_tok[:, 256:512].rearrange("p (h f) -> p h f", h=16),
                KRSC, None, ALU.mult), waits=freew("phikraw"), inc=("sV", 1))
            kev = cnt["sV"]
            emit("vector", cp(v_sb.rearrange("p (h v) -> p h v", h=16)[:, 0:8, 0:64],
                              psB[:, 0:512].rearrange("p (h v) -> p h v", h=8)),
                 waits=freew("v"))
            emit("vector", cp(v_sb.rearrange("p (h v) -> p h v", h=16)[:, 8:16, 0:64],
                              psC[:, 0:512].rearrange("p (h v) -> p h v", h=8)))
            emit("vector", cp(v_sb.rearrange("p (h v) -> p h v", h=16)[:, :, 64:65],
                              cst_sb[:, vcol:vcol + 1].unsqueeze(1).broadcast_to([128, 16, 1])),
                 inc=("sV", 1))
            projev = cnt["sV"]
            reads("psA", "sV")
            reads("psB", "sV")
            reads("psC", "sV")

            for h in range(H):
                w = [("sV", kev)] + freew(f"phik{h}")
                def kq(e, h=h, qk_tok=qk_tok, ksc=ksc):
                    o = phik[:, DPHI * h + 17:DPHI * h + 273].rearrange(
                        "p (i j) -> p i j", i=16)
                    i0 = ksc[:, 16 * h:16 * h + 16].unsqueeze(2).broadcast_to([128, 16, 16])
                    i1 = qk_tok[:, 256 + 16 * h:256 + 16 * h + 16].unsqueeze(1).broadcast_to([128, 16, 16])
                    return e.tensor_tensor(o, i0, i1, ALU.mult)
                eng = "vector" if (kv_only and h % 8 == 7) else "gpsimd"
                sem = "sV" if eng == "vector" else "sG"
                emit(eng, kq, waits=w, inc=(sem, 1))
                res[f"kfm{h}"] = [(sem, cnt[sem])]

            if kv_only:
                for h in range(H):
                    ps, bank = (psE, "psE") if h % 2 == 0 else (psF, "psF")
                    c0 = PACK * h
                    w = res[f"kfm{h}"] + freew(bank)
                    if h == 0:
                        w.append(("sV", projev))
                    emit("tensor", mm(ps[:, 0:65], phik[:, DPHI * h:DPHI * h + 128],
                                      v_sb[:, 65 * h:65 * h + 65], True, True), waits=w)
                    emit("tensor", mm(ps[:, 65:130], phik[:, DPHI * h + 128:DPHI * h + 256],
                                      v_sb[:, 65 * h:65 * h + 65], True, True))
                    emit("tensor", mm(ps[0:17, 130:195], phik[:, DPHI * h + 256:DPHI * h + 273],
                                      v_sb[:, 65 * h:65 * h + 65], True, True),
                         inc=("sT", 1))
                    stv = cnt["sT"]
                    res[f"phik{h}"] = [("sT", stv)]
                    emit("vector", lambda e, ps=ps, c0=c0: e.tensor_tensor(
                        state[:, c0:c0 + PACK], ps[:, 0:PACK],
                        state[:, c0:c0 + PACK], ALU.add),
                         waits=[("sT", stv)] + freew("state") + [addsnap[h]],
                         inc=("sV", 1))
                    addsnap[h] = ("sV", cnt["sV"])
                    reads(bank, "sV")
                res["v"] = [("sT", cnt["sT"])]
                res["qk"] = [("sG", cnt["sG"]), ("sT", cnt["sT"])]
                res["qksc"] = [("sG", cnt["sG"])]
                res["phikraw"] = [("sG", cnt["sG"]), ("sT", cnt["sT"])]
                return

            if after_proj:
                after_proj()
            w = [("sV", projev)] + freew("psT")
            for h in range(H):
                j, g = h % 4, h // 4
                emit("tensor", tr(psT[32 * j:32 * j + 16, 128 * g:128 * g + 128],
                                  qk_tok[:, 16 * h:16 * h + 16], (0, 32 * j)),
                     waits=w if h == 0 else ())
                w = ()
            for h in range(H):
                j, g = h % 4, h // 4
                emit("tensor", tr(psT[32 * j:32 * j + 16, 512 + 128 * g:512 + 128 * g + 128],
                                  qk_tok[:, 256 + 16 * h:256 + 16 * h + 16], (0, 32 * j)),
                     inc=("sT", 1) if h == H - 1 else None)
            tpads = cnt["sT"]
            emit("vector", cp(qk_pad[:, :], psT[:, :]),
                 waits=[("sT", tpads)] + freew("attnfm"), inc=("sV", 1))
            padsev = cnt["sV"]
            reads("psT", "sV")
            if dump_cb:
                dump_cb("pads0", qk_pad[:, :])
            if after_pads:
                after_pads()

            def u_bank(h):
                return (psD, "psD") if h % 2 == 0 else (psF, "psF")

            def emit_u(h):
                j, g = h % 4, h // 4
                ub, ubn = u_bank(h)
                col = 128 * ((h // 2) % 2)
                w = [("sV", padsev)] + freew(f"uslot{h % 4}")
                if h < 2:
                    w += freew("psD")
                elif h < 4:
                    w += freew("psF")
                else:
                    pass
                if h >= 2:
                    w += [("sA", poly_l[h - 2])]
                emit("tensor", mm(ub[:, col:col + 128],
                                  qk_pad[32 * j:32 * j + 16, 512 + 128 * g:512 + 128 * g + 128],
                                  qk_pad[32 * j:32 * j + 16, 128 * g:128 * g + 128],
                                  True, True, (32 * j, 0)), waits=w, inc=("sT", 1))
                return cnt["sT"]

            def emit_qfm(h):
                sl = phiq[:, DPHI * (h % NQSL):DPHI * (h % NQSL) + DPHI]
                slot_free = freew(f"phiq{h % NQSL}")
                emit("scalar", lambda e, h=h, sl=sl: e.copy(
                    sl[:, 1:17], qk_tok[:, 16 * h:16 * h + 16]),
                     waits=slot_free + [("sV", kev)], inc=("sA", 1))
                wr = [("sA", cnt["sA"])]
                def qq(e, h=h, sl=sl):
                    o = sl[:, 17:273].rearrange("p (i j) -> p i j", i=16)
                    i0 = qk_tok[:, 16 * h:16 * h + 16].unsqueeze(2).broadcast_to([128, 16, 16])
                    i1 = qk_tok[:, 16 * h:16 * h + 16].unsqueeze(1).broadcast_to([128, 16, 16])
                    return e.tensor_tensor(o, i0, i1, ALU.mult)
                eng = "vector" if h % 2 == 0 else "gpsimd"
                sem = "sV" if eng == "vector" else "sG"
                emit(eng, qq, waits=slot_free + [("sV", kev)], inc=(sem, 1))
                return wr + [(sem, cnt[sem])]

            def emit_poly(h, tu):
                ub, ubn = u_bank(h)
                s1 = pu_s1t[:, 128 * (h % 4):128 * (h % 4) + 128]
                ucol = 128 * ((h // 2) % 2)
                emit("scalar", lambda e, ub=ub, s1=s1, ucol=ucol: e.activation(
                    s1, ub[:, ucol:ucol + 128], AF.Square,
                    bias=cst_sb[:, 2:3]),
                     waits=[("sT", tu)] + freew(f"pus1_{h % 4}"), inc=("sA", 1))
                reads(f"uslot{h % 4}", "sA")
                reads(ubn, "sA")
                return cnt["sA"]

            def emit_stt(h, polyA):
                ub, ubn = u_bank(h)
                s1 = pu_s1t[:, 128 * (h % 4):128 * (h % 4) + 128]
                pu = pu_sb[:, 128 * (h % 2):128 * (h % 2) + 128]
                emit("vector", lambda e, s1=s1, pu=pu: e.scalar_tensor_tensor(
                    pu, s1, 16.0, m32_sb[:, :], ALU.add, ALU.mult),
                     waits=[("sA", polyA)] + freew(f"pu_{h % 2}"), inc=("sV", 1))
                res[f"pus1_{h % 4}"] = [("sV", cnt["sV"])]
                reads(ubn, "sV")
                return cnt["sV"]

            def emit_phiT(h, qfm_w):
                phb, phn = (psT, "psT") if h % 2 == 0 else (psU, "psU")
                sl = phiq[:, DPHI * (h % NQSL):DPHI * (h % NQSL) + DPHI]
                w = list(qfm_w) + freew(phn)
                emit("tensor", tr(phb[:, 0:128], sl[:, 0:128]), waits=w)
                emit("tensor", tr(phb[:, 128:256], sl[:, 128:256]))
                emit("tensor", tr(phb[0:17, 256:384], sl[:, 256:273]), inc=("sT", 1))
                res[f"phiq{h % NQSL}"] = [("sT", cnt["sT"])]
                return cnt["sT"]

            def emit_phiev(h, tphi):
                phb, phn = (psT, "psT") if h % 2 == 0 else (psU, "psU")
                pf = phiqfm[:, 384 * (h % 2):384 * (h % 2) + 384]
                emit("scalar", lambda e, pb=phb, pf=pf: e.copy(pf, pb[:, 0:384]),
                     waits=[("sT", tphi)] + freew(f"phifm{h % 2}"), inc=("sA", 1))
                reads(phn, "sA")
                return cnt["sA"]

            def emit_qkv_inter_state(h, puV, phievA):
                osl, obank, ocol = out_slot(h)
                pu = pu_sb[:, 128 * (h % 2):128 * (h % 2) + 128]
                pf = phiqfm[:, 384 * (h % 2):384 * (h % 2) + 384]
                emit("tensor", mm(osl[:, ocol:ocol + 65], pu,
                                  v_sb[:, 65 * h:65 * h + 65], h in (0, 7, 14), False),
                     waits=[("sV", puV)], inc=("sT", 1))
                res[f"pu_{h % 2}"] = [("sT", cnt["sT"])]
                c0 = PACK * h
                w = [("sA", phievA), addsnap[h]]
                emit("tensor", mm(osl[:, ocol:ocol + 65], pf[:, 0:128],
                                  state[:, c0:c0 + 65], False, False), waits=w)
                emit("tensor", mm(osl[:, ocol:ocol + 65], pf[:, 128:256],
                                  state[:, c0 + 65:c0 + 130], False, False))
                emit("tensor", mm(osl[:, ocol:ocol + 65], pf[0:17, 256:384],
                                  state[0:17, c0 + 130:c0 + 195], False, True),
                     inc=("sT", 1))
                res[f"phifm{h % 2}"] = [("sT", cnt["sT"])]
                half = 195 * (h % 2)
                w = res[f"kfm{h}"] + (freew("psE") if h % 2 == 0 else [])
                emit("tensor", mm(psE[:, half:half + 65], phik[:, DPHI * h:DPHI * h + 128],
                                  v_sb[:, 65 * h:65 * h + 65], True, True), waits=w)
                emit("tensor", mm(psE[:, half + 65:half + 130],
                                  phik[:, DPHI * h + 128:DPHI * h + 256],
                                  v_sb[:, 65 * h:65 * h + 65], True, True))
                emit("tensor", mm(psE[0:17, half + 130:half + 195],
                                  phik[:, DPHI * h + 256:DPHI * h + 273],
                                  v_sb[:, 65 * h:65 * h + 65], True, True),
                     inc=("sT", 1))
                stv = cnt["sT"]
                res[f"phik{h}"] = [("sT", stv)]
                if h % 2 == 1:
                    c0p = PACK * (h - 1)
                    emit("vector", lambda e, c0p=c0p: e.tensor_tensor(
                        state[:, c0p:c0p + 2 * PACK], psE[:, 0:2 * PACK],
                        state[:, c0p:c0p + 2 * PACK], ALU.add),
                         waits=[("sT", stv)] + freew("state")
                               + [addsnap[h - 1], addsnap[h]], inc=("sV", 1))
                    addsnap[h - 1] = addsnap[h] = ("sV", cnt["sV"])
                    reads("psE", "sV")
                return stv

            def emit_div_group(h, stv):
                glo = {6: 0, 13: 7, 15: 14}[h]
                gbank, gname = {6: (psA, "psA"), 13: (psB, "psB"),
                                15: (psC, "psC")}[h]
                nh = h + 1 - glo
                w = [("sT", stv)]
                if h == 6:
                    w += freew("attn")
                if gname == "psC":
                    dsrc = psC[:, 130:260].rearrange("p (h v) -> p h v", h=2)[:, :, 64:65].squeeze(2)
                else:
                    dsrc = gbank[:, 0:455].rearrange("p (h v) -> p h v", h=7)[:, :, 64:65].squeeze(2)
                emit("vector", cp(den_sb[:, glo:glo + nh], dsrc), waits=w, inc=("sV", 1))
                genv = cnt["sV"]
                emit("vector", lambda e, glo=glo, nh=nh: e.tensor_scalar(
                    den_sb[:, glo:glo + nh], den_sb[:, glo:glo + nh],
                    EPS, None, ALU.add), waits=[("sV", genv)], inc=("sV", 1))
                epsv = cnt["sV"]
                emit("vector", lambda e, glo=glo, nh=nh: e.reciprocal(
                    rec_sb[:, glo:glo + nh], den_sb[:, glo:glo + nh]),
                     waits=[("sV", epsv)], inc=("sV", 1))
                recv = cnt["sV"]
                for h2 in range(glo, h + 1):
                    osl2, _, ocol2 = out_slot(h2)
                    emit("vector", lambda e, h2=h2, osl2=osl2, ocol2=ocol2: e.tensor_scalar(
                        attn[:, 64 * h2:64 * h2 + 64], osl2[:, ocol2:ocol2 + 64],
                        rec_sb[:, h2:h2 + 1], None, ALU.mult),
                         waits=[("sV", recv)] if h2 == glo else (),
                         inc=("sV", 1) if h2 == h else None)
                reads(gname, "sV")

            DEPTH_U = 3
            DEPTH_STT = 2
            qfm_l, tu_l, poly_l, stt_l = {}, {}, {}, {}
            phiT_l, phiev_l = {}, {}

            def advance(h):
                if h < H and h not in poly_l:
                    qfm_l[h] = emit_qfm(h)
                    tu_l[h] = emit_u(h)
                    poly_l[h] = emit_poly(h, tu_l[h])

            def advance_stt(h):
                if h < H and h not in stt_l:
                    stt_l[h] = emit_stt(h, poly_l[h])

            for h0 in range(min(DEPTH_U, H)):
                advance(h0)
            for h0 in range(min(DEPTH_STT, H)):
                advance_stt(h0)
            for h in range(H):
                if h not in phiev_l:
                    phiT_l[h] = emit_phiT(h, qfm_l[h])
                    phiev_l[h] = emit_phiev(h, phiT_l[h])
                if dump_cb and h == 0:
                    dump_cb("pu0", pu_sb[:, 0:128])
                    dump_cb("s1h1", pu_s1t[:, 0:128])
                    dump_cb("phiq0", phiq[:, 0:DPHI])
                stv = emit_qkv_inter_state(h, stt_l[h], phiev_l[h])
                if dump_cb and h == 0:
                    dump_cb("phifm0", phiqfm[:, 0:384])
                if h + 1 < H:
                    phiT_l[h + 1] = emit_phiT(h + 1, qfm_l[h + 1])
                    phiev_l[h + 1] = emit_phiev(h + 1, phiT_l[h + 1])
                advance(h + DEPTH_U)
                advance_stt(h + DEPTH_STT)
                if h in (6, 13, 15):
                    emit_div_group(h, stv)
            divV = cnt["sV"]
            theads = cnt["sT"]
            res["v"] = [("sT", theads)]
            res["qk"] = [("sT", theads), ("sG", cnt["sG"]), ("sV", cnt["sV"])]
            res["qksc"] = [("sG", cnt["sG"]), ("sV", cnt["sV"])]
            res["phikraw"] = [("sG", cnt["sG"]), ("sT", theads)]

            if dump_cb:
                emit("vector", cp(attnfm.bitcast(F32)[:, 0:455], psA[:, 0:455]),
                     waits=[("sT", theads)] + freew("attnfm"), inc=("sV", 1))
                res["attnfm"] = [("sV", cnt["sV"])]
                dump_cb("o0", attnfm.bitcast(F32)[:, 0:455])
                dump_cb("den0", den_sb[:, :])
                dump_cb("rec0", rec_sb[:, :])

            def tail_a(divV=divV):
                w = [("sV", divV)] + freew("psT")
                for dc in range(8):
                    emit("tensor", tr(psT[:, 128 * dc:128 * dc + 128],
                                      attn[:, 128 * dc:128 * dc + 128]),
                         waits=w if dc == 0 else (), inc=("sT", 1) if dc == 7 else None)
                tat = cnt["sT"]
                reads("attn", "sT")
                emit("scalar", lambda e: e.copy(attnfm[:, :], psT[:, :]),
                     waits=[("sT", tat)] + freew("attnfm"), inc=("sA", 1))
                reads("psT", "sA")

            def tail_b(ci=ci):
                atev = cnt["sA"]
                w = [("sA", atev)] + freew("psD", "psE")
                for dc in range(8):
                    emit("tensor", mm(psE[:, 0:512], attnfm[:, 128 * dc:128 * dc + 128],
                                      wo_sb[:, 1024 * dc:1024 * dc + 512], dc == 0, dc == 7),
                         waits=w if dc == 0 else ())
                    emit("tensor", mm(psD[:, 0:512], attnfm[:, 128 * dc:128 * dc + 128],
                                      wo_sb[:, 1024 * dc + 512:1024 * dc + 1024], dc == 0, dc == 7),
                         inc=("sT", 1) if dc == 7 else None)
                two = cnt["sT"]
                res["attnfm"] = [("sT", two)]
                emit("vector", lambda e: e.tensor_tensor(
                    stream[:, DM * ci:DM * ci + 512], psE[:, 0:512],
                    stream[:, DM * ci:DM * ci + 512], ALU.add),
                     waits=[("sT", two), ("dG", x_ld)])
                emit("vector", lambda e: e.tensor_tensor(
                    stream[:, DM * ci + 512:DM * ci + 1024], psD[:, 0:512],
                    stream[:, DM * ci + 512:DM * ci + 1024], ALU.add), inc=("sV", 1))
                rv = cnt["sV"]
                reads("psD", "sV")
                reads("psE", "sV")
                res["wo_resid"] = [("sV", rv)]
                return rv
            return tail_a, tail_b

        def load_xfm(src_t, ci, slot):
            emit("gpsimd", dma(
                xfm[:, DM * slot:DM * slot + DM].rearrange("p (dc t) -> p dc t", dc=8),
                src_t.rearrange("(dc p) t -> p dc t", p=128)[:, :, C * ci:C * ci + C]),
                 waits=freew(f"xfm{slot}"), inc=("dG", 16))
            return cnt["dG"]

        for ci in range(npre):
            xld = load_xfm(xpt_h, ci, ci % 2)
            scan_chunk(ci, ci % 2, 1, True, xld, qk_tok=pk_tok, ksc=pk_sc)
            res[f"xfm{ci % 2}"] = [("sT", cnt["sT"])]

        if dumps:
            dump("state_pre", state[:, :], PACK * H)

        def ln_stats(ci, resid_sv):
            w = [("sV", resid_sv)]
            emit("scalar", lambda e, ci=ci: e.activation(
                attn[:, :], stream[:, DM * ci:DM * ci + DM],
                AF.Square, accum_out=ln_sb[:, 16 + ci:17 + ci]),
                 waits=w, inc=("sA", 1))
            reads("attn", "sA")
            emit("vector", lambda e, ci=ci: e.reduce_sum(
                ln_sb[:, ci:ci + 1], stream[:, DM * ci:DM * ci + DM], AX),
                 waits=w, inc=("sV", 1))


        pend = [None, None]
        no_pipe = False
        for ci in range(nch):
            emit("gpsimd", dma(stream[:, DM * ci:DM * ci + DM], x_h[C * ci:C * ci + C, :]),
                 inc=("dG", 16))
            x_ld = cnt["dG"]
            xld = load_xfm(xt_h, ci, ci % 2)
            if pend[0] is not None:
                pend[0]()
                pend[0] = None
            def tailb_only(ci=ci):
                if pend[1] is not None:
                    rv = pend[1]()
                    ln_stats(ci - 1, rv)
                    if ci - 1 == 1:
                        dump("stream1", stream[:, DM:2 * DM], DM)
                    pend[1] = None
            tails = scan_chunk(ci, ci % 2, 0, False, xld,
                               dump_cb=(dump if ci == 0 else None),
                               after_proj=tailb_only,
                               after_pads=None, x_ld=x_ld)
            pend = list(tails)
            res[f"xfm{ci % 2}"] = [("sT", cnt["sT"])]
            if ci == 0:
                dump("qk0", qk_tok[:, :], 512)
                dump("v0", v_sb[:, :], 1040)
                dump("phik0", phik[:, :], DPHI * H)
                dump("attn0", attn[:, :], DM)
                dump("state0", state[:, :], PACK * H)
        pend[0]()
        rv = pend[1]()
        ln_stats(nch - 1, rv)

        def ln_finish():
            sAs, sVs = cnt["sA"], cnt["sV"]
            emit("vector", lambda e: e.tensor_scalar(
                ln_sb[:, 32:48], ln_sb[:, 0:16], 1.0 / DM, None, ALU.mult),
                 waits=[("sA", sAs), ("sV", sVs)], inc=("sV", 1))
            meanv = cnt["sV"]
            emit("vector", lambda e: e.tensor_tensor(
                den_sb[:, 0:16], ln_sb[:, 32:48], ln_sb[:, 32:48], ALU.mult),
                 waits=[("sV", meanv)], inc=("sV", 1))
            msqv = cnt["sV"]
            emit("vector", lambda e: e.scalar_tensor_tensor(
                ln_sb[:, 64:80], ln_sb[:, 16:32], 1.0 / DM, den_sb[:, 0:16],
                ALU.mult, ALU.subtract), waits=[("sV", msqv)], inc=("sV", 1))
            vv = cnt["sV"]
            emit("scalar", lambda e: e.activation(rec_sb[:, 0:16], ln_sb[:, 64:80],
                                                  AF.Ln, bias=cst_sb[:, 3:4]), waits=[("sV", vv)])
            emit("scalar", lambda e: e.activation(ln_sb[:, 48:64], rec_sb[:, 0:16],
                                                  AF.Exp, scale=cst_sb[:, 4:5]), inc=("sA", 1))
            sa = cnt["sA"]
            emit("vector", lambda e: e.scalar_tensor_tensor(
                ln_sb[:, 64:80], ln_sb[:, 32:48], -1.0, ln_sb[:, 48:64],
                ALU.mult, ALU.mult), waits=[("sA", sa)], inc=("sV", 1))
            return cnt["sV"]

        vneg = ln_finish()
        if dumps:
            dump("lnsb", ln_sb[:, :], 96)
        for ci in range(nch):
            emit("vector", lambda e, ci=ci: e.tensor_scalar(
                stream[:, DM * ci:DM * ci + DM], stream[:, DM * ci:DM * ci + DM],
                ln_sb[:, 48 + ci:49 + ci], ln_sb[:, 64 + ci:65 + ci],
                ALU.mult, ALU.add),
                 waits=[("sV", vneg)] if ci == 0 else (), inc=("sV", 1))
            res[f"norm{ci}"] = [("sV", cnt["sV"])]
        if dumps:
            dump("y1", stream[:, 0:DM], DM)

        emit("gpsimd", dma(wd_sb.rearrange("p (hc j) -> p hc j", hc=24),
                           wd_h.rearrange("(hc p) j -> p hc j", p=128)),
             waits=[("sT", cnt["sT"])], inc=("dG", 16))
        W_WD = cnt["dG"]

        for ci in range(nch):
            w = res.pop(f"norm{ci}") + freew("psT")
            for dc in range(8):
                emit("tensor", tr(psT[:, 128 * dc:128 * dc + 128],
                                  stream[:, DM * ci + 128 * dc:DM * ci + 128 * dc + 128]),
                     waits=w if dc == 0 else (), inc=("sT", 1) if dc == 7 else None)
                w = ()
            ty = cnt["sT"]
            emit("vector", cp(xfm[:, 0:DM], psT[:, :]),
                 waits=[("sT", ty)] + freew("xfm0") + freew("xfm1"), inc=("sV", 1))
            reads("psT", "sV")
            y1ev = cnt["sV"]

            hm = []
            for half in range(2):
                w = [("sV", y1ev), ("dG", W_GU)] + freew(
                    "psA", "psB", "psC", "psD", "psE", "psF")
                for dc in range(8):
                    lhsT = xfm[:, 128 * dc:128 * dc + 128]
                    for k, ps in enumerate((psA, psB, psC)):
                        emit("tensor", mm(ps[:, 0:512], lhsT,
                                          wg_sb[:, 3072 * dc + 1536 * half + 512 * k:
                                                3072 * dc + 1536 * half + 512 * k + 512],
                                          dc == 0, dc == 7),
                             waits=w if (dc == 0 and k == 0) else ())
                        w = ()
                    for k, ps in enumerate((psD, psE, psF)):
                        emit("tensor", mm(ps[:, 0:512], lhsT,
                                          wu_sb[:, 3072 * dc + 1536 * half + 512 * k:
                                                3072 * dc + 1536 * half + 512 * k + 512],
                                          dc == 0, dc == 7),
                             inc=("sT", 1) if (dc == 7 and k == 2) else None)
                tgu = cnt["sT"]
                for k, (psg, psu2) in enumerate(((psA, psD), (psB, psE), (psC, psF))):
                    emit("scalar", lambda e, k=k, psg=psg: e.activation(
                        silu_sb[:, 512 * k:512 * k + 512], psg[:, 0:512], AF.Silu,
                        scale=cst_sb[:, 5:6]),
                         waits=[("sT", tgu)] + (freew("silu") if k == 0 else []),
                         inc=("sA", 1))
                    sk = cnt["sA"]
                    gname = ("psA", "psB", "psC")[k]
                    reads(gname, "sA")
                    emit("vector", lambda e, k=k, half=half, psu2=psu2: e.tensor_tensor(
                        h_tok[:, 1536 * half + 512 * k:1536 * half + 512 * k + 512],
                        silu_sb[:, 512 * k:512 * k + 512], psu2[:, 0:512], ALU.mult),
                         waits=[("sA", sk)] + (freew("htok") if (half == 0 and k == 0) else []),
                         inc=("sV", 1))
                    uname = ("psD", "psE", "psF")[k]
                    reads(uname, "sV")
                hm.append(cnt["sV"])
                res["silu"] = [("sV", cnt["sV"])]

            hevs = []
            for grp in range(3):
                phb, phn = (psT, "psT") if grp % 2 == 0 else (psU, "psU")
                w = [("sV", hm[0] if grp == 0 else hm[1])] + freew(phn)
                for i in range(8):
                    hc = 8 * grp + i
                    emit("tensor", tr(phb[:, 128 * i:128 * i + 128],
                                      h_tok[:, 128 * hc:128 * hc + 128]),
                         waits=w if i == 0 else (), inc=("sT", 1) if i == 7 else None)
                    w = ()
                tg = cnt["sT"]
                emit("scalar", lambda e, phb=phb, grp=grp: e.copy(
                    hfm[:, 1024 * grp:1024 * grp + 1024], phb[:, :]),
                     waits=[("sT", tg)] + (freew("hfm") if grp == 0 else []),
                     inc=("sA", 1))
                reads(phn, "sA")
                hevs.append(cnt["sA"])

            w = [("dG", W_WD)] + freew("psA", "psB")
            for hc in range(24):
                if hc % 8 == 0:
                    w = w + [("sA", hevs[hc // 8])]
                emit("tensor", mm(psA[:, 0:512], hfm[:, 128 * hc:128 * hc + 128],
                                  wd_sb[:, 1024 * hc:1024 * hc + 512], hc == 0, hc == 23),
                     waits=w)
                w = []
                emit("tensor", mm(psB[:, 0:512], hfm[:, 128 * hc:128 * hc + 128],
                                  wd_sb[:, 1024 * hc + 512:1024 * hc + 1024], hc == 0, hc == 23),
                     inc=("sT", 1) if hc == 23 else None)
            td = cnt["sT"]
            res["hfm"] = [("sT", td)]
            res["htok"] = [("sT", td)]
            emit("vector", lambda e, ci=ci: e.scalar_tensor_tensor(
                stream[:, DM * ci:DM * ci + 512], psA[:, 0:512], FFSC,
                stream[:, DM * ci:DM * ci + 512], ALU.mult, ALU.add),
                 waits=[("sT", td)])
            emit("vector", lambda e, ci=ci: e.scalar_tensor_tensor(
                stream[:, DM * ci + 512:DM * ci + 1024], psB[:, 0:512], FFSC,
                stream[:, DM * ci + 512:DM * ci + 1024], ALU.mult, ALU.add),
                 inc=("sV", 1))
            zr = cnt["sV"]
            res["psA"] = [("sV", zr)]
            res["psB"] = [("sV", zr)]
            res[f"z{ci}"] = [("sV", zr)]
            ln_stats(ci, zr)

        if dumps:
            dump("z0", stream[:, 0:DM], DM)

        vneg2 = ln_finish()
        for ci in range(nch):
            stg = ostage[:, 1024 * (ci % 2):1024 * (ci % 2) + 1024]
            w = [("sV", vneg2)] if ci == 0 else []
            if ci >= 2:
                w.append(("dS", 16 * (ci - 1)))
            emit("vector", lambda e, ci=ci, stg=stg: e.tensor_scalar(
                stg, stream[:, DM * ci:DM * ci + DM],
                ln_sb[:, 48 + ci:49 + ci], ln_sb[:, 64 + ci:65 + ci],
                ALU.mult, ALU.add),
                 waits=w, inc=("sV", 1))
            nv = cnt["sV"]
            emit("scalar", lambda e, ci=ci, stg=stg: e.dma_start(
                out=y_h[C * ci:C * ci + C, :], in_=stg),
                 waits=[("sV", nv)], inc=("dS", 16))

        final_dS = cnt["dS"]
        final_dD = cnt["dD"]
        emit("scalar", lambda e: e.nop(), waits=[("dS", final_dS)])
        emit("sync", lambda e: e.nop(),
             waits=[("dS", final_dS)] + ([("dD", final_dD)] if final_dD else []))
        emit("gpsimd", lambda e: e.nop(), waits=[("dS", final_dS)])

        with nc.Block() as block:
            def runner(eng_name):
                def go(e):
                    for fn, waits, inc in streams[eng_name]:
                        for s, v in waits:
                            e.wait_ge(sems[s], v)
                        i = fn(e)
                        if inc:
                            i.then_inc(sems[inc[0]], inc[1])
                return go
            block.sync(runner("sync"))
            block.gpsimd(runner("gpsimd"))
            block.vector(runner("vector"))
            block.scalar(runner("scalar"))
            block.tensor(runner("tensor"))

    return nc


def make_inputs_for_core(core, x_full, w):
    b, half = core // 2, core % 2
    x_main = np.ascontiguousarray(x_full[b, half * T:(half + 1) * T])
    x_t = np.ascontiguousarray(x_main.T)
    if half == 1:
        x_pt = np.ascontiguousarray(x_full[b, 0:T].T)
        pmask = 1.0
    else:
        x_pt = np.zeros((DM, T), np.float32)
        pmask = 0.0
    cst = np.zeros((128, 8), np.float32)
    cst[:, 0] = 1.0
    cst[:, 1] = pmask
    cst[:, 2] = 4.0
    cst[:, 3] = LNEPS
    cst[:, 4] = -0.5
    cst[:, 5] = 1.0 / 127.0
    m = {"x": x_main, "xt": x_t, "xpt": x_pt, "cst": cst}
    m.update(w)
    return m


def shared_weights(inputs):
    ident = np.eye(128, dtype=np.float32)
    m32 = (np.triu(np.ones((128, 128), np.float32)) / 32.0).astype(np.float32)
    return {
        "wq": np.ascontiguousarray(inputs["L0_attn_Wq"], np.float32),
        "wk": np.ascontiguousarray(inputs["L0_attn_Wk"], np.float32),
        "wv": np.ascontiguousarray(inputs["L0_attn_Wv"], np.float32),
        "wo": np.ascontiguousarray(inputs["L0_attn_Wo"], np.float32),
        "wg": np.ascontiguousarray(inputs["L0_ffn_w_gate"].T),
        "wu": np.ascontiguousarray(inputs["L0_ffn_w_up"].T),
        "wd": np.ascontiguousarray(inputs["L0_ffn_w_down"].T),
        "ident": ident,
        "m32": m32,
    }


def kernel(L0_attn_hidden_states, L0_attn_Wq, L0_attn_Wk, L0_attn_Wv,
           L0_attn_Wo, L0_ffn_w_gate, L0_ffn_w_up, L0_ffn_w_down):
    inputs = {
        "L0_attn_hidden_states": np.asarray(L0_attn_hidden_states),
        "L0_attn_Wq": np.asarray(L0_attn_Wq), "L0_attn_Wk": np.asarray(L0_attn_Wk),
        "L0_attn_Wv": np.asarray(L0_attn_Wv), "L0_attn_Wo": np.asarray(L0_attn_Wo),
        "L0_ffn_w_gate": np.asarray(L0_ffn_w_gate),
        "L0_ffn_w_up": np.asarray(L0_ffn_w_up),
        "L0_ffn_w_down": np.asarray(L0_ffn_w_down),
    }
    x = inputs["L0_attn_hidden_states"]
    w = shared_weights(inputs)
    nc = build_nc()
    in_maps = [make_inputs_for_core(c, x, w) for c in range(8)]
    res = run_bass_kernel_spmd(nc, in_maps, core_ids=list(range(8)))
    out = np.empty((4, 4096, DM), np.float32)
    for c in range(8):
        b, half = c // 2, c % 2
        out[b, half * T:(half + 1) * T] = res.results[c]["y"]
    return out


# version 09
# speedup vs baseline: 1.3258x; 1.0625x over previous
import contextlib
import numpy as np
import concourse.bass as bass
import concourse.mybir as mybir
from concourse.bass_utils import run_bass_kernel_spmd

F32 = mybir.dt.float32
BF16 = mybir.dt.bfloat16
I32 = mybir.dt.int32
AF = mybir.ActivationFunctionType
ALU = mybir.AluOpType
AX = mybir.AxisListType.X

C = 128
H = 16
DM = 1024
DFF = 3072
T = 2048
DPHI = 273
PACK = 195
EPS = 1e-9
LNEPS = 1e-5
KQSC = 1.0 / 32.0
KRSC = 0.25
FFSC = float(1.0 / (127.0 * 127.0))


def build_nc(nch=T // C, npre=T // C, dumps=(), pus1_sbuf=False, late_div=False,
             attnfm_sep=False, rec_sep=False, phiq3=True):
    nc = bass.Bass("TRN2", target_bir_lowering=False)
    x_h = nc.dram_tensor("x", [T, DM], F32, kind="ExternalInput").ap()
    xt_h = nc.dram_tensor("xt", [DM, T], F32, kind="ExternalInput").ap()
    xpt_h = nc.dram_tensor("xpt", [DM, T], F32, kind="ExternalInput").ap()
    cst_h = nc.dram_tensor("cst", [128, 8], F32, kind="ExternalInput").ap()
    wq_h = nc.dram_tensor("wq", [DM, 256], F32, kind="ExternalInput").ap()
    wk_h = nc.dram_tensor("wk", [DM, 256], F32, kind="ExternalInput").ap()
    wv_h = nc.dram_tensor("wv", [DM, DM], F32, kind="ExternalInput").ap()
    wo_h = nc.dram_tensor("wo", [DM, DM], F32, kind="ExternalInput").ap()
    wg_h = nc.dram_tensor("wg", [DM, DFF], I32, kind="ExternalInput").ap()
    wu_h = nc.dram_tensor("wu", [DM, DFF], I32, kind="ExternalInput").ap()
    wd_h = nc.dram_tensor("wd", [DFF, DM], I32, kind="ExternalInput").ap()
    id_h = nc.dram_tensor("ident", [128, 128], F32, kind="ExternalInput").ap()
    m32_h = nc.dram_tensor("m32", [128, 128], F32, kind="ExternalInput").ap()
    y_h = nc.dram_tensor("y", [T, DM], F32, kind="ExternalOutput").ap()
    dump_h = {}
    for tag, shape in dumps:
        dump_h[tag] = nc.dram_tensor("dbg_" + tag, shape, F32, kind="ExternalOutput").ap()

    streams = {e: [] for e in ("sync", "gpsimd", "vector", "scalar", "tensor")}
    cnt = {s: 0 for s in ("dG", "dS", "dD", "sT", "sV", "sA", "sG")}
    res = {}

    pending_dump = {}

    def emit(eng, fn, waits=(), inc=None):
        wmax = {}
        pd = pending_dump.pop(eng, None)
        waits = list(waits) + ([("dD", pd)] if pd else [])
        for s, v in waits:
            assert v <= cnt[s], f"wait {s}>={v}, emitted {cnt[s]}"
            if v > 0:
                wmax[s] = max(wmax.get(s, 0), v)
        if inc:
            s, n = inc
            cnt[s] += n
        streams[eng].append((fn, tuple(wmax.items()), inc))

    def freew(*names):
        w = []
        for n in names:
            w += res.pop(n, [])
        return w

    def reads(name, sem):
        res.setdefault(name, []).append((sem, cnt[sem]))

    with contextlib.ExitStack() as stk:
        E = stk.enter_context
        stream = E(nc.sbuf_tensor("stream", [128, nch * DM], BF16))
        xfm = E(nc.sbuf_tensor("xfm", [128, 2 * DM], BF16))
        warena = E(nc.sbuf_tensor("warena", [128, 24576], BF16))
        wg_sb = E(nc.sbuf_tensor("wg_sb", [128, 24576], BF16))
        wu_sb = E(nc.sbuf_tensor("wu_sb", [128, 24576], BF16))
        scr = E(nc.sbuf_tensor("scr", [128, 13188], BF16))
        m32_sb = E(nc.sbuf_tensor("m32_sb", [128, 128], BF16))
        id_sb = E(nc.sbuf_tensor("id_sb", [128, 128], BF16))
        cst_sb = E(nc.sbuf_tensor("cst_sb", [128, 8], F32))
        den_sb = E(nc.sbuf_tensor("den_sb", [128, 16], F32))
        ln_sb = E(nc.sbuf_tensor("ln_sb", [128, 80], F32))
        pu_s1t = E(nc.sbuf_tensor("pu_s1t", [128, 512], BF16))
        attnfm_t = E(nc.sbuf_tensor("attnfm_t", [128, 1024], BF16)) if attnfm_sep else None
        rec_t = E(nc.sbuf_tensor("rec_t", [128, 16], F32)) if rec_sep else None

        rec_sb = ln_sb[:, 64:80]
        if rec_sep:
            rec_sb = rec_t

        xps = scr[:, 0:2048]
        qk_tok = scr[:, 0:512]
        ksc = scr[:, 512:768]
        qk_pad = scr[:, 768:1792]
        pu_sb = scr[:, 1792:2048]
        pk_tok = scr[:, 12164:12676]
        pk_sc = scr[:, 12676:12932]
        phik = scr[:, 2048:6416]
        phiq = scr[:, 6416:7235]
        NQSL = 3
        phiqfm = scr[:, 7235:8003]
        v_sb = scr[:, 8003:9043]
        state = scr[:, 9043:12163]
        attn = scr[:, 12164:13188]
        attnfm = scr[:, 768:1792]
        if attnfm_sep:
            attnfm = attnfm_t
        h_tok = scr[:, 0:3072]
        hfm = scr[:, 3072:6144]
        silu_sb = scr[:, 6144:7680]
        ostage = scr[:, 7680:11776].bitcast(F32)

        psA = E(nc.psum_tensor("psA", [128, 512], F32))
        psB = E(nc.psum_tensor("psB", [128, 512], F32))
        psC = E(nc.psum_tensor("psC", [128, 512], F32))
        psD = E(nc.psum_tensor("psD", [128, 512], F32))
        psE = E(nc.psum_tensor("psE", [128, 512], F32))
        psF = E(nc.psum_tensor("psF", [128, 512], F32))
        psT = E(nc.psum_tensor("psT", [128, 1024], BF16))
        psU = E(nc.psum_tensor("psU", [128, 1024], BF16))

        sems = {s: E(nc.semaphore(s)) for s in cnt}

        wq_sb = warena[:, 0:2048]
        wk_sb = warena[:, 2048:4096]
        wv_sb = warena[:, 4096:12288]
        wo_sb = warena[:, 12288:20480]
        wd_sb = warena[:, 0:24576]

        def dma(dst, src):
            return lambda e, d=dst, s=src: e.dma_start(out=d, in_=s)

        emit("gpsimd", dma(id_sb[:, :], id_h[:, :]), inc=("dG", 16))

        emit("gpsimd", dma(wq_sb.rearrange("p (d j) -> p d j", d=8),
                           wq_h.rearrange("(d p) j -> p d j", p=128)), inc=("dG", 16))
        emit("gpsimd", dma(wk_sb.rearrange("p (d j) -> p d j", d=8),
                           wk_h.rearrange("(d p) j -> p d j", p=128)), inc=("dG", 16))
        emit("gpsimd", dma(wv_sb.rearrange("p (d j) -> p d j", d=8),
                           wv_h.rearrange("(d p) j -> p d j", p=128)), inc=("dG", 16))
        emit("gpsimd", dma(wo_sb.rearrange("p (d j) -> p d j", d=8),
                           wo_h.rearrange("(d p) j -> p d j", p=128)), inc=("dG", 16))
        W_ATTN = cnt["dG"]
        emit("gpsimd", dma(m32_sb[:, :], m32_h[:, :]), inc=("dG", 16))
        emit("sync", dma(cst_sb[:, :], cst_h[:, :]), inc=("dG", 16))
        W_CONST = cnt["dG"]

        emit("vector", lambda e: e.memset(
            phik.rearrange("p (h d) -> p h d", h=16)[:, :, 0:1], 1.0))
        emit("vector", lambda e: e.memset(
            phiq.rearrange("p (s d) -> p s d", s=3)[:, :, 0:1], 1.0))
        emit("vector", lambda e: e.memset(state[:, :], 0.0), inc=("sV", 1))
        res["state"] = [("sV", cnt["sV"])]

        emit("gpsimd", dma(wg_sb.rearrange("p (d j) -> p d j", d=8),
                           wg_h.rearrange("(d p) j -> p d j", p=128)), inc=("dG", 16))
        emit("gpsimd", dma(wu_sb.rearrange("p (d j) -> p d j", d=8),
                           wu_h.rearrange("(d p) j -> p d j", p=128)), inc=("dG", 16))
        W_GU = cnt["dG"]

        def tr(out, src, tp=None):
            return lambda e, o=out, s=src, t=tp: e.transpose(o, s, id_sb[:, :],
                                                             tile_position=t)

        def mm(out, lhsT, rhs, start, stop, tp=None):
            return lambda e, o=out, l=lhsT, r=rhs, a=start, b=stop, t=tp: e.matmul(
                o, l, r, start=a, stop=b, tile_position=t)

        def cp(eng_dst, src):
            return lambda e, d=eng_dst, s=src: e.tensor_copy(d, s)

        def out_slot(h):
            if h < 7:
                return psA, "psA", 65 * h
            if h < 14:
                return psB, "psB", 65 * (h - 7)
            return psC, "psC", 130 + 65 * (h - 14)

        addsnap = {h: ("sV", 0) for h in range(H)}

        def dump(tag, src_bf16, n_f32_cols=None):
            if tag in dump_h:
                w = [(s, cnt[s]) for s in ("sT", "sV", "sA", "sG")]
                emit("gpsimd", dma(dump_h[tag][:, :], src_bf16), waits=w, inc=("dD", 16))
                for eng in streams:
                    pending_dump[eng] = cnt["dD"]

        def scan_chunk(ci, xslot, vcol, kv_only, xld, qk_tok=qk_tok, ksc=ksc,
                       dump_cb=None, after_proj=None, after_pads=None, x_ld=0):
            xf = xfm[:, DM * xslot:DM * xslot + DM]
            w = [("dG", xld), ("dG", W_CONST), ("dG", W_ATTN)] + freew(
                "psA", "psB", "psC")
            for dc in range(8):
                lhsT = xf[:, 128 * dc:128 * dc + 128]
                if not kv_only:
                    emit("tensor", mm(psA[:, 0:256], lhsT,
                                      wq_sb[:, 256 * dc:256 * dc + 256], dc == 0, dc == 7),
                         waits=w if dc == 0 else ())
                    w = ()
                emit("tensor", mm(psA[:, 256:512], lhsT,
                                  wk_sb[:, 256 * dc:256 * dc + 256],
                                  (dc == 0) and kv_only, dc == 7),
                     waits=w if dc == 0 else ())
                w = ()
                emit("tensor", mm(psB[:, 0:512], lhsT,
                                  wv_sb[:, 1024 * dc:1024 * dc + 512], dc == 0, dc == 7))
                emit("tensor", mm(psC[:, 0:512], lhsT,
                                  wv_sb[:, 1024 * dc + 512:1024 * dc + 1024], dc == 0, dc == 7),
                     inc=("sT", 1) if dc == 7 else None)
            tproj = cnt["sT"]
            last_proj_t[0] = tproj

            lo = 256 if kv_only else 0
            emit("vector", cp(qk_tok[:, lo:512], psA[:, lo:512]),
                 waits=[("sT", tproj)] + freew("qk"))
            emit("vector", lambda e: e.tensor_scalar(
                ksc[:, 0:256], qk_tok[:, 256:512], KQSC, None, ALU.mult),
                 waits=freew("qksc"))
            emit("vector", lambda e: e.tensor_scalar(
                phik.rearrange("p (h d) -> p h d", h=16)[:, :, 1:17],
                qk_tok[:, 256:512].rearrange("p (h f) -> p h f", h=16),
                KRSC, None, ALU.mult), waits=freew("phikraw"), inc=("sV", 1))
            kev = cnt["sV"]
            emit("vector", cp(v_sb.rearrange("p (h v) -> p h v", h=16)[:, 0:8, 0:64],
                              psB[:, 0:512].rearrange("p (h v) -> p h v", h=8)),
                 waits=freew("v"))
            emit("vector", cp(v_sb.rearrange("p (h v) -> p h v", h=16)[:, 8:16, 0:64],
                              psC[:, 0:512].rearrange("p (h v) -> p h v", h=8)))
            emit("vector", cp(v_sb.rearrange("p (h v) -> p h v", h=16)[:, :, 64:65],
                              cst_sb[:, vcol:vcol + 1].unsqueeze(1).broadcast_to([128, 16, 1])),
                 inc=("sV", 1))
            projev = cnt["sV"]
            reads("psA", "sV")
            reads("psB", "sV")
            reads("psC", "sV")

            for h in range(H):
                w = [("sV", kev)] + freew(f"phik{h}")
                def kq(e, h=h, qk_tok=qk_tok, ksc=ksc):
                    o = phik[:, DPHI * h + 17:DPHI * h + 273].rearrange(
                        "p (i j) -> p i j", i=16)
                    i0 = ksc[:, 16 * h:16 * h + 16].unsqueeze(2).broadcast_to([128, 16, 16])
                    i1 = qk_tok[:, 256 + 16 * h:256 + 16 * h + 16].unsqueeze(1).broadcast_to([128, 16, 16])
                    return e.tensor_tensor(o, i0, i1, ALU.mult)
                eng = "vector" if (kv_only and h % 3 == 2) else "gpsimd"
                sem = "sV" if eng == "vector" else "sG"
                emit(eng, kq, waits=w, inc=(sem, 1))
                res[f"kfm{h}"] = [(sem, cnt[sem])]

            if kv_only:
                for h in range(H):
                    half = 195 * (h % 2)
                    w = res[f"kfm{h}"] + (freew("psE") if h % 2 == 0 else [])
                    if h == 0:
                        w.append(("sV", projev))
                    emit("tensor", mm(psE[:, half:half + 65], phik[:, DPHI * h:DPHI * h + 128],
                                      v_sb[:, 65 * h:65 * h + 65], True, True), waits=w)
                    emit("tensor", mm(psE[:, half + 65:half + 130],
                                      phik[:, DPHI * h + 128:DPHI * h + 256],
                                      v_sb[:, 65 * h:65 * h + 65], True, True))
                    emit("tensor", mm(psE[0:17, half + 130:half + 195],
                                      phik[:, DPHI * h + 256:DPHI * h + 273],
                                      v_sb[:, 65 * h:65 * h + 65], True, True),
                         inc=("sT", 1))
                    stv = cnt["sT"]
                    res[f"phik{h}"] = [("sT", stv)]
                    if h % 2 == 1:
                        c0p = PACK * (h - 1)
                        emit("vector", lambda e, c0p=c0p: e.tensor_tensor(
                            state[:, c0p:c0p + 2 * PACK], psE[:, 0:2 * PACK],
                            state[:, c0p:c0p + 2 * PACK], ALU.add),
                             waits=[("sT", stv)] + freew("state")
                                   + [addsnap[h - 1], addsnap[h]], inc=("sV", 1))
                        addsnap[h - 1] = addsnap[h] = ("sV", cnt["sV"])
                        reads("psE", "sV")
                res["v"] = [("sT", cnt["sT"])]
                res["qk"] = [("sG", cnt["sG"]), ("sT", cnt["sT"])]
                res["qksc"] = [("sG", cnt["sG"])]
                res["phikraw"] = [("sG", cnt["sG"]), ("sT", cnt["sT"])]
                return

            if after_proj:
                after_proj()
            w = [("sV", projev)] + freew("psT")
            for h in range(H):
                j, g = h % 4, h // 4
                emit("tensor", tr(psT[32 * j:32 * j + 16, 128 * g:128 * g + 128],
                                  qk_tok[:, 16 * h:16 * h + 16], (0, 32 * j)),
                     waits=w if h == 0 else ())
                w = ()
            for h in range(H):
                j, g = h % 4, h // 4
                emit("tensor", tr(psT[32 * j:32 * j + 16, 512 + 128 * g:512 + 128 * g + 128],
                                  qk_tok[:, 256 + 16 * h:256 + 16 * h + 16], (0, 32 * j)),
                     inc=("sT", 1) if h == H - 1 else None)
            tpads = cnt["sT"]
            emit("vector", cp(qk_pad[:, :], psT[:, :]),
                 waits=[("sT", tpads)] + freew("attnfm"), inc=("sV", 1))
            padsev = cnt["sV"]
            reads("psT", "sV")
            if dump_cb:
                dump_cb("pads0", qk_pad[:, :])
            if after_pads:
                after_pads()

            def u_bank(h):
                return (psD, "psD") if h % 2 == 0 else (psF, "psF")

            def emit_u(h):
                j, g = h % 4, h // 4
                ub, ubn = u_bank(h)
                col = 128 * ((h // 2) % 2)
                w = [("sV", padsev)] + freew(f"uslot{h % 4}")
                if h < 2:
                    w += freew("psD")
                elif h < 4:
                    w += freew("psF")
                else:
                    pass
                if h >= 2:
                    w += [("sA", poly_l[h - 2])]
                emit("tensor", mm(ub[:, col:col + 128],
                                  qk_pad[32 * j:32 * j + 16, 512 + 128 * g:512 + 128 * g + 128],
                                  qk_pad[32 * j:32 * j + 16, 128 * g:128 * g + 128],
                                  True, True, (32 * j, 0)), waits=w, inc=("sT", 1))
                return cnt["sT"]

            def emit_qfm(h):
                sl = phiq[:, DPHI * (h % NQSL):DPHI * (h % NQSL) + DPHI]
                slot_free = freew(f"phiq{h % NQSL}")
                emit("scalar", lambda e, h=h, sl=sl: e.copy(
                    sl[:, 1:17], qk_tok[:, 16 * h:16 * h + 16]),
                     waits=slot_free + [("sV", kev)], inc=("sA", 1))
                wr = [("sA", cnt["sA"])]
                def qq(e, h=h, sl=sl):
                    o = sl[:, 17:273].rearrange("p (i j) -> p i j", i=16)
                    i0 = qk_tok[:, 16 * h:16 * h + 16].unsqueeze(2).broadcast_to([128, 16, 16])
                    i1 = qk_tok[:, 16 * h:16 * h + 16].unsqueeze(1).broadcast_to([128, 16, 16])
                    return e.tensor_tensor(o, i0, i1, ALU.mult)
                eng = "vector" if h % 2 == 0 else "gpsimd"
                sem = "sV" if eng == "vector" else "sG"
                emit(eng, qq, waits=slot_free + [("sV", kev)], inc=(sem, 1))
                return wr + [(sem, cnt[sem])]

            def emit_poly(h, tu):
                ub, ubn = u_bank(h)
                s1 = pu_s1t[:, 128 * (h % 4):128 * (h % 4) + 128]
                ucol = 128 * ((h // 2) % 2)
                emit("scalar", lambda e, ub=ub, s1=s1, ucol=ucol: e.activation(
                    s1, ub[:, ucol:ucol + 128], AF.Square,
                    bias=cst_sb[:, 2:3]),
                     waits=[("sT", tu)] + freew(f"pus1_{h % 4}"), inc=("sA", 1))
                reads(f"uslot{h % 4}", "sA")
                reads(ubn, "sA")
                return cnt["sA"]

            def emit_stt(h, polyA):
                ub, ubn = u_bank(h)
                s1 = pu_s1t[:, 128 * (h % 4):128 * (h % 4) + 128]
                pu = pu_sb[:, 128 * (h % 2):128 * (h % 2) + 128]
                emit("vector", lambda e, s1=s1, pu=pu: e.scalar_tensor_tensor(
                    pu, s1, 16.0, m32_sb[:, :], ALU.add, ALU.mult),
                     waits=[("sA", polyA)] + freew(f"pu_{h % 2}"), inc=("sV", 1))
                res[f"pus1_{h % 4}"] = [("sV", cnt["sV"])]
                reads(ubn, "sV")
                return cnt["sV"]

            def emit_phiT(h, qfm_w):
                phb, phn = (psT, "psT") if h % 2 == 0 else (psU, "psU")
                sl = phiq[:, DPHI * (h % NQSL):DPHI * (h % NQSL) + DPHI]
                w = list(qfm_w) + freew(phn)
                emit("tensor", tr(phb[:, 0:128], sl[:, 0:128]), waits=w)
                emit("tensor", tr(phb[:, 128:256], sl[:, 128:256]))
                emit("tensor", tr(phb[0:17, 256:384], sl[:, 256:273]), inc=("sT", 1))
                res[f"phiq{h % NQSL}"] = [("sT", cnt["sT"])]
                return cnt["sT"]

            def emit_phiev(h, tphi):
                phb, phn = (psT, "psT") if h % 2 == 0 else (psU, "psU")
                pf = phiqfm[:, 384 * (h % 2):384 * (h % 2) + 384]
                emit("scalar", lambda e, pb=phb, pf=pf: e.copy(pf, pb[:, 0:384]),
                     waits=[("sT", tphi)] + freew(f"phifm{h % 2}"), inc=("sA", 1))
                reads(phn, "sA")
                return cnt["sA"]

            def emit_qkv_inter_state(h, puV, phievA):
                osl, obank, ocol = out_slot(h)
                pu = pu_sb[:, 128 * (h % 2):128 * (h % 2) + 128]
                pf = phiqfm[:, 384 * (h % 2):384 * (h % 2) + 384]
                emit("tensor", mm(osl[:, ocol:ocol + 65], pu,
                                  v_sb[:, 65 * h:65 * h + 65], h in (0, 7, 14), False),
                     waits=[("sV", puV)], inc=("sT", 1))
                res[f"pu_{h % 2}"] = [("sT", cnt["sT"])]
                c0 = PACK * h
                w = [("sA", phievA), addsnap[h]]
                emit("tensor", mm(osl[:, ocol:ocol + 65], pf[:, 0:128],
                                  state[:, c0:c0 + 65], False, False), waits=w)
                emit("tensor", mm(osl[:, ocol:ocol + 65], pf[:, 128:256],
                                  state[:, c0 + 65:c0 + 130], False, False))
                emit("tensor", mm(osl[:, ocol:ocol + 65], pf[0:17, 256:384],
                                  state[0:17, c0 + 130:c0 + 195], False, True),
                     inc=("sT", 1))
                res[f"phifm{h % 2}"] = [("sT", cnt["sT"])]
                half = 195 * (h % 2)
                w = res[f"kfm{h}"] + (freew("psE") if h % 2 == 0 else [])
                emit("tensor", mm(psE[:, half:half + 65], phik[:, DPHI * h:DPHI * h + 128],
                                  v_sb[:, 65 * h:65 * h + 65], True, True), waits=w)
                emit("tensor", mm(psE[:, half + 65:half + 130],
                                  phik[:, DPHI * h + 128:DPHI * h + 256],
                                  v_sb[:, 65 * h:65 * h + 65], True, True))
                emit("tensor", mm(psE[0:17, half + 130:half + 195],
                                  phik[:, DPHI * h + 256:DPHI * h + 273],
                                  v_sb[:, 65 * h:65 * h + 65], True, True),
                     inc=("sT", 1))
                stv = cnt["sT"]
                res[f"phik{h}"] = [("sT", stv)]
                if h % 2 == 1:
                    c0p = PACK * (h - 1)
                    emit("vector", lambda e, c0p=c0p: e.tensor_tensor(
                        state[:, c0p:c0p + 2 * PACK], psE[:, 0:2 * PACK],
                        state[:, c0p:c0p + 2 * PACK], ALU.add),
                         waits=[("sT", stv)] + freew("state")
                               + [addsnap[h - 1], addsnap[h]], inc=("sV", 1))
                    addsnap[h - 1] = addsnap[h] = ("sV", cnt["sV"])
                    reads("psE", "sV")
                return stv

            def emit_div_group(h, stv):
                glo = {6: 0, 13: 7, 15: 14}[h]
                gbank, gname = {6: (psA, "psA"), 13: (psB, "psB"),
                                15: (psC, "psC")}[h]
                nh = h + 1 - glo
                w = [("sT", stv)]
                if h == 6:
                    w += freew("attn")
                if gname == "psC":
                    dsrc = psC[:, 130:260].rearrange("p (h v) -> p h v", h=2)[:, :, 64:65].squeeze(2)
                else:
                    dsrc = gbank[:, 0:455].rearrange("p (h v) -> p h v", h=7)[:, :, 64:65].squeeze(2)
                emit("vector", cp(den_sb[:, glo:glo + nh], dsrc), waits=w, inc=("sV", 1))
                genv = cnt["sV"]
                emit("vector", lambda e, glo=glo, nh=nh: e.tensor_scalar(
                    den_sb[:, glo:glo + nh], den_sb[:, glo:glo + nh],
                    EPS, None, ALU.add), waits=[("sV", genv)], inc=("sV", 1))
                epsv = cnt["sV"]
                emit("vector", lambda e, glo=glo, nh=nh: e.reciprocal(
                    rec_sb[:, glo:glo + nh], den_sb[:, glo:glo + nh]),
                     waits=[("sV", epsv)], inc=("sV", 1))
                recv = cnt["sV"]
                for h2 in range(glo, h + 1):
                    osl2, _, ocol2 = out_slot(h2)
                    emit("vector", lambda e, h2=h2, osl2=osl2, ocol2=ocol2: e.tensor_scalar(
                        attn[:, 64 * h2:64 * h2 + 64], osl2[:, ocol2:ocol2 + 64],
                        rec_sb[:, h2:h2 + 1], None, ALU.mult),
                         waits=[("sV", recv)] if h2 == glo else (),
                         inc=("sV", 1) if h2 == h else None)
                reads(gname, "sV")

            DEPTH_U = 3
            DEPTH_STT = 2
            qfm_l, tu_l, poly_l, stt_l = {}, {}, {}, {}
            phiT_l, phiev_l = {}, {}

            def advance(h):
                if h < H and h not in poly_l:
                    qfm_l[h] = emit_qfm(h)
                    tu_l[h] = emit_u(h)
                    poly_l[h] = emit_poly(h, tu_l[h])

            def advance_stt(h):
                if h < H and h not in stt_l:
                    stt_l[h] = emit_stt(h, poly_l[h])

            for h0 in range(min(DEPTH_U, H)):
                advance(h0)
            for h0 in range(min(DEPTH_STT, H)):
                advance_stt(h0)
            for h in range(H):
                if h not in phiev_l:
                    phiT_l[h] = emit_phiT(h, qfm_l[h])
                    phiev_l[h] = emit_phiev(h, phiT_l[h])
                if dump_cb and h == 0:
                    dump_cb("pu0", pu_sb[:, 0:128])
                    dump_cb("s1h1", pu_s1t[:, 0:128])
                    dump_cb("phiq0", phiq[:, 0:DPHI])
                stv = emit_qkv_inter_state(h, stt_l[h], phiev_l[h])
                if dump_cb and h == 0:
                    dump_cb("phifm0", phiqfm[:, 0:384])
                if h + 1 < H:
                    phiT_l[h + 1] = emit_phiT(h + 1, qfm_l[h + 1])
                    phiev_l[h + 1] = emit_phiev(h + 1, phiT_l[h + 1])
                advance(h + DEPTH_U)
                advance_stt(h + DEPTH_STT)
                if h in (6, 13, 15):
                    emit_div_group(h, stv)
            divV = cnt["sV"]
            theads = cnt["sT"]
            res["v"] = [("sT", theads)]
            res["qk"] = [("sT", theads), ("sG", cnt["sG"]), ("sV", cnt["sV"])]
            res["qksc"] = [("sG", cnt["sG"]), ("sV", cnt["sV"])]
            res["phikraw"] = [("sG", cnt["sG"]), ("sT", theads)]

            if dump_cb:
                emit("vector", cp(attnfm.bitcast(F32)[:, 0:455], psA[:, 0:455]),
                     waits=[("sT", theads)] + freew("attnfm"), inc=("sV", 1))
                res["attnfm"] = [("sV", cnt["sV"])]
                dump_cb("o0", attnfm.bitcast(F32)[:, 0:455])
                dump_cb("den0", den_sb[:, :])
                dump_cb("rec0", rec_sb[:, :])

            def tail_a(divV=divV):
                w = [("sV", divV)] + freew("psT")
                for dc in range(8):
                    emit("tensor", tr(psT[:, 128 * dc:128 * dc + 128],
                                      attn[:, 128 * dc:128 * dc + 128]),
                         waits=w if dc == 0 else (), inc=("sT", 1) if dc == 7 else None)
                tat = cnt["sT"]
                reads("attn", "sT")
                emit("scalar", lambda e: e.copy(attnfm[:, :], psT[:, :]),
                     waits=[("sT", tat)] + freew("attnfm"), inc=("sA", 1))
                reads("psT", "sA")

            def tail_b(ci=ci):
                atev = cnt["sA"]
                w = [("sA", atev)] + freew("psD", "psE")
                for dc in range(8):
                    emit("tensor", mm(psE[:, 0:512], attnfm[:, 128 * dc:128 * dc + 128],
                                      wo_sb[:, 1024 * dc:1024 * dc + 512], dc == 0, dc == 7),
                         waits=w if dc == 0 else ())
                    emit("tensor", mm(psD[:, 0:512], attnfm[:, 128 * dc:128 * dc + 128],
                                      wo_sb[:, 1024 * dc + 512:1024 * dc + 1024], dc == 0, dc == 7),
                         inc=("sT", 1) if dc == 7 else None)
                two = cnt["sT"]
                res["attnfm"] = [("sT", two)]
                emit("vector", lambda e: e.tensor_tensor(
                    stream[:, DM * ci:DM * ci + 512], psE[:, 0:512],
                    stream[:, DM * ci:DM * ci + 512], ALU.add),
                     waits=[("sT", two), ("dG", x_ld)])
                emit("vector", lambda e: e.tensor_tensor(
                    stream[:, DM * ci + 512:DM * ci + 1024], psD[:, 0:512],
                    stream[:, DM * ci + 512:DM * ci + 1024], ALU.add), inc=("sV", 1))
                rv = cnt["sV"]
                reads("psD", "sV")
                reads("psE", "sV")
                res["wo_resid"] = [("sV", rv)]
                return rv
            return tail_a, tail_b

        last_proj_t = [0]
        def load_xfm(src_t, ci, slot):
            emit("gpsimd", dma(
                xfm[:, DM * slot:DM * slot + DM].rearrange("p (dc t) -> p dc t", dc=8),
                src_t.rearrange("(dc p) t -> p dc t", p=128)[:, :, C * ci:C * ci + C]),
                 waits=freew(f"xfm{slot}"), inc=("dG", 16))
            return cnt["dG"]

        for ci in range(npre):
            xld = load_xfm(xpt_h, ci, ci % 2)
            scan_chunk(ci, ci % 2, 1, True, xld, qk_tok=pk_tok, ksc=pk_sc)
            res[f"xfm{ci % 2}"] = [("sT", cnt["sT"])]

        if dumps:
            dump("state_pre", state[:, :], PACK * H)

        def ln_stats(ci, resid_sv):
            w = [("sV", resid_sv)]
            emit("scalar", lambda e, ci=ci: e.activation(
                attn[:, :], stream[:, DM * ci:DM * ci + DM],
                AF.Square, accum_out=ln_sb[:, 16 + ci:17 + ci]),
                 waits=w, inc=("sA", 1))
            reads("attn", "sA")
            emit("vector", lambda e, ci=ci: e.reduce_sum(
                ln_sb[:, ci:ci + 1], stream[:, DM * ci:DM * ci + DM], AX),
                 waits=w, inc=("sV", 1))


        pend = [None, None]
        no_pipe = False
        for ci in range(nch):
            emit("gpsimd", dma(stream[:, DM * ci:DM * ci + DM], x_h[C * ci:C * ci + C, :]),
                 inc=("dG", 16))
            x_ld = cnt["dG"]
            xld = load_xfm(xt_h, ci, ci % 2)
            if pend[0] is not None:
                pend[0]()
                pend[0] = None
            def tailb_only(ci=ci):
                if pend[1] is not None:
                    rv = pend[1]()
                    ln_stats(ci - 1, rv)
                    if ci - 1 == 1:
                        dump("stream1", stream[:, DM:2 * DM], DM)
                    pend[1] = None
            tails = scan_chunk(ci, ci % 2, 0, False, xld,
                               dump_cb=(dump if ci == 0 else None),
                               after_proj=tailb_only,
                               after_pads=None, x_ld=x_ld)
            pend = list(tails)
            res[f"xfm{ci % 2}"] = [("sT", cnt["sT"])]
            if ci == 0:
                dump("qk0", qk_tok[:, :], 512)
                dump("v0", v_sb[:, :], 1040)
                dump("phik0", phik[:, :], DPHI * H)
                dump("attn0", attn[:, :], DM)
                dump("state0", state[:, :], PACK * H)
        pend[0]()
        rv = pend[1]()
        ln_stats(nch - 1, rv)

        def ln_finish():
            sAs, sVs = cnt["sA"], cnt["sV"]
            emit("vector", lambda e: e.tensor_scalar(
                ln_sb[:, 32:48], ln_sb[:, 0:16], 1.0 / DM, None, ALU.mult),
                 waits=[("sA", sAs), ("sV", sVs)], inc=("sV", 1))
            meanv = cnt["sV"]
            emit("vector", lambda e: e.tensor_tensor(
                den_sb[:, 0:16], ln_sb[:, 32:48], ln_sb[:, 32:48], ALU.mult),
                 waits=[("sV", meanv)], inc=("sV", 1))
            msqv = cnt["sV"]
            emit("vector", lambda e: e.scalar_tensor_tensor(
                ln_sb[:, 64:80], ln_sb[:, 16:32], 1.0 / DM, den_sb[:, 0:16],
                ALU.mult, ALU.subtract), waits=[("sV", msqv)], inc=("sV", 1))
            vv = cnt["sV"]
            emit("scalar", lambda e: e.activation(rec_sb[:, 0:16], ln_sb[:, 64:80],
                                                  AF.Ln, bias=cst_sb[:, 3:4]), waits=[("sV", vv)])
            emit("scalar", lambda e: e.activation(ln_sb[:, 48:64], rec_sb[:, 0:16],
                                                  AF.Exp, scale=cst_sb[:, 4:5]), inc=("sA", 1))
            sa = cnt["sA"]
            emit("vector", lambda e: e.scalar_tensor_tensor(
                ln_sb[:, 64:80], ln_sb[:, 32:48], -1.0, ln_sb[:, 48:64],
                ALU.mult, ALU.mult), waits=[("sA", sa)], inc=("sV", 1))
            return cnt["sV"]

        vneg = ln_finish()
        if dumps:
            dump("lnsb", ln_sb[:, :], 96)
        for ci in range(nch):
            emit("vector", lambda e, ci=ci: e.tensor_scalar(
                stream[:, DM * ci:DM * ci + DM], stream[:, DM * ci:DM * ci + DM],
                ln_sb[:, 48 + ci:49 + ci], ln_sb[:, 64 + ci:65 + ci],
                ALU.mult, ALU.add),
                 waits=[("sV", vneg)] if ci == 0 else (), inc=("sV", 1))
            res[f"norm{ci}"] = [("sV", cnt["sV"])]
        if dumps:
            dump("y1", stream[:, 0:DM], DM)

        wd3 = wd_h.rearrange("(hc p) j -> p hc j", p=128)
        emit("gpsimd", dma(wd_sb[:, 20480:24576].rearrange("p (hc j) -> p hc j", hc=4),
                           wd3[:, 20:24]), inc=("dG", 16))
        emit("gpsimd", dma(wd_sb[:, 0:12288].rearrange("p (hc j) -> p hc j", hc=12),
                           wd3[:, 0:12]),
             waits=[("sT", last_proj_t[0])], inc=("dG", 16))
        emit("gpsimd", dma(wd_sb[:, 12288:20480].rearrange("p (hc j) -> p hc j", hc=8),
                           wd3[:, 12:20]),
             waits=[("sT", cnt["sT"])], inc=("dG", 16))
        W_WD = cnt["dG"]

        for ci in range(nch):
            w = res.pop(f"norm{ci}") + freew("psT")
            for dc in range(8):
                emit("tensor", tr(psT[:, 128 * dc:128 * dc + 128],
                                  stream[:, DM * ci + 128 * dc:DM * ci + 128 * dc + 128]),
                     waits=w if dc == 0 else (), inc=("sT", 1) if dc == 7 else None)
                w = ()
            ty = cnt["sT"]
            emit("vector", cp(xfm[:, 0:DM], psT[:, :]),
                 waits=[("sT", ty)] + freew("xfm0") + freew("xfm1"), inc=("sV", 1))
            reads("psT", "sV")
            y1ev = cnt["sV"]

            hm6 = []
            for q in range(6):
                psg = (psA, psB, psC)[q % 3]
                psu2 = (psD, psE, psF)[q % 3]
                gname = ("psA", "psB", "psC")[q % 3]
                uname = ("psD", "psE", "psF")[q % 3]
                w = [("dG", W_GU)] + freew(gname, uname)
                if q == 0:
                    w += [("sV", y1ev)] + freew("htok")
                for dc in range(8):
                    lhsT = xfm[:, 128 * dc:128 * dc + 128]
                    emit("tensor", mm(psg[:, 0:512], lhsT,
                                      wg_sb[:, 3072 * dc + 512 * q:3072 * dc + 512 * q + 512],
                                      dc == 0, dc == 7),
                         waits=w if dc == 0 else ())
                    w = ()
                    emit("tensor", mm(psu2[:, 0:512], lhsT,
                                      wu_sb[:, 3072 * dc + 512 * q:3072 * dc + 512 * q + 512],
                                      dc == 0, dc == 7),
                         inc=("sT", 1) if dc == 7 else None)
                tgu = cnt["sT"]
                emit("scalar", lambda e, q=q, psg=psg: e.activation(
                    silu_sb[:, 512 * (q % 3):512 * (q % 3) + 512], psg[:, 0:512], AF.Silu,
                    scale=cst_sb[:, 5:6]),
                     waits=[("sT", tgu)] + freew(f"silu{q % 3}"), inc=("sA", 1))
                sk = cnt["sA"]
                reads(gname, "sA")
                emit("vector", lambda e, q=q, psu2=psu2: e.tensor_tensor(
                    h_tok[:, 512 * q:512 * q + 512],
                    silu_sb[:, 512 * (q % 3):512 * (q % 3) + 512], psu2[:, 0:512],
                    ALU.mult),
                     waits=[("sA", sk)], inc=("sV", 1))
                res[f"silu{q % 3}"] = [("sV", cnt["sV"])]
                reads(uname, "sV")
                hm6.append(cnt["sV"])
            hm = [hm6[2], hm6[5]]

            hevs = []
            for grp in range(3):
                phb, phn = (psT, "psT") if grp % 2 == 0 else (psU, "psU")
                w = [("sV", hm6[2 * grp + 1])] + freew(phn)
                for i in range(8):
                    hc = 8 * grp + i
                    emit("tensor", tr(phb[:, 128 * i:128 * i + 128],
                                      h_tok[:, 128 * hc:128 * hc + 128]),
                         waits=w if i == 0 else (), inc=("sT", 1) if i == 7 else None)
                    w = ()
                tg = cnt["sT"]
                emit("scalar", lambda e, phb=phb, grp=grp: e.copy(
                    hfm[:, 1024 * grp:1024 * grp + 1024], phb[:, :]),
                     waits=[("sT", tg)] + (freew("hfm") if grp == 0 else []),
                     inc=("sA", 1))
                reads(phn, "sA")
                hevs.append(cnt["sA"])

            w = [("dG", W_WD)] + freew("psA", "psB")
            for hc in range(24):
                if hc % 8 == 0:
                    w = w + [("sA", hevs[hc // 8])]
                emit("tensor", mm(psA[:, 0:512], hfm[:, 128 * hc:128 * hc + 128],
                                  wd_sb[:, 1024 * hc:1024 * hc + 512], hc == 0, hc == 23),
                     waits=w)
                w = []
                emit("tensor", mm(psB[:, 0:512], hfm[:, 128 * hc:128 * hc + 128],
                                  wd_sb[:, 1024 * hc + 512:1024 * hc + 1024], hc == 0, hc == 23),
                     inc=("sT", 1) if hc == 23 else None)
            td = cnt["sT"]
            res["hfm"] = [("sT", td)]
            res["htok"] = [("sT", td)]
            emit("vector", lambda e, ci=ci: e.scalar_tensor_tensor(
                stream[:, DM * ci:DM * ci + 512], psA[:, 0:512], FFSC,
                stream[:, DM * ci:DM * ci + 512], ALU.mult, ALU.add),
                 waits=[("sT", td)])
            emit("vector", lambda e, ci=ci: e.scalar_tensor_tensor(
                stream[:, DM * ci + 512:DM * ci + 1024], psB[:, 0:512], FFSC,
                stream[:, DM * ci + 512:DM * ci + 1024], ALU.mult, ALU.add),
                 inc=("sV", 1))
            zr = cnt["sV"]
            res["psA"] = [("sV", zr)]
            res["psB"] = [("sV", zr)]
            res[f"z{ci}"] = [("sV", zr)]
            ln_stats(ci, zr)

        if dumps:
            dump("z0", stream[:, 0:DM], DM)

        vneg2 = ln_finish()
        for ci in range(nch):
            stg = ostage[:, 1024 * (ci % 2):1024 * (ci % 2) + 1024]
            w = [("sV", vneg2)] if ci == 0 else []
            if ci >= 2:
                w.append(("dS", 16 * (ci - 1)))
            emit("vector", lambda e, ci=ci, stg=stg: e.tensor_scalar(
                stg, stream[:, DM * ci:DM * ci + DM],
                ln_sb[:, 48 + ci:49 + ci], ln_sb[:, 64 + ci:65 + ci],
                ALU.mult, ALU.add),
                 waits=w, inc=("sV", 1))
            nv = cnt["sV"]
            emit("scalar", lambda e, ci=ci, stg=stg: e.dma_start(
                out=y_h[C * ci:C * ci + C, :], in_=stg),
                 waits=[("sV", nv)], inc=("dS", 16))

        final_dS = cnt["dS"]
        final_dD = cnt["dD"]
        emit("scalar", lambda e: e.nop(), waits=[("dS", final_dS)])
        emit("sync", lambda e: e.nop(),
             waits=[("dS", final_dS)] + ([("dD", final_dD)] if final_dD else []))
        emit("gpsimd", lambda e: e.nop(), waits=[("dS", final_dS)])

        with nc.Block() as block:
            def runner(eng_name):
                def go(e):
                    for fn, waits, inc in streams[eng_name]:
                        for s, v in waits:
                            e.wait_ge(sems[s], v)
                        i = fn(e)
                        if inc:
                            i.then_inc(sems[inc[0]], inc[1])
                return go
            block.sync(runner("sync"))
            block.gpsimd(runner("gpsimd"))
            block.vector(runner("vector"))
            block.scalar(runner("scalar"))
            block.tensor(runner("tensor"))

    return nc


def make_inputs_for_core(core, x_full, w):
    b, half = core // 2, core % 2
    x_main = np.ascontiguousarray(x_full[b, half * T:(half + 1) * T])
    x_t = np.ascontiguousarray(x_main.T)
    if half == 1:
        x_pt = np.ascontiguousarray(x_full[b, 0:T].T)
        pmask = 1.0
    else:
        x_pt = np.zeros((DM, T), np.float32)
        pmask = 0.0
    cst = np.zeros((128, 8), np.float32)
    cst[:, 0] = 1.0
    cst[:, 1] = pmask
    cst[:, 2] = 4.0
    cst[:, 3] = LNEPS
    cst[:, 4] = -0.5
    cst[:, 5] = 1.0 / 127.0
    m = {"x": x_main, "xt": x_t, "xpt": x_pt, "cst": cst}
    m.update(w)
    return m


def shared_weights(inputs):
    ident = np.eye(128, dtype=np.float32)
    m32 = (np.triu(np.ones((128, 128), np.float32)) / 32.0).astype(np.float32)
    return {
        "wq": np.ascontiguousarray(inputs["L0_attn_Wq"], np.float32),
        "wk": np.ascontiguousarray(inputs["L0_attn_Wk"], np.float32),
        "wv": np.ascontiguousarray(inputs["L0_attn_Wv"], np.float32),
        "wo": np.ascontiguousarray(inputs["L0_attn_Wo"], np.float32),
        "wg": np.ascontiguousarray(inputs["L0_ffn_w_gate"].T),
        "wu": np.ascontiguousarray(inputs["L0_ffn_w_up"].T),
        "wd": np.ascontiguousarray(inputs["L0_ffn_w_down"].T),
        "ident": ident,
        "m32": m32,
    }


def kernel(L0_attn_hidden_states, L0_attn_Wq, L0_attn_Wk, L0_attn_Wv,
           L0_attn_Wo, L0_ffn_w_gate, L0_ffn_w_up, L0_ffn_w_down):
    inputs = {
        "L0_attn_hidden_states": np.asarray(L0_attn_hidden_states),
        "L0_attn_Wq": np.asarray(L0_attn_Wq), "L0_attn_Wk": np.asarray(L0_attn_Wk),
        "L0_attn_Wv": np.asarray(L0_attn_Wv), "L0_attn_Wo": np.asarray(L0_attn_Wo),
        "L0_ffn_w_gate": np.asarray(L0_ffn_w_gate),
        "L0_ffn_w_up": np.asarray(L0_ffn_w_up),
        "L0_ffn_w_down": np.asarray(L0_ffn_w_down),
    }
    x = inputs["L0_attn_hidden_states"]
    w = shared_weights(inputs)
    nc = build_nc()
    in_maps = [make_inputs_for_core(c, x, w) for c in range(8)]
    res = run_bass_kernel_spmd(nc, in_maps, core_ids=list(range(8)))
    out = np.empty((4, 4096, DM), np.float32)
    for c in range(8):
        b, half = c // 2, c % 2
        out[b, half * T:(half + 1) * T] = res.results[c]["y"]
    return out


# version 10
# speedup vs baseline: 1.3591x; 1.0251x over previous
import contextlib
import numpy as np
import concourse.bass as bass
import concourse.mybir as mybir
from concourse.bass_utils import run_bass_kernel_spmd

F32 = mybir.dt.float32
BF16 = mybir.dt.bfloat16
I32 = mybir.dt.int32
AF = mybir.ActivationFunctionType
ALU = mybir.AluOpType
AX = mybir.AxisListType.X

C = 128
H = 16
DM = 1024
DFF = 3072
T = 2048
DPHI = 273
PACK = 195
EPS = 1e-9
LNEPS = 1e-5
KQSC = 1.0 / 32.0
KRSC = 0.25
FFSC = float(1.0 / (127.0 * 127.0))


def build_nc(nch=T // C, npre=T // C, dumps=(), pus1_sbuf=False, late_div=False,
             attnfm_sep=False, rec_sep=False, phiq3=True):
    nc = bass.Bass("TRN2", target_bir_lowering=False)
    x_h = nc.dram_tensor("x", [T, DM], F32, kind="ExternalInput").ap()
    xt_h = nc.dram_tensor("xt", [DM, T], F32, kind="ExternalInput").ap()
    xpt_h = nc.dram_tensor("xpt", [DM, T], F32, kind="ExternalInput").ap()
    cst_h = nc.dram_tensor("cst", [128, 8], F32, kind="ExternalInput").ap()
    wq_h = nc.dram_tensor("wq", [DM, 256], F32, kind="ExternalInput").ap()
    wk_h = nc.dram_tensor("wk", [DM, 256], F32, kind="ExternalInput").ap()
    wv_h = nc.dram_tensor("wv", [DM, DM], F32, kind="ExternalInput").ap()
    wo_h = nc.dram_tensor("wo", [DM, DM], F32, kind="ExternalInput").ap()
    wg_h = nc.dram_tensor("wg", [DM, DFF], I32, kind="ExternalInput").ap()
    wu_h = nc.dram_tensor("wu", [DM, DFF], I32, kind="ExternalInput").ap()
    wd_h = nc.dram_tensor("wd", [DFF, DM], I32, kind="ExternalInput").ap()
    id_h = nc.dram_tensor("ident", [128, 128], F32, kind="ExternalInput").ap()
    m32_h = nc.dram_tensor("m32", [128, 128], F32, kind="ExternalInput").ap()
    y_h = nc.dram_tensor("y", [T, DM], F32, kind="ExternalOutput").ap()
    dump_h = {}
    for tag, shape in dumps:
        dump_h[tag] = nc.dram_tensor("dbg_" + tag, shape, F32, kind="ExternalOutput").ap()

    streams = {e: [] for e in ("sync", "gpsimd", "vector", "scalar", "tensor")}
    cnt = {s: 0 for s in ("dG", "dS", "dD", "sT", "sV", "sA", "sG")}
    res = {}

    pending_dump = {}

    def emit(eng, fn, waits=(), inc=None):
        wmax = {}
        pd = pending_dump.pop(eng, None)
        waits = list(waits) + ([("dD", pd)] if pd else [])
        for s, v in waits:
            assert v <= cnt[s], f"wait {s}>={v}, emitted {cnt[s]}"
            if v > 0:
                wmax[s] = max(wmax.get(s, 0), v)
        if inc:
            s, n = inc
            cnt[s] += n
        streams[eng].append((fn, tuple(wmax.items()), inc))

    def freew(*names):
        w = []
        for n in names:
            w += res.pop(n, [])
        return w

    def reads(name, sem):
        res.setdefault(name, []).append((sem, cnt[sem]))

    with contextlib.ExitStack() as stk:
        E = stk.enter_context
        stream = E(nc.sbuf_tensor("stream", [128, nch * DM], BF16))
        xfm = E(nc.sbuf_tensor("xfm", [128, 2 * DM], BF16))
        warena = E(nc.sbuf_tensor("warena", [128, 24576], BF16))
        wg_sb = E(nc.sbuf_tensor("wg_sb", [128, 24576], BF16))
        wu_sb = E(nc.sbuf_tensor("wu_sb", [128, 24576], BF16))
        scr = E(nc.sbuf_tensor("scr", [128, 13188], BF16))
        m32_sb = E(nc.sbuf_tensor("m32_sb", [128, 128], BF16))
        id_sb = E(nc.sbuf_tensor("id_sb", [128, 128], BF16))
        cst_sb = E(nc.sbuf_tensor("cst_sb", [128, 8], F32))
        den_sb = E(nc.sbuf_tensor("den_sb", [128, 16], F32))
        ln_sb = E(nc.sbuf_tensor("ln_sb", [128, 80], F32))
        pu_s1t = E(nc.sbuf_tensor("pu_s1t", [128, 512], BF16))
        attnfm_t = E(nc.sbuf_tensor("attnfm_t", [128, 1024], BF16)) if attnfm_sep else None
        rec_t = E(nc.sbuf_tensor("rec_t", [128, 16], F32)) if rec_sep else None

        rec_sb = ln_sb[:, 64:80]
        if rec_sep:
            rec_sb = rec_t

        xps = scr[:, 0:2048]
        qk_tok = scr[:, 0:512]
        ksc = scr[:, 512:768]
        qk_pad = scr[:, 768:1792]
        pu_sb = scr[:, 1792:2048]
        pk_tok = scr[:, 12164:12676]
        pk_sc = scr[:, 12676:12932]
        phik = scr[:, 2048:6416]
        phiq = scr[:, 6416:7235]
        NQSL = 3
        phiqfm = scr[:, 7235:8003]
        v_sb = scr[:, 8003:9043]
        state = scr[:, 9043:12163]
        attn = scr[:, 12164:13188]
        attnfm = scr[:, 768:1792]
        if attnfm_sep:
            attnfm = attnfm_t
        h_tok = scr[:, 0:3072]
        hfm = scr[:, 3072:6144]
        silu_sb = scr[:, 6144:7680]
        ostage = scr[:, 7680:11776].bitcast(F32)

        psA = E(nc.psum_tensor("psA", [128, 512], F32))
        psB = E(nc.psum_tensor("psB", [128, 512], F32))
        psC = E(nc.psum_tensor("psC", [128, 512], F32))
        psD = E(nc.psum_tensor("psD", [128, 512], F32))
        psE = E(nc.psum_tensor("psE", [128, 512], F32))
        psF = E(nc.psum_tensor("psF", [128, 512], F32))
        psT = E(nc.psum_tensor("psT", [128, 1024], BF16))
        psU = E(nc.psum_tensor("psU", [128, 1024], BF16))

        sems = {s: E(nc.semaphore(s)) for s in cnt}

        wq_sb = warena[:, 0:2048]
        wk_sb = warena[:, 2048:4096]
        wv_sb = warena[:, 4096:12288]
        wo_sb = warena[:, 12288:20480]
        wd_sb = warena[:, 0:24576]

        def dma(dst, src):
            return lambda e, d=dst, s=src: e.dma_start(out=d, in_=s)

        emit("gpsimd", dma(id_sb[:, :], id_h[:, :]), inc=("dG", 16))

        emit("gpsimd", dma(wq_sb.rearrange("p (d j) -> p d j", d=8),
                           wq_h.rearrange("(d p) j -> p d j", p=128)), inc=("dG", 16))
        emit("gpsimd", dma(wk_sb.rearrange("p (d j) -> p d j", d=8),
                           wk_h.rearrange("(d p) j -> p d j", p=128)), inc=("dG", 16))
        emit("gpsimd", dma(wv_sb.rearrange("p (d j) -> p d j", d=8),
                           wv_h.rearrange("(d p) j -> p d j", p=128)), inc=("dG", 16))
        emit("gpsimd", dma(wo_sb.rearrange("p (d j) -> p d j", d=8),
                           wo_h.rearrange("(d p) j -> p d j", p=128)), inc=("dG", 16))
        W_ATTN = cnt["dG"]
        emit("gpsimd", dma(m32_sb[:, :], m32_h[:, :]), inc=("dG", 16))
        emit("sync", dma(cst_sb[:, :], cst_h[:, :]), inc=("dG", 16))
        W_CONST = cnt["dG"]

        emit("vector", lambda e: e.memset(
            phik.rearrange("p (h d) -> p h d", h=16)[:, :, 0:1], 1.0))
        emit("vector", lambda e: e.memset(
            phiq.rearrange("p (s d) -> p s d", s=3)[:, :, 0:1], 1.0))
        emit("vector", lambda e: e.memset(state[:, :], 0.0), inc=("sV", 1))
        res["state"] = [("sV", cnt["sV"])]

        emit("gpsimd", dma(wg_sb.rearrange("p (d j) -> p d j", d=8),
                           wg_h.rearrange("(d p) j -> p d j", p=128)), inc=("dG", 16))
        emit("gpsimd", dma(wu_sb.rearrange("p (d j) -> p d j", d=8),
                           wu_h.rearrange("(d p) j -> p d j", p=128)), inc=("dG", 16))
        W_GU = cnt["dG"]

        def tr(out, src, tp=None):
            return lambda e, o=out, s=src, t=tp: e.transpose(o, s, id_sb[:, :],
                                                             tile_position=t)

        def mm(out, lhsT, rhs, start, stop, tp=None):
            return lambda e, o=out, l=lhsT, r=rhs, a=start, b=stop, t=tp: e.matmul(
                o, l, r, start=a, stop=b, tile_position=t)

        def cp(eng_dst, src):
            return lambda e, d=eng_dst, s=src: e.tensor_copy(d, s)

        def out_slot(h):
            if h < 7:
                return psA, "psA", 65 * h
            if h < 14:
                return psB, "psB", 65 * (h - 7)
            return psC, "psC", 130 + 65 * (h - 14)

        addsnap = {h: ("sV", 0) for h in range(H)}

        def dump(tag, src_bf16, n_f32_cols=None):
            if tag in dump_h:
                w = [(s, cnt[s]) for s in ("sT", "sV", "sA", "sG")]
                emit("gpsimd", dma(dump_h[tag][:, :], src_bf16), waits=w, inc=("dD", 16))
                for eng in streams:
                    pending_dump[eng] = cnt["dD"]

        def scan_chunk(ci, xslot, vcol, kv_only, xld, qk_tok=qk_tok, ksc=ksc,
                       dump_cb=None, after_proj=None, after_pads=None, x_ld=0):
            xf = xfm[:, DM * xslot:DM * xslot + DM]
            w = [("dG", xld), ("dG", W_CONST), ("dG", W_ATTN)] + freew(
                "psA", "psB", "psC")
            for dc in range(8):
                lhsT = xf[:, 128 * dc:128 * dc + 128]
                if not kv_only:
                    emit("tensor", mm(psA[:, 0:256], lhsT,
                                      wq_sb[:, 256 * dc:256 * dc + 256], dc == 0, dc == 7),
                         waits=w if dc == 0 else ())
                    w = ()
                emit("tensor", mm(psA[:, 256:512], lhsT,
                                  wk_sb[:, 256 * dc:256 * dc + 256],
                                  (dc == 0) and kv_only, dc == 7),
                     waits=w if dc == 0 else ())
                w = ()
                emit("tensor", mm(psB[:, 0:512], lhsT,
                                  wv_sb[:, 1024 * dc:1024 * dc + 512], dc == 0, dc == 7))
                emit("tensor", mm(psC[:, 0:512], lhsT,
                                  wv_sb[:, 1024 * dc + 512:1024 * dc + 1024], dc == 0, dc == 7),
                     inc=("sT", 1) if dc == 7 else None)
            tproj = cnt["sT"]
            last_proj_t[0] = tproj

            lo = 256 if kv_only else 0
            emit("vector", cp(qk_tok[:, lo:512], psA[:, lo:512]),
                 waits=[("sT", tproj)] + freew("qk"))
            emit("vector", lambda e: e.tensor_scalar(
                ksc[:, 0:256], qk_tok[:, 256:512], KQSC, None, ALU.mult),
                 waits=freew("qksc"))
            emit("vector", lambda e: e.tensor_scalar(
                phik.rearrange("p (h d) -> p h d", h=16)[:, :, 1:17],
                qk_tok[:, 256:512].rearrange("p (h f) -> p h f", h=16),
                KRSC, None, ALU.mult), waits=freew("phikraw"), inc=("sV", 1))
            kev = cnt["sV"]
            emit("vector", cp(v_sb.rearrange("p (h v) -> p h v", h=16)[:, 0:8, 0:64],
                              psB[:, 0:512].rearrange("p (h v) -> p h v", h=8)),
                 waits=freew("v"))
            emit("vector", cp(v_sb.rearrange("p (h v) -> p h v", h=16)[:, 8:16, 0:64],
                              psC[:, 0:512].rearrange("p (h v) -> p h v", h=8)))
            emit("vector", cp(v_sb.rearrange("p (h v) -> p h v", h=16)[:, :, 64:65],
                              cst_sb[:, vcol:vcol + 1].unsqueeze(1).broadcast_to([128, 16, 1])),
                 inc=("sV", 1))
            projev = cnt["sV"]
            reads("psA", "sV")
            reads("psB", "sV")
            reads("psC", "sV")

            for h in range(H):
                w = [("sV", kev)] + freew(f"phik{h}")
                def kq(e, h=h, qk_tok=qk_tok, ksc=ksc):
                    o = phik[:, DPHI * h + 17:DPHI * h + 273].rearrange(
                        "p (i j) -> p i j", i=16)
                    i0 = ksc[:, 16 * h:16 * h + 16].unsqueeze(2).broadcast_to([128, 16, 16])
                    i1 = qk_tok[:, 256 + 16 * h:256 + 16 * h + 16].unsqueeze(1).broadcast_to([128, 16, 16])
                    return e.tensor_tensor(o, i0, i1, ALU.mult)
                eng = "vector" if (kv_only and h % 3 == 2) else "gpsimd"
                sem = "sV" if eng == "vector" else "sG"
                emit(eng, kq, waits=w, inc=(sem, 1))
                res[f"kfm{h}"] = [(sem, cnt[sem])]

            if kv_only:
                for h in range(H):
                    half = 195 * (h % 2)
                    w = res[f"kfm{h}"] + (freew("psE") if h % 2 == 0 else [])
                    if h == 0:
                        w.append(("sV", projev))
                    emit("tensor", mm(psE[:, half:half + 65], phik[:, DPHI * h:DPHI * h + 128],
                                      v_sb[:, 65 * h:65 * h + 65], True, True), waits=w)
                    emit("tensor", mm(psE[:, half + 65:half + 130],
                                      phik[:, DPHI * h + 128:DPHI * h + 256],
                                      v_sb[:, 65 * h:65 * h + 65], True, True))
                    emit("tensor", mm(psE[0:17, half + 130:half + 195],
                                      phik[:, DPHI * h + 256:DPHI * h + 273],
                                      v_sb[:, 65 * h:65 * h + 65], True, True),
                         inc=("sT", 1))
                    stv = cnt["sT"]
                    res[f"phik{h}"] = [("sT", stv)]
                    if h % 2 == 1:
                        c0p = PACK * (h - 1)
                        emit("vector", lambda e, c0p=c0p: e.tensor_tensor(
                            state[:, c0p:c0p + 2 * PACK], psE[:, 0:2 * PACK],
                            state[:, c0p:c0p + 2 * PACK], ALU.add),
                             waits=[("sT", stv)] + freew("state")
                                   + [addsnap[h - 1], addsnap[h]], inc=("sV", 1))
                        addsnap[h - 1] = addsnap[h] = ("sV", cnt["sV"])
                        reads("psE", "sV")
                res["v"] = [("sT", cnt["sT"])]
                res["qk"] = [("sG", cnt["sG"]), ("sT", cnt["sT"])]
                res["qksc"] = [("sG", cnt["sG"])]
                res["phikraw"] = [("sG", cnt["sG"]), ("sT", cnt["sT"])]
                return

            if after_proj:
                after_proj()
            w = [("sV", projev)] + freew("psT")
            for h in range(H):
                j, g = h % 4, h // 4
                emit("tensor", tr(psT[32 * j:32 * j + 16, 128 * g:128 * g + 128],
                                  qk_tok[:, 16 * h:16 * h + 16], (0, 32 * j)),
                     waits=w if h == 0 else ())
                w = ()
            for h in range(H):
                j, g = h % 4, h // 4
                emit("tensor", tr(psT[32 * j:32 * j + 16, 512 + 128 * g:512 + 128 * g + 128],
                                  qk_tok[:, 256 + 16 * h:256 + 16 * h + 16], (0, 32 * j)),
                     inc=("sT", 1) if h == H - 1 else None)
            tpads = cnt["sT"]
            emit("vector", cp(qk_pad[:, :], psT[:, :]),
                 waits=[("sT", tpads)] + freew("attnfm"), inc=("sV", 1))
            padsev = cnt["sV"]
            reads("psT", "sV")
            if dump_cb:
                dump_cb("pads0", qk_pad[:, :])
            if after_pads:
                after_pads()

            def u_bank(h):
                return (psD, "psD") if h % 2 == 0 else (psF, "psF")

            def emit_u(h):
                j, g = h % 4, h // 4
                ub, ubn = u_bank(h)
                col = 128 * ((h // 2) % 2)
                w = [("sV", padsev)] + freew(f"uslot{h % 4}")
                if h < 2:
                    w += freew("psD")
                elif h < 4:
                    w += freew("psF")
                else:
                    pass
                if h >= 2:
                    w += [("sA", poly_l[h - 2])]
                emit("tensor", mm(ub[:, col:col + 128],
                                  qk_pad[32 * j:32 * j + 16, 512 + 128 * g:512 + 128 * g + 128],
                                  qk_pad[32 * j:32 * j + 16, 128 * g:128 * g + 128],
                                  True, True, (32 * j, 0)), waits=w, inc=("sT", 1))
                return cnt["sT"]

            def emit_qfm(h):
                sl = phiq[:, DPHI * (h % NQSL):DPHI * (h % NQSL) + DPHI]
                slot_free = freew(f"phiq{h % NQSL}")
                emit("scalar", lambda e, h=h, sl=sl: e.copy(
                    sl[:, 1:17], qk_tok[:, 16 * h:16 * h + 16]),
                     waits=slot_free + [("sV", kev)], inc=("sA", 1))
                wr = [("sA", cnt["sA"])]
                def qq(e, h=h, sl=sl):
                    o = sl[:, 17:273].rearrange("p (i j) -> p i j", i=16)
                    i0 = qk_tok[:, 16 * h:16 * h + 16].unsqueeze(2).broadcast_to([128, 16, 16])
                    i1 = qk_tok[:, 16 * h:16 * h + 16].unsqueeze(1).broadcast_to([128, 16, 16])
                    return e.tensor_tensor(o, i0, i1, ALU.mult)
                eng = "vector" if h % 2 == 0 else "gpsimd"
                sem = "sV" if eng == "vector" else "sG"
                emit(eng, qq, waits=slot_free + [("sV", kev)], inc=(sem, 1))
                return wr + [(sem, cnt[sem])]

            def emit_poly(h, tu):
                ub, ubn = u_bank(h)
                s1 = pu_s1t[:, 128 * (h % 4):128 * (h % 4) + 128]
                ucol = 128 * ((h // 2) % 2)
                emit("scalar", lambda e, ub=ub, s1=s1, ucol=ucol: e.activation(
                    s1, ub[:, ucol:ucol + 128], AF.Square,
                    bias=cst_sb[:, 2:3]),
                     waits=[("sT", tu)] + freew(f"pus1_{h % 4}"), inc=("sA", 1))
                reads(f"uslot{h % 4}", "sA")
                reads(ubn, "sA")
                return cnt["sA"]

            def emit_stt(h, polyA):
                ub, ubn = u_bank(h)
                s1 = pu_s1t[:, 128 * (h % 4):128 * (h % 4) + 128]
                pu = pu_sb[:, 128 * (h % 2):128 * (h % 2) + 128]
                emit("vector", lambda e, s1=s1, pu=pu: e.scalar_tensor_tensor(
                    pu, s1, 16.0, m32_sb[:, :], ALU.add, ALU.mult),
                     waits=[("sA", polyA)] + freew(f"pu_{h % 2}"), inc=("sV", 1))
                res[f"pus1_{h % 4}"] = [("sV", cnt["sV"])]
                reads(ubn, "sV")
                return cnt["sV"]

            def emit_phiT(h, qfm_w):
                phb, phn = (psT, "psT") if h % 2 == 0 else (psU, "psU")
                sl = phiq[:, DPHI * (h % NQSL):DPHI * (h % NQSL) + DPHI]
                w = list(qfm_w) + freew(phn)
                emit("tensor", tr(phb[:, 0:128], sl[:, 0:128]), waits=w)
                emit("tensor", tr(phb[:, 128:256], sl[:, 128:256]))
                emit("tensor", tr(phb[0:17, 256:384], sl[:, 256:273]), inc=("sT", 1))
                res[f"phiq{h % NQSL}"] = [("sT", cnt["sT"])]
                return cnt["sT"]

            def emit_phiev(h, tphi):
                phb, phn = (psT, "psT") if h % 2 == 0 else (psU, "psU")
                pf = phiqfm[:, 384 * (h % 2):384 * (h % 2) + 384]
                emit("scalar", lambda e, pb=phb, pf=pf: e.copy(pf, pb[:, 0:384]),
                     waits=[("sT", tphi)] + freew(f"phifm{h % 2}"), inc=("sA", 1))
                reads(phn, "sA")
                return cnt["sA"]

            def emit_qkv_inter_state(h, puV, phievA):
                osl, obank, ocol = out_slot(h)
                pu = pu_sb[:, 128 * (h % 2):128 * (h % 2) + 128]
                pf = phiqfm[:, 384 * (h % 2):384 * (h % 2) + 384]
                emit("tensor", mm(osl[:, ocol:ocol + 65], pu,
                                  v_sb[:, 65 * h:65 * h + 65], h in (0, 7, 14), False),
                     waits=[("sV", puV)], inc=("sT", 1))
                res[f"pu_{h % 2}"] = [("sT", cnt["sT"])]
                c0 = PACK * h
                w = [("sA", phievA), addsnap[h]]
                emit("tensor", mm(osl[:, ocol:ocol + 65], pf[:, 0:128],
                                  state[:, c0:c0 + 65], False, False), waits=w)
                emit("tensor", mm(osl[:, ocol:ocol + 65], pf[:, 128:256],
                                  state[:, c0 + 65:c0 + 130], False, False))
                emit("tensor", mm(osl[:, ocol:ocol + 65], pf[0:17, 256:384],
                                  state[0:17, c0 + 130:c0 + 195], False, True),
                     inc=("sT", 1))
                res[f"phifm{h % 2}"] = [("sT", cnt["sT"])]
                half = 195 * (h % 2)
                w = res[f"kfm{h}"] + (freew("psE") if h % 2 == 0 else [])
                emit("tensor", mm(psE[:, half:half + 65], phik[:, DPHI * h:DPHI * h + 128],
                                  v_sb[:, 65 * h:65 * h + 65], True, True), waits=w)
                emit("tensor", mm(psE[:, half + 65:half + 130],
                                  phik[:, DPHI * h + 128:DPHI * h + 256],
                                  v_sb[:, 65 * h:65 * h + 65], True, True))
                emit("tensor", mm(psE[0:17, half + 130:half + 195],
                                  phik[:, DPHI * h + 256:DPHI * h + 273],
                                  v_sb[:, 65 * h:65 * h + 65], True, True),
                     inc=("sT", 1))
                stv = cnt["sT"]
                res[f"phik{h}"] = [("sT", stv)]
                if h % 2 == 1:
                    c0p = PACK * (h - 1)
                    emit("vector", lambda e, c0p=c0p: e.tensor_tensor(
                        state[:, c0p:c0p + 2 * PACK], psE[:, 0:2 * PACK],
                        state[:, c0p:c0p + 2 * PACK], ALU.add),
                         waits=[("sT", stv)] + freew("state")
                               + [addsnap[h - 1], addsnap[h]], inc=("sV", 1))
                    addsnap[h - 1] = addsnap[h] = ("sV", cnt["sV"])
                    reads("psE", "sV")
                return stv

            def emit_div_group(h, stv):
                glo = {6: 0, 13: 7, 15: 14}[h]
                gbank, gname = {6: (psA, "psA"), 13: (psB, "psB"),
                                15: (psC, "psC")}[h]
                nh = h + 1 - glo
                w = [("sT", stv)]
                if h == 6:
                    w += freew("attn")
                if gname == "psC":
                    dsrc = psC[:, 130:260].rearrange("p (h v) -> p h v", h=2)[:, :, 64:65].squeeze(2)
                else:
                    dsrc = gbank[:, 0:455].rearrange("p (h v) -> p h v", h=7)[:, :, 64:65].squeeze(2)
                emit("vector", cp(den_sb[:, glo:glo + nh], dsrc), waits=w, inc=("sV", 1))
                genv = cnt["sV"]
                emit("vector", lambda e, glo=glo, nh=nh: e.tensor_scalar(
                    den_sb[:, glo:glo + nh], den_sb[:, glo:glo + nh],
                    EPS, None, ALU.add), waits=[("sV", genv)], inc=("sV", 1))
                epsv = cnt["sV"]
                emit("vector", lambda e, glo=glo, nh=nh: e.reciprocal(
                    rec_sb[:, glo:glo + nh], den_sb[:, glo:glo + nh]),
                     waits=[("sV", epsv)], inc=("sV", 1))
                recv = cnt["sV"]
                for h2 in range(glo, h + 1):
                    osl2, _, ocol2 = out_slot(h2)
                    emit("vector", lambda e, h2=h2, osl2=osl2, ocol2=ocol2: e.tensor_scalar(
                        attn[:, 64 * h2:64 * h2 + 64], osl2[:, ocol2:ocol2 + 64],
                        rec_sb[:, h2:h2 + 1], None, ALU.mult),
                         waits=[("sV", recv)] if h2 == glo else (),
                         inc=("sV", 1) if h2 == h else None)
                reads(gname, "sV")

            DEPTH_U = 3
            DEPTH_STT = 2
            qfm_l, tu_l, poly_l, stt_l = {}, {}, {}, {}
            phiT_l, phiev_l = {}, {}

            def advance(h):
                if h < H and h not in poly_l:
                    qfm_l[h] = emit_qfm(h)
                    tu_l[h] = emit_u(h)
                    poly_l[h] = emit_poly(h, tu_l[h])

            def advance_stt(h):
                if h < H and h not in stt_l:
                    stt_l[h] = emit_stt(h, poly_l[h])

            for h0 in range(min(DEPTH_U, H)):
                advance(h0)
            for h0 in range(min(DEPTH_STT, H)):
                advance_stt(h0)
            for h in range(H):
                if h not in phiev_l:
                    phiT_l[h] = emit_phiT(h, qfm_l[h])
                    phiev_l[h] = emit_phiev(h, phiT_l[h])
                if dump_cb and h == 0:
                    dump_cb("pu0", pu_sb[:, 0:128])
                    dump_cb("s1h1", pu_s1t[:, 0:128])
                    dump_cb("phiq0", phiq[:, 0:DPHI])
                stv = emit_qkv_inter_state(h, stt_l[h], phiev_l[h])
                if dump_cb and h == 0:
                    dump_cb("phifm0", phiqfm[:, 0:384])
                if h + 1 < H:
                    phiT_l[h + 1] = emit_phiT(h + 1, qfm_l[h + 1])
                    phiev_l[h + 1] = emit_phiev(h + 1, phiT_l[h + 1])
                advance(h + DEPTH_U)
                advance_stt(h + DEPTH_STT)
                if h in (6, 13, 15):
                    emit_div_group(h, stv)
            divV = cnt["sV"]
            theads = cnt["sT"]
            res["v"] = [("sT", theads)]
            res["qk"] = [("sT", theads), ("sG", cnt["sG"]), ("sV", cnt["sV"])]
            res["qksc"] = [("sG", cnt["sG"]), ("sV", cnt["sV"])]
            res["phikraw"] = [("sG", cnt["sG"]), ("sT", theads)]

            if dump_cb:
                emit("vector", cp(attnfm.bitcast(F32)[:, 0:455], psA[:, 0:455]),
                     waits=[("sT", theads)] + freew("attnfm"), inc=("sV", 1))
                res["attnfm"] = [("sV", cnt["sV"])]
                dump_cb("o0", attnfm.bitcast(F32)[:, 0:455])
                dump_cb("den0", den_sb[:, :])
                dump_cb("rec0", rec_sb[:, :])

            def tail_a(divV=divV):
                w = [("sV", divV)] + freew("psT")
                for dc in range(8):
                    emit("tensor", tr(psT[:, 128 * dc:128 * dc + 128],
                                      attn[:, 128 * dc:128 * dc + 128]),
                         waits=w if dc == 0 else (), inc=("sT", 1) if dc == 7 else None)
                tat = cnt["sT"]
                reads("attn", "sT")
                emit("scalar", lambda e: e.copy(attnfm[:, :], psT[:, :]),
                     waits=[("sT", tat)] + freew("attnfm"), inc=("sA", 1))
                reads("psT", "sA")

            def tail_b(ci=ci):
                atev = cnt["sA"]
                w = [("sA", atev)] + freew("psD", "psE")
                for dc in range(8):
                    emit("tensor", mm(psE[:, 0:512], attnfm[:, 128 * dc:128 * dc + 128],
                                      wo_sb[:, 1024 * dc:1024 * dc + 512], dc == 0, dc == 7),
                         waits=w if dc == 0 else ())
                    emit("tensor", mm(psD[:, 0:512], attnfm[:, 128 * dc:128 * dc + 128],
                                      wo_sb[:, 1024 * dc + 512:1024 * dc + 1024], dc == 0, dc == 7),
                         inc=("sT", 1) if dc == 7 else None)
                two = cnt["sT"]
                res["attnfm"] = [("sT", two)]
                emit("vector", lambda e: e.tensor_tensor(
                    stream[:, DM * ci:DM * ci + 512], psE[:, 0:512],
                    stream[:, DM * ci:DM * ci + 512], ALU.add),
                     waits=[("sT", two), ("dG", x_ld)])
                emit("vector", lambda e: e.tensor_tensor(
                    stream[:, DM * ci + 512:DM * ci + 1024], psD[:, 0:512],
                    stream[:, DM * ci + 512:DM * ci + 1024], ALU.add), inc=("sV", 1))
                rv = cnt["sV"]
                reads("psD", "sV")
                reads("psE", "sV")
                res["wo_resid"] = [("sV", rv)]
                return rv
            return tail_a, tail_b

        last_proj_t = [0]
        def load_xfm(src_t, ci, slot):
            emit("gpsimd", dma(
                xfm[:, DM * slot:DM * slot + DM].rearrange("p (dc t) -> p dc t", dc=8),
                src_t.rearrange("(dc p) t -> p dc t", p=128)[:, :, C * ci:C * ci + C]),
                 waits=freew(f"xfm{slot}"), inc=("dG", 16))
            return cnt["dG"]

        for ci in range(npre):
            xld = load_xfm(xpt_h, ci, ci % 2)
            scan_chunk(ci, ci % 2, 1, True, xld, qk_tok=pk_tok, ksc=pk_sc)
            res[f"xfm{ci % 2}"] = [("sT", cnt["sT"])]

        if dumps:
            dump("state_pre", state[:, :], PACK * H)

        def ln_stats(ci, resid_sv):
            w = [("sV", resid_sv)]
            emit("scalar", lambda e, ci=ci: e.activation(
                attn[:, :], stream[:, DM * ci:DM * ci + DM],
                AF.Square, accum_out=ln_sb[:, 16 + ci:17 + ci]),
                 waits=w, inc=("sA", 1))
            reads("attn", "sA")
            emit("vector", lambda e, ci=ci: e.reduce_sum(
                ln_sb[:, ci:ci + 1], stream[:, DM * ci:DM * ci + DM], AX),
                 waits=w, inc=("sV", 1))


        pend = [None, None]
        no_pipe = False
        for ci in range(nch):
            emit("gpsimd", dma(stream[:, DM * ci:DM * ci + DM], x_h[C * ci:C * ci + C, :]),
                 inc=("dG", 16))
            x_ld = cnt["dG"]
            xld = load_xfm(xt_h, ci, ci % 2)
            if pend[0] is not None:
                pend[0]()
                pend[0] = None
            def tailb_only(ci=ci):
                if pend[1] is not None:
                    rv = pend[1]()
                    ln_stats(ci - 1, rv)
                    if ci - 1 == 1:
                        dump("stream1", stream[:, DM:2 * DM], DM)
                    pend[1] = None
            tails = scan_chunk(ci, ci % 2, 0, False, xld,
                               dump_cb=(dump if ci == 0 else None),
                               after_proj=tailb_only,
                               after_pads=None, x_ld=x_ld)
            pend = list(tails)
            res[f"xfm{ci % 2}"] = [("sT", cnt["sT"])]
            if ci == 0:
                dump("qk0", qk_tok[:, :], 512)
                dump("v0", v_sb[:, :], 1040)
                dump("phik0", phik[:, :], DPHI * H)
                dump("attn0", attn[:, :], DM)
                dump("state0", state[:, :], PACK * H)
        pend[0]()
        rv = pend[1]()
        ln_stats(nch - 1, rv)

        def ln_finish():
            sAs, sVs = cnt["sA"], cnt["sV"]
            emit("vector", lambda e: e.tensor_scalar(
                ln_sb[:, 32:48], ln_sb[:, 0:16], 1.0 / DM, None, ALU.mult),
                 waits=[("sA", sAs), ("sV", sVs)], inc=("sV", 1))
            meanv = cnt["sV"]
            emit("vector", lambda e: e.tensor_tensor(
                den_sb[:, 0:16], ln_sb[:, 32:48], ln_sb[:, 32:48], ALU.mult),
                 waits=[("sV", meanv)], inc=("sV", 1))
            msqv = cnt["sV"]
            emit("vector", lambda e: e.scalar_tensor_tensor(
                ln_sb[:, 64:80], ln_sb[:, 16:32], 1.0 / DM, den_sb[:, 0:16],
                ALU.mult, ALU.subtract), waits=[("sV", msqv)], inc=("sV", 1))
            vv = cnt["sV"]
            emit("scalar", lambda e: e.activation(rec_sb[:, 0:16], ln_sb[:, 64:80],
                                                  AF.Ln, bias=cst_sb[:, 3:4]), waits=[("sV", vv)])
            emit("scalar", lambda e: e.activation(ln_sb[:, 48:64], rec_sb[:, 0:16],
                                                  AF.Exp, scale=cst_sb[:, 4:5]), inc=("sA", 1))
            sa = cnt["sA"]
            emit("vector", lambda e: e.scalar_tensor_tensor(
                ln_sb[:, 64:80], ln_sb[:, 32:48], -1.0, ln_sb[:, 48:64],
                ALU.mult, ALU.mult), waits=[("sA", sa)], inc=("sV", 1))
            return cnt["sV"]

        vneg = ln_finish()
        if dumps:
            dump("lnsb", ln_sb[:, :], 96)
        for ci in range(nch):
            emit("vector", lambda e, ci=ci: e.tensor_scalar(
                stream[:, DM * ci:DM * ci + DM], stream[:, DM * ci:DM * ci + DM],
                ln_sb[:, 48 + ci:49 + ci], ln_sb[:, 64 + ci:65 + ci],
                ALU.mult, ALU.add),
                 waits=[("sV", vneg)] if ci == 0 else (), inc=("sV", 1))
            res[f"norm{ci}"] = [("sV", cnt["sV"])]
        if dumps:
            dump("y1", stream[:, 0:DM], DM)

        wd3 = wd_h.rearrange("(hc p) j -> p hc j", p=128)
        emit("gpsimd", dma(wd_sb[:, 20480:24576].rearrange("p (hc j) -> p hc j", hc=4),
                           wd3[:, 20:24]), inc=("dG", 16))
        emit("gpsimd", dma(wd_sb[:, 0:12288].rearrange("p (hc j) -> p hc j", hc=12),
                           wd3[:, 0:12]),
             waits=[("sT", last_proj_t[0])], inc=("dG", 16))
        emit("gpsimd", dma(wd_sb[:, 12288:20480].rearrange("p (hc j) -> p hc j", hc=8),
                           wd3[:, 12:20]),
             waits=[("sT", cnt["sT"])], inc=("dG", 16))
        W_WD = cnt["dG"]

        for ci in range(nch):
            if ci == 0:
                w = res.pop(f"norm{ci}") + freew("psT")
                for dc in range(8):
                    emit("tensor", tr(psT[:, 128 * dc:128 * dc + 128],
                                      stream[:, DM * ci + 128 * dc:DM * ci + 128 * dc + 128]),
                         waits=w if dc == 0 else (), inc=("sT", 1) if dc == 7 else None)
                    w = ()
                ty = cnt["sT"]
                emit("vector", cp(xfm[:, 0:DM], psT[:, :]),
                     waits=[("sT", ty)] + freew("xfm0") + freew("xfm1"), inc=("sV", 1))
                reads("psT", "sV")
                y1ev = cnt["sV"]
            else:
                y1ev = res.pop("y1ev_next")[0][1]

            hm6 = []
            for q in range(6):
                psg = (psA, psB, psC)[q % 3]
                psu2 = (psD, psE, psF)[q % 3]
                gname = ("psA", "psB", "psC")[q % 3]
                uname = ("psD", "psE", "psF")[q % 3]
                w = [("dG", W_GU)] + freew(gname, uname)
                if q == 0:
                    w += [("sV", y1ev)] + freew("htok")
                for dc in range(8):
                    lhsT = xfm[:, 128 * dc:128 * dc + 128]
                    emit("tensor", mm(psg[:, 0:512], lhsT,
                                      wg_sb[:, 3072 * dc + 512 * q:3072 * dc + 512 * q + 512],
                                      dc == 0, dc == 7),
                         waits=w if dc == 0 else ())
                    w = ()
                    emit("tensor", mm(psu2[:, 0:512], lhsT,
                                      wu_sb[:, 3072 * dc + 512 * q:3072 * dc + 512 * q + 512],
                                      dc == 0, dc == 7),
                         inc=("sT", 1) if dc == 7 else None)
                tgu = cnt["sT"]
                emit("scalar", lambda e, q=q, psg=psg: e.activation(
                    silu_sb[:, 512 * (q % 3):512 * (q % 3) + 512], psg[:, 0:512], AF.Silu,
                    scale=cst_sb[:, 5:6]),
                     waits=[("sT", tgu)] + freew(f"silu{q % 3}"), inc=("sA", 1))
                sk = cnt["sA"]
                reads(gname, "sA")
                emit("vector", lambda e, q=q, psu2=psu2: e.tensor_tensor(
                    h_tok[:, 512 * q:512 * q + 512],
                    silu_sb[:, 512 * (q % 3):512 * (q % 3) + 512], psu2[:, 0:512],
                    ALU.mult),
                     waits=[("sA", sk)], inc=("sV", 1))
                res[f"silu{q % 3}"] = [("sV", cnt["sV"])]
                reads(uname, "sV")
                hm6.append(cnt["sV"])
            hm = [hm6[2], hm6[5]]

            hevs = []
            for grp in range(3):
                phb, phn = (psT, "psT") if grp % 2 == 0 else (psU, "psU")
                w = [("sV", hm6[2 * grp + 1])] + freew(phn)
                for i in range(8):
                    hc = 8 * grp + i
                    emit("tensor", tr(phb[:, 128 * i:128 * i + 128],
                                      h_tok[:, 128 * hc:128 * hc + 128]),
                         waits=w if i == 0 else (), inc=("sT", 1) if i == 7 else None)
                    w = ()
                tg = cnt["sT"]
                emit("scalar", lambda e, phb=phb, grp=grp: e.copy(
                    hfm[:, 1024 * grp:1024 * grp + 1024], phb[:, :]),
                     waits=[("sT", tg)] + (freew("hfm") if grp == 0 else []),
                     inc=("sA", 1))
                reads(phn, "sA")
                hevs.append(cnt["sA"])

            if ci + 1 < nch:
                w2 = res.pop(f"norm{ci + 1}") + freew("psU")
                for dc in range(8):
                    emit("tensor", tr(psU[:, 128 * dc:128 * dc + 128],
                                      stream[:, DM * (ci + 1) + 128 * dc:DM * (ci + 1) + 128 * dc + 128]),
                         waits=w2 if dc == 0 else (), inc=("sT", 1) if dc == 7 else None)
                ty2 = cnt["sT"]
                emit("vector", cp(xfm[:, 0:DM], psU[:, :]),
                     waits=[("sT", ty2)] + freew("xfm0") + freew("xfm1"), inc=("sV", 1))
                reads("psU", "sV")
                res["y1ev_next"] = [("sV", cnt["sV"])]

            w = freew("psB", "psC")
            first = True
            for hc in list(range(12)) + list(range(20, 24)) + list(range(12, 20)):
                w2 = list(w) + [("sA", hevs[hc // 8])]
                if hc >= 12:
                    w2.append(("dG", W_WD))
                w = []
                emit("tensor", mm(psB[:, 0:512], hfm[:, 128 * hc:128 * hc + 128],
                                  wd_sb[:, 1024 * hc:1024 * hc + 512], first, False),
                     waits=w2)
                emit("tensor", mm(psC[:, 0:512], hfm[:, 128 * hc:128 * hc + 128],
                                  wd_sb[:, 1024 * hc + 512:1024 * hc + 1024], first, False),
                     inc=("sT", 1) if hc == 19 else None)
                first = False
            td = cnt["sT"]
            res["hfm"] = [("sT", td)]
            res["htok"] = [("sT", td)]
            emit("vector", lambda e, ci=ci: e.scalar_tensor_tensor(
                stream[:, DM * ci:DM * ci + 512], psB[:, 0:512], FFSC,
                stream[:, DM * ci:DM * ci + 512], ALU.mult, ALU.add),
                 waits=[("sT", td)])
            emit("vector", lambda e, ci=ci: e.scalar_tensor_tensor(
                stream[:, DM * ci + 512:DM * ci + 1024], psC[:, 0:512], FFSC,
                stream[:, DM * ci + 512:DM * ci + 1024], ALU.mult, ALU.add),
                 inc=("sV", 1))
            zr = cnt["sV"]
            res["psB"] = [("sV", zr)]
            res["psC"] = [("sV", zr)]
            res[f"z{ci}"] = [("sV", zr)]
            ln_stats(ci, zr)

        if dumps:
            dump("z0", stream[:, 0:DM], DM)

        vneg2 = ln_finish()
        for ci in range(nch):
            stg = ostage[:, 1024 * (ci % 2):1024 * (ci % 2) + 1024]
            w = [("sV", vneg2)] if ci == 0 else []
            if ci >= 2:
                w.append(("dS", 16 * (ci - 1)))
            emit("vector", lambda e, ci=ci, stg=stg: e.tensor_scalar(
                stg, stream[:, DM * ci:DM * ci + DM],
                ln_sb[:, 48 + ci:49 + ci], ln_sb[:, 64 + ci:65 + ci],
                ALU.mult, ALU.add),
                 waits=w, inc=("sV", 1))
            nv = cnt["sV"]
            emit("scalar", lambda e, ci=ci, stg=stg: e.dma_start(
                out=y_h[C * ci:C * ci + C, :], in_=stg),
                 waits=[("sV", nv)], inc=("dS", 16))

        final_dS = cnt["dS"]
        final_dD = cnt["dD"]
        emit("scalar", lambda e: e.nop(), waits=[("dS", final_dS)])
        emit("sync", lambda e: e.nop(),
             waits=[("dS", final_dS)] + ([("dD", final_dD)] if final_dD else []))
        emit("gpsimd", lambda e: e.nop(), waits=[("dS", final_dS)])

        with nc.Block() as block:
            def runner(eng_name):
                def go(e):
                    for fn, waits, inc in streams[eng_name]:
                        for s, v in waits:
                            e.wait_ge(sems[s], v)
                        i = fn(e)
                        if inc:
                            i.then_inc(sems[inc[0]], inc[1])
                return go
            block.sync(runner("sync"))
            block.gpsimd(runner("gpsimd"))
            block.vector(runner("vector"))
            block.scalar(runner("scalar"))
            block.tensor(runner("tensor"))

    return nc


def make_inputs_for_core(core, x_full, w):
    b, half = core // 2, core % 2
    x_main = np.ascontiguousarray(x_full[b, half * T:(half + 1) * T])
    x_t = np.ascontiguousarray(x_main.T)
    if half == 1:
        x_pt = np.ascontiguousarray(x_full[b, 0:T].T)
        pmask = 1.0
    else:
        x_pt = np.zeros((DM, T), np.float32)
        pmask = 0.0
    cst = np.zeros((128, 8), np.float32)
    cst[:, 0] = 1.0
    cst[:, 1] = pmask
    cst[:, 2] = 4.0
    cst[:, 3] = LNEPS
    cst[:, 4] = -0.5
    cst[:, 5] = 1.0 / 127.0
    m = {"x": x_main, "xt": x_t, "xpt": x_pt, "cst": cst}
    m.update(w)
    return m


def shared_weights(inputs):
    ident = np.eye(128, dtype=np.float32)
    m32 = (np.triu(np.ones((128, 128), np.float32)) / 32.0).astype(np.float32)
    return {
        "wq": np.ascontiguousarray(inputs["L0_attn_Wq"], np.float32),
        "wk": np.ascontiguousarray(inputs["L0_attn_Wk"], np.float32),
        "wv": np.ascontiguousarray(inputs["L0_attn_Wv"], np.float32),
        "wo": np.ascontiguousarray(inputs["L0_attn_Wo"], np.float32),
        "wg": np.ascontiguousarray(inputs["L0_ffn_w_gate"].T),
        "wu": np.ascontiguousarray(inputs["L0_ffn_w_up"].T),
        "wd": np.ascontiguousarray(inputs["L0_ffn_w_down"].T),
        "ident": ident,
        "m32": m32,
    }


def kernel(L0_attn_hidden_states, L0_attn_Wq, L0_attn_Wk, L0_attn_Wv,
           L0_attn_Wo, L0_ffn_w_gate, L0_ffn_w_up, L0_ffn_w_down):
    inputs = {
        "L0_attn_hidden_states": np.asarray(L0_attn_hidden_states),
        "L0_attn_Wq": np.asarray(L0_attn_Wq), "L0_attn_Wk": np.asarray(L0_attn_Wk),
        "L0_attn_Wv": np.asarray(L0_attn_Wv), "L0_attn_Wo": np.asarray(L0_attn_Wo),
        "L0_ffn_w_gate": np.asarray(L0_ffn_w_gate),
        "L0_ffn_w_up": np.asarray(L0_ffn_w_up),
        "L0_ffn_w_down": np.asarray(L0_ffn_w_down),
    }
    x = inputs["L0_attn_hidden_states"]
    w = shared_weights(inputs)
    nc = build_nc()
    in_maps = [make_inputs_for_core(c, x, w) for c in range(8)]
    res = run_bass_kernel_spmd(nc, in_maps, core_ids=list(range(8)))
    out = np.empty((4, 4096, DM), np.float32)
    for c in range(8):
        b, half = c // 2, c % 2
        out[b, half * T:(half + 1) * T] = res.results[c]["y"]
    return out
